# version 13
# baseline (speedup 1.0000x reference)
import numpy as np
from contextlib import ExitStack

import concourse.bass as bass
import concourse.mybir as mybir
from concourse.bass_utils import run_bass_kernel_spmd

F32 = mybir.dt.float32
BF16 = mybir.dt.bfloat16
AF = mybir.ActivationFunctionType
ALU = mybir.AluOpType
AX = mybir.AxisListType

T = 1024
NCORES = 8
DBG = {}



def _tile_w(w, transpose):
    w = np.asarray(w, np.float32)
    if transpose:
        w = np.ascontiguousarray(w.T)
    in_f, out_f = w.shape
    assert in_f % 128 == 0
    kc = in_f // 128
    m = (out_f + 127) // 128
    wp = np.zeros((in_f, m * 128), np.float32)
    wp[:, :out_f] = w
    out = wp.reshape(kc, 128, m, 128).transpose(2, 1, 0, 3)
    return np.ascontiguousarray(out)


def stage_inputs(inputs):
    import ml_dtypes
    g = {}
    g["ident_f"] = np.eye(128, dtype=np.float32)
    g["ident_b"] = np.eye(128, dtype=ml_dtypes.bfloat16)
    g["triu_f"] = np.triu(np.ones((128, 128), np.float32))
    g["onesb"] = np.ones((128, 128), dtype=ml_dtypes.bfloat16)
    i8e = np.zeros((9, 9, 128), dtype=ml_dtypes.bfloat16)
    for e in range(9):
        i8e[e, e, :] = 1.0
    g["i8e_b"] = i8e

    def dpair_of(dp):
        dp = np.asarray(dp, np.float32)
        out = np.zeros((128, 32), np.float32)
        out[0:64, :] = dp[0::2][None, :]
        out[64:128, :] = dp[1::2][None, :]
        return out

    g["l0_win"] = _tile_w(inputs["L0_attn_W_in"], True)
    g["l0_wout"] = _tile_w(inputs["L0_attn_W_out"], True)
    g["l0_convw"] = np.ascontiguousarray(
        np.asarray(inputs["L0_attn_conv_w"], np.float32).reshape(33, 128, 4)
        .transpose(1, 0, 2))
    g["l0_convb"] = np.ascontiguousarray(
        np.asarray(inputs["L0_attn_conv_b"], np.float32).reshape(33, 128)
        .transpose(1, 0))
    g["l0_dtb"] = np.asarray(inputs["L0_attn_dt_bias"], np.float32).reshape(64, 1)
    g["l0_alog"] = np.asarray(inputs["L0_attn_A_log"], np.float32).reshape(64, 1)
    g["l0_dpair"] = dpair_of(inputs["L0_attn_D_param"])
    g["l0_normw"] = np.ascontiguousarray(
        np.asarray(inputs["L0_attn_norm_w"], np.float32).reshape(32, 128)
        .transpose(1, 0))
    g["l0_gwT"] = np.ascontiguousarray(
        np.asarray(inputs["L0_ffn_gate_w"], np.float32).T.reshape(16, 128, 8)
        .transpose(1, 0, 2))
    g["l0_gbrep"] = np.ascontiguousarray(np.broadcast_to(
        np.asarray(inputs["L0_ffn_gate_b"], np.float32), (128, 8)))
    g["l0_w1"] = np.stack([_tile_w(inputs["L0_ffn_W1"][e], True) for e in range(8)])
    g["l0_w3"] = np.stack([_tile_w(inputs["L0_ffn_W3"][e], True) for e in range(8)])
    g["l0_w2"] = np.stack([_tile_w(inputs["L0_ffn_W2"][e], True) for e in range(8)])
    g["l0_ws1"] = _tile_w(inputs["L0_ffn_Ws1"], True)
    g["l0_ws3"] = _tile_w(inputs["L0_ffn_Ws3"], True)
    g["l0_ws2"] = _tile_w(inputs["L0_ffn_Ws2"], True)
    g["l1_win"] = _tile_w(inputs["L1_attn_in_proj_w"], False)
    g["l1_wout"] = _tile_w(inputs["L1_attn_out_proj_w"], False)
    cw1 = np.ascontiguousarray(np.asarray(inputs["L1_attn_conv_w"], np.float32)[:, 0, :].T)
    g["l1_convw"] = np.ascontiguousarray(cw1.reshape(48, 128, 4).transpose(1, 0, 2))
    g["l1_convb"] = np.ascontiguousarray(
        np.asarray(inputs["L1_attn_conv_b"], np.float32).reshape(48, 128)
        .transpose(1, 0))
    g["l1_dtb"] = np.asarray(inputs["L1_attn_dt_bias"], np.float32).reshape(64, 1)
    g["l1_alog"] = np.asarray(inputs["L1_attn_A_log"], np.float32).reshape(64, 1)
    g["l1_dpair"] = dpair_of(inputs["L1_attn_D"])
    g["l1_normw"] = np.ascontiguousarray(
        np.asarray(inputs["L1_attn_norm_w"], np.float32).reshape(32, 128)
        .transpose(1, 0))
    gw9 = np.concatenate([np.asarray(inputs["L1_ffn_gate_w"], np.float32),
                          np.asarray(inputs["L1_ffn_shared_expert_gate_w"], np.float32)],
                         axis=0)
    g["l1_gwT"] = np.ascontiguousarray(gw9.T.reshape(16, 128, 9).transpose(1, 0, 2))
    g["l1_gup"] = np.stack([_tile_w(inputs["L1_ffn_gate_up_proj"][e], True)
                            for e in range(8)])
    g["l1_down"] = np.stack([_tile_w(inputs["L1_ffn_down_proj"][e], True)
                             for e in range(8)])
    g["l1_sg"] = _tile_w(inputs["L1_ffn_shared_gate_w"], True)
    g["l1_su"] = _tile_w(inputs["L1_ffn_shared_up_w"], True)
    g["l1_sd"] = _tile_w(inputs["L1_ffn_shared_down_w"], True)

    u = np.asarray(inputs["L0_attn_u"], np.float32)
    percore = []
    for c in range(NCORES):
        bb, h = c // 2, c % 2
        sl = u[bb, h * T:(h + 1) * T, :]
        x0 = np.ascontiguousarray(sl.T).reshape(16, 128, T)
        percore.append({"x0": x0,
                        "flag_hi": np.full((128, 1), float(h), np.float32),
                        "flag_lo": np.full((128, 1), float(1 - h), np.float32)})
    return g, percore



class Tok:
    __slots__ = ("sem", "v")

    def __init__(self, sem, v):
        self.sem = sem
        self.v = v


class Builder:
    def __init__(self, nc, ctx):
        self.nc = nc
        self.ctx = ctx
        self.pe = nc.tensor
        self.ve = nc.vector
        self.se = nc.scalar
        self.po = nc.gpsimd
        self.sy = nc.sync
        self.sems = {}
        for name in ["ld", "gd", "pe", "ve", "se", "po", "cc", "sy"]:
            self.sems[name] = ctx.enter_context(nc.semaphore(name))
        self.cnt = {name: 0 for name in self.sems}
        self.pA = ctx.enter_context(nc.psum_tensor("pA", [128, 1024], F32))
        self.pB = ctx.enter_context(nc.psum_tensor("pB", [128, 1024], F32))
        self.pC = ctx.enter_context(nc.psum_tensor("pC", [128, 1024], F32))
        self.pD = ctx.enter_context(nc.psum_tensor("pD", [128, 512], F32))
        self.pE = ctx.enter_context(nc.psum_tensor("pE", [128, 512], F32))
        self.CAP = 206 * 1024
        self.arena = ctx.enter_context(
            nc.sbuf_tensor("arena", [128, self.CAP // 4], F32))
        self.off = 0
        self.marks = []
        self.peak = 0
        self.last_pe = None

    def inc(self, name, inst, n=1):
        inst.then_inc(self.sems[name], n)
        self.cnt[name] += n
        return Tok(name, self.cnt[name])

    def dslot(self, name):
        if name not in self.sems:
            self.sems[name] = self.ctx.enter_context(self.nc.semaphore(name))
            self.cnt[name] = 0
        return name

    def dma(self, q, slot, out, in_):
        self.dslot(slot)
        return self.inc(slot, q.dma_start(out=out, in_=in_), 16)

    def tok(self, name):
        return Tok(name, self.cnt[name])

    def pe_tok(self, inst=None):
        if inst is None:
            inst = self.last_pe
        if inst is not None:
            return self.inc("pe", inst)
        return self.inc("pe", self.pe.drain())

    def mm(self, *a, **k):
        self.last_pe = self.pe.matmul(*a, **k)
        return self.last_pe

    def tr(self, *a, **k):
        self.last_pe = self.pe.transpose(*a, **k)
        return self.last_pe

    def pe_warm(self, cons, n):
        for _ in range(n):
            self.last_pe = self.pe.matmul(self.pA[:, 0:128], cons["onesb"][:, :],
                                          cons["ident_b"][:, :],
                                          start=True, stop=True)

    def wait(self, eng, tok):
        if tok is not None:
            eng.wait_ge(self.sems[tok.sem], tok.v)

    def barrier(self):
        b = self
        for name, c in list(b.cnt.items()):
            if name in ("pe", "ve", "se", "po", "cc", "sy", "gd") or c == 0:
                continue
            if name.startswith("st"):
                continue
            b.sy.wait_ge(b.sems[name], c)
        tsy = b.inc("sy", b.sy.nop(nofuse=True))
        for name, c in list(b.cnt.items()):
            if c and (name == "gd" or name.startswith("st")):
                b.po.wait_ge(b.sems[name], c)
        b.ve.drain()
        tv = b.inc("ve", b.ve.nop(nofuse=True))
        b.se.drain()
        ts = b.inc("se", b.se.nop(nofuse=True))
        b.po.drain()
        tp = b.inc("po", b.po.nop(nofuse=True))
        tt = b.inc("pe", b.pe.drain())
        for eng in [b.pe, b.ve, b.se, b.po, b.sy]:
            for t in [tv, ts, tp, tt, tsy]:
                b.wait(eng, t)

    def alloc(self, nbytes, align=64):
        nbytes = -(-nbytes // align) * align
        o = -(-self.off // align) * align
        self.off = o + nbytes
        self.peak = max(self.peak, self.off)
        assert self.off <= self.CAP, f"SBUF arena overflow: {self.off}"
        return o

    def mark(self):
        self.marks.append(self.off)

    def release(self):
        self.off = self.marks.pop()

    def view(self, off, shape, dtype=F32):
        p = shape[0]
        free = 1
        for s in shape[1:]:
            free *= s
        dsz = 2 if dtype == BF16 else 4
        assert off % 4 == 0
        ap = self.arena[0:p, off // 4:(off + free * dsz + 3) // 4]
        if dtype != F32:
            ap = ap.bitcast(dtype)
            ap = ap[:, 0:free]
        if len(shape) == 2:
            return ap
        if len(shape) == 3:
            return ap.rearrange("p (a c) -> p a c", a=shape[1])
        if len(shape) == 4:
            return ap.rearrange("p (a c d) -> p a c d", a=shape[1], c=shape[2])
        raise AssertionError

    def tile(self, shape, dtype=F32):
        dsz = 2 if dtype == BF16 else 4
        free = 1
        for s in shape[1:]:
            free *= s
        return self.view(self.alloc(free * dsz), shape, dtype)



def emit_ln(b, cons, x_dram, xnorm, eps_name):
    b.mark()
    x_ld = [b.tile((128, T)) for _ in range(2)]
    sq = b.tile((128, T), BF16)
    mrep = b.tile((128, T))
    rstd = b.tile((128, T))
    c0 = b.tile((128, T))
    casttok = {}
    pet = {}
    for m in range(16):
        buf = x_ld[m % 2]
        if m >= 2:
            b.wait(b.sy, casttok[m - 2])
        tok = b.dma(b.sy, f"lnx{m % 2}", buf[:, :], x_dram[m])
        b.wait(b.ve, tok)
        casttok[m] = b.inc("ve", b.ve.tensor_copy(xnorm[:, m, :], buf[:, :]))
        b.wait(b.se, casttok[m])
        if m >= 1:
            b.wait(b.se, pet[m - 1])
        tsq = b.inc("se", b.se.activation(sq[:, :], xnorm[:, m, :], AF.Square))
        b.wait(b.pe, casttok[m])
        b.mm(b.pA[:, 0:512], cons["onesb"][:, :], xnorm[:, m, 0:512],
                    start=(m == 0), stop=(m == 15))
        b.mm(b.pA[:, 512:1024], cons["onesb"][:, :], xnorm[:, m, 512:1024],
                    start=(m == 0), stop=(m == 15))
        b.wait(b.pe, tsq)
        b.mm(b.pB[:, 0:512], cons["onesb"][:, :], sq[:, 0:512],
                    start=(m == 0), stop=(m == 15))
        lastmm = b.mm(b.pB[:, 512:1024], cons["onesb"][:, :], sq[:, 512:1024],
                             start=(m == 0), stop=(m == 15))
        pet[m] = b.pe_tok(lastmm)
    tstats = pet[15]
    inv = 1.0 / 2048.0
    b.wait(b.ve, tstats)
    b.ve.tensor_scalar(mrep[:, :], b.pA[:, :], inv, None, ALU.mult)
    b.ve.drain()
    b.ve.tensor_tensor(c0[:, :], mrep[:, :], mrep[:, :], ALU.mult)
    b.ve.drain()
    tv = b.inc("ve", b.ve.scalar_tensor_tensor(
        rstd[:, :], b.pB[:, :], inv, c0[:, :], ALU.mult, ALU.subtract))
    b.wait(b.se, tv)
    b.se.activation(rstd[:, :], rstd[:, :], AF.Ln, bias=cons[eps_name][:, 0:1])
    b.se.drain()
    ts = b.inc("se", b.se.activation(rstd[:, :], rstd[:, :], AF.Exp, scale=-0.5))
    b.wait(b.ve, ts)
    b.ve.drain()
    b.inc("ve", b.ve.scalar_tensor_tensor(
        c0[:, :], mrep[:, :], -1.0, rstd[:, :], ALU.mult, ALU.mult))
    b.ve.drain()
    vt = {-2: casttok[14], -1: casttok[15]}
    for m in range(16):
        buf = x_ld[m % 2]
        b.wait(b.sy, vt[m - 2])
        tok = b.dma(b.sy, f"lnx{m % 2}", buf[:, :], x_dram[m])
        b.wait(b.ve, tok)
        b.ve.tensor_tensor(buf[:, :], buf[:, :], rstd[:, :], ALU.mult)
        b.ve.drain()
        vt[m] = b.inc("ve", b.ve.tensor_tensor(
            xnorm[:, m, :], buf[:, :], c0[:, :], ALU.add))
    b.barrier()
    b.release()



def emit_gemm(b, w_dram, M, KC, rhs_fn, epi_fn, w2_dram=None, n_tok=T,
              tail_rhs=None, tail_psum=None, tail_kc=None, nbuf=3):
    if KC > 16 or w2_dram is not None:
        nbuf = 2
    b.mark()
    wf = [b.tile((128, KC, 128)) for _ in range(nbuf)]
    wb = [b.tile((128, KC, 128), BF16) for _ in range(nbuf)]
    wf2 = wb2 = None
    if w2_dram is not None:
        wf2 = [b.tile((128, KC, 128)) for _ in range(nbuf)]
        wb2 = [b.tile((128, KC, 128), BF16) for _ in range(nbuf)]
    NT = -(-n_tok // 512)
    ldt, pot, epit, pet = {}, {}, {}, {}

    def issue_load(i):
        if i >= nbuf:
            b.wait(b.sy, pot[i - nbuf])
        t = b.dma(b.sy, f"gw{i % nbuf}", wf[i % nbuf][:, :, :], w_dram[i])
        if w2_dram is not None:
            t = b.dma(b.sy, f"gw{i % nbuf}", wf2[i % nbuf][:, :, :], w2_dram[i])
        ldt[i] = t

    def issue_cast(i):
        eng, enm = (b.po, "po")
        b.wait(eng, ldt[i])
        if i >= nbuf:
            b.wait(eng, pet[i - nbuf])
        c = eng.tensor_copy(wb[i % nbuf][:, :, :], wf[i % nbuf][:, :, :]) \
            if enm == "po" else eng.copy(wb[i % nbuf][:, :, :], wf[i % nbuf][:, :, :])
        if w2_dram is not None:
            c = eng.tensor_copy(wb2[i % nbuf][:, :, :], wf2[i % nbuf][:, :, :]) \
                if enm == "po" else eng.copy(wb2[i % nbuf][:, :, :],
                                             wf2[i % nbuf][:, :, :])
        pot[i] = b.inc(enm, c)

    for i in range(min(nbuf, M)):
        issue_load(i)
        issue_cast(i)
    for i in range(M):
        if i + nbuf < M:
            issue_load(i + nbuf)
        b.wait(b.pe, pot[i])
        if w2_dram is not None:
            ps, ps2 = b.pA, b.pB
            if epit.get(i - 1) is not None:
                b.wait(b.pe, epit[i - 1])
        else:
            ps, ps2 = (b.pA, None) if i % 2 == 0 else (b.pB, None)
            if epit.get(i - 2) is not None:
                b.wait(b.pe, epit[i - 2])
        last = None
        for nt in range(NT):
            t0, t1 = nt * 512, min((nt + 1) * 512, n_tok)
            for kc in range(KC):
                last = b.mm(ps[:, t0:t1], wb[i % nbuf][:, kc, :],
                                   rhs_fn(kc, t0, t1),
                                   start=(kc == 0), stop=(kc == KC - 1))
            if w2_dram is not None:
                for kc in range(KC):
                    last = b.mm(ps2[:, t0:t1], wb2[i % nbuf][:, kc, :],
                                       rhs_fn(kc, t0, t1),
                                       start=(kc == 0), stop=(kc == KC - 1))
        pet[i] = b.pe_tok(last)
        if i + nbuf < M:
            issue_cast(i + nbuf)
        tk = None
        for nt in range(NT):
            t0, t1 = nt * 512, min((nt + 1) * 512, n_tok)
            r = epi_fn(i, t0, t1, ps[:, t0:t1],
                       None if ps2 is None else ps2[:, t0:t1], pet[i])
            if r is not None:
                tk = r
        epit[i] = tk
    b.barrier()
    b.release()



def emit_gating(b, cons, L, xnorm, dwT, renorm):
    ncol = L["ngcol"]
    b.mark()
    gw = b.tile((128, 16, ncol))
    gwb = b.tile((128, 16, ncol), BF16)
    logits = b.tile((128, 8, ncol))
    p = b.tile((128, 8, 8))
    msk = b.tile((128, 8, 8))
    tmp = b.tile((128, 8, 8))
    red = b.tile((128, 8))
    dwtm = b.tile((128, 8, 8), BF16)
    sgtm = b.tile((128, 8), BF16) if L.get("nsig") else None
    tok = b.dma(b.sy, "misc0", gw[:, :, :], cons[L["gwT"]])
    b.wait(b.po, tok)
    tc_ = b.inc("po", b.po.tensor_copy(gwb[:, :, :], gw[:, :, :]))
    b.wait(b.pe, tc_)
    lastmm = None
    for tc in range(8):
        for kc in range(16):
            lastmm = b.mm(b.pD[:, tc * ncol:(tc + 1) * ncol],
                                 xnorm[:, kc, tc * 128:(tc + 1) * 128],
                                 gwb[:, kc, :], start=(kc == 0), stop=(kc == 15))
    tmm = b.pe_tok()
    b.wait(b.ve, tmm)
    b.ve.tensor_copy(logits[:, :, :],
                     b.pD[:, 0:8 * ncol].rearrange("p (a c) -> p a c", a=8))
    b.ve.drain()
    lg8 = logits[:, :, 0:8]
    b.ve.tensor_reduce(red[:, :], lg8, axis=AX.X, op=ALU.max)
    b.ve.drain()
    for tc in range(8):
        b.ve.tensor_scalar(p[:, tc, :], lg8[:, tc, :], red[:, tc:tc + 1], None,
                           ALU.subtract)
    b.ve.drain()
    tv = b.inc("ve", b.ve.nop(nofuse=True))
    b.wait(b.se, tv)
    ts = b.inc("se", b.se.activation(p[:, :, :], p[:, :, :], AF.Exp))
    b.wait(b.ve, ts)
    b.ve.tensor_reduce(red[:, :], p[:, :, :], axis=AX.X, op=ALU.add)
    b.ve.drain()
    b.ve.reciprocal(red[:, :], red[:, :])
    b.ve.drain()
    for tc in range(8):
        b.ve.tensor_scalar(p[:, tc, :], p[:, tc, :], red[:, tc:tc + 1], None,
                           ALU.mult)
    b.ve.drain()
    if L.get("gbrep"):
        gb = b.tile((128, 8))
        tok = b.dma(b.sy, "misc1", gb[:, :], cons[L["gbrep"]])
        b.wait(b.ve, tok)
        bview = gb[:, :].unsqueeze(1).broadcast_to([128, 8, 8])
        b.ve.tensor_tensor(tmp[:, :, :], p[:, :, :], bview, ALU.add)
    else:
        b.ve.tensor_copy(tmp[:, :, :], p[:, :, :])
    b.ve.drain()
    b.ve.tensor_reduce(red[:, :], tmp[:, :, :], axis=AX.X, op=ALU.max)
    b.ve.drain()
    for tc in range(8):
        b.ve.tensor_scalar(msk[:, tc, :], tmp[:, tc, :], red[:, tc:tc + 1], None,
                           ALU.is_equal)
    b.ve.drain()
    b.ve.scalar_tensor_tensor(tmp[:, :, :], msk[:, :, :], -1e30, tmp[:, :, :],
                              ALU.mult, ALU.add)
    b.ve.drain()
    b.ve.tensor_reduce(red[:, :], tmp[:, :, :], axis=AX.X, op=ALU.max)
    b.ve.drain()
    for tc in range(8):
        b.ve.tensor_scalar(tmp[:, tc, :], tmp[:, tc, :], red[:, tc:tc + 1], None,
                           ALU.is_equal)
    b.ve.drain()
    b.ve.tensor_tensor(msk[:, :, :], msk[:, :, :], tmp[:, :, :], ALU.add)
    b.ve.drain()
    b.ve.tensor_tensor(p[:, :, :], p[:, :, :], msk[:, :, :], ALU.mult)
    b.ve.drain()
    if renorm:
        b.ve.tensor_reduce(red[:, :], p[:, :, :], axis=AX.X, op=ALU.add)
        b.ve.drain()
        b.ve.reciprocal(red[:, :], red[:, :])
        b.ve.drain()
        for tc in range(8):
            b.ve.tensor_scalar(p[:, tc, :], p[:, tc, :], red[:, tc:tc + 1], None,
                               ALU.mult)
        b.ve.drain()
    tv = b.inc("ve", b.ve.tensor_copy(dwtm[:, :, :], p[:, :, :]))
    tv2 = None
    if sgtm is not None:
        b.wait(b.se, tmm)
        b.se.drain()
        tv2 = b.inc("se", b.se.activation(sgtm[:, :], logits[:, :, 8], AF.Sigmoid))
    b.wait(b.pe, tv)
    for tc in range(8):
        nrow = 8 + (1 if sgtm is not None else 0)
        if sgtm is not None:
            src = b.tile((128, 9), BF16)
            b.wait(b.ve, tv2)
            b.ve.tensor_copy(src[:, 0:8], dwtm[:, tc, :])
            b.ve.drain()
            tvv = b.inc("ve", b.ve.tensor_copy(src[:, 8:9], sgtm[:, tc:tc + 1]))
            b.wait(b.pe, tvv)
            b.tr(b.pE.bitcast(BF16)[0:nrow, tc * 128:(tc + 1) * 128],
                           src[:, :], cons["ident_b"][:, :])
        else:
            b.tr(b.pE.bitcast(BF16)[0:8, tc * 128:(tc + 1) * 128],
                           dwtm[:, tc, :], cons["ident_b"][:, :])
    tpe = b.pe_tok()
    b.wait(b.ve, tpe)
    b.inc("ve", b.ve.tensor_copy(
        dwT[:, :], b.pE.bitcast(BF16)[0:dwT.shape[0], 0:1024]))
    b.barrier()
    b.release()


def emit_bcast_rows(b, cons, rowsrc, nrows, out_bf, srow=0):
    for e in range(nrows):
        ei = min(srow + e, 8)
        nr = rowsrc.shape[0]
        last = None
        for half in range(2):
            last = b.mm(b.pC[:, half * 512:(half + 1) * 512],
                               cons["i8e_b"][ei][0:nr, :],
                               rowsrc[0:nr, half * 512:(half + 1) * 512],
                               start=True, stop=True)
        t = b.pe_tok()
        b.wait(b.ve, t)
        b.inc("ve", b.ve.tensor_copy(out_bf[:, e, :], b.pC[:, 0:1024]))
        b.ve.drain()
        b.wait(b.pe, b.tok("ve"))



def emit_ffn(b, cons, L, x_in, x_out, dram):
    b.mark()
    xnorm = b.tile((128, 16, T), BF16)
    y_acc = b.tile((128, 16, T))
    emit_ln(b, cons, x_in, xnorm, L["eps_ln"])

    nsig = L.get("nsig", 0)
    sigb = b.tile((128, 1, 1024), BF16) if nsig else None

    for m in range(16):
        b.po.memset(y_acc[:, m, :], 0.0)
    b.po.drain()
    b.inc("po", b.po.nop(nofuse=True))

    def rhs(kc, t0, t1):
        return xnorm[:, kc, t0:t1]

    b.mark()
    dwT = b.tile((8 + nsig, 1024), BF16)
    emit_gating(b, cons, L, xnorm, dwT, renorm=L["renorm"])
    dwb = b.tile((128, 8, 1024), BF16)
    emit_bcast_rows(b, cons, dwT, 8, dwb)
    if nsig:
        emit_bcast_rows(b, cons, dwT, 1, sigb, srow=8)
    b.barrier()
    nhc = L["EI"] // 128
    hbuf = b.tile((128, nhc, T), BF16)
    tmpb = [b.tile((128, 512), BF16) for _ in range(2)]
    gu = b.tile((128, L["MI"], T), BF16) if L["wgt"] is None else None
    cell = {"prev_ve": None, "slot": 0}

    for e in range(L["E"]):
        if L["wgt"] is not None:
            gt_d = cons[L["wgt"]][e]
            up_d = cons[L["wup"]][e]

            def epi_h(m, t0, t1, ps, ps2, tok, e=e):
                sl = cell["slot"] % 2
                cell["slot"] += 1
                b.wait(b.se, tok)
                if cell["prev_ve"] is not None:
                    b.wait(b.se, cell["prev_ve"])
                b.se.drain()
                tsl = b.inc("se", b.se.activation(tmpb[sl][:, 0:t1 - t0], ps,
                                                  AF.Silu))
                b.wait(b.ve, tsl)
                b.ve.tensor_tensor(hbuf[:, m, t0:t1], tmpb[sl][:, 0:t1 - t0],
                                   ps2, ALU.mult)
                b.ve.drain()
                b.ve.tensor_tensor(hbuf[:, m, t0:t1], hbuf[:, m, t0:t1],
                                   dwb[:, e, t0:t1], ALU.mult)
                b.ve.drain()
                cell["prev_ve"] = b.inc("ve", b.ve.nop(nofuse=True))
                return cell["prev_ve"]

            emit_gemm(b, gt_d, L["MI"], 16, rhs, epi_h, w2_dram=up_d)
        else:

            def epi_gu(m, t0, t1, ps, ps2, tok):
                b.wait(b.se, tok)
                b.se.drain()
                b.inc("se", b.se.copy(gu[:, m, t0:t1], ps))
                return b.tok("se")

            emit_gemm(b, cons[L["wgup"]][e], L["MI"], 16, rhs, epi_gu)
            half = L["MI"] // 2
            for mi in range(half):
                for nt in range(2):
                    t0, t1 = nt * 512, (nt + 1) * 512
                    sl = cell["slot"] % 2
                    cell["slot"] += 1
                    if cell["prev_ve"] is not None:
                        b.wait(b.se, cell["prev_ve"])
                    b.se.drain()
                    tsl = b.inc("se", b.se.activation(
                        tmpb[sl][:, 0:512], gu[:, mi, t0:t1], AF.Silu))
                    b.wait(b.ve, tsl)
                    b.ve.tensor_tensor(hbuf[:, mi, t0:t1], tmpb[sl][:, 0:512],
                                       gu[:, half + mi, t0:t1], ALU.mult)
                    b.ve.drain()
                    b.ve.tensor_tensor(hbuf[:, mi, t0:t1], hbuf[:, mi, t0:t1],
                                       dwb[:, e, t0:t1], ALU.mult)
                    b.ve.drain()
                    cell["prev_ve"] = b.inc("ve", b.ve.nop(nofuse=True))
            b.barrier()

        def rhs_h(kc, t0, t1):
            return hbuf[:, kc, t0:t1]

        def epi_y(m, t0, t1, ps, ps2, tok):
            b.wait(b.ve, tok)
            b.ve.tensor_tensor(y_acc[:, m, t0:t1], y_acc[:, m, t0:t1], ps,
                               ALU.add)
            b.ve.drain()
            return b.inc("ve", b.ve.nop(nofuse=True))

        emit_gemm(b, cons[L["wdn"]][e], 16, nhc, rhs_h, epi_y)
    b.release()

    b.mark()
    msh = L["MSH"]
    nsplit = L["sh_split"]
    hsh = b.tile((128, msh, T // nsplit), BF16)
    tmpb2 = [b.tile((128, 512), BF16) for _ in range(2)]
    cell2 = {"prev_ve": None, "slot": 0}
    for sp in range(nsplit):
        tt0 = sp * (T // nsplit)

        def rhs_s(kc, t0, t1, tt0=tt0):
            return xnorm[:, kc, tt0 + t0:tt0 + t1]

        def epi_hs(m, t0, t1, ps, ps2, tok):
            sl = cell2["slot"] % 2
            cell2["slot"] += 1
            b.wait(b.se, tok)
            if cell2["prev_ve"] is not None:
                b.wait(b.se, cell2["prev_ve"])
            b.se.drain()
            tsl = b.inc("se", b.se.activation(tmpb2[sl][:, 0:t1 - t0], ps,
                                              AF.Silu))
            b.wait(b.ve, tsl)
            b.ve.tensor_tensor(hsh[:, m, t0:t1], tmpb2[sl][:, 0:t1 - t0], ps2,
                               ALU.mult)
            b.ve.drain()
            cell2["prev_ve"] = b.inc("ve", b.ve.nop(nofuse=True))
            return cell2["prev_ve"]

        emit_gemm(b, cons[L["wsg"]], msh, 16, rhs_s, epi_hs,
                  w2_dram=cons[L["wsu"]], n_tok=T // nsplit)

        def rhs_hs(kc, t0, t1):
            return hsh[:, kc, t0:t1]

        def epi_ys(m, t0, t1, ps, ps2, tok, tt0=tt0):
            b.wait(b.ve, tok)
            if nsig:
                b.ve.tensor_tensor(ps, ps, sigb[:, 0, tt0 + t0:tt0 + t1], ALU.mult)
                b.ve.drain()
            b.ve.tensor_tensor(y_acc[:, m, tt0 + t0:tt0 + t1],
                               y_acc[:, m, tt0 + t0:tt0 + t1], ps, ALU.add)
            b.ve.drain()
            return b.inc("ve", b.ve.nop(nofuse=True))

        emit_gemm(b, cons[L["wsd"]], 16, msh, rhs_hs, epi_ys, n_tok=T // nsplit)
    b.release()

    b.mark()
    x_ld = [b.tile((128, T)) for _ in range(2)]
    res = [b.tile((128, T)) for _ in range(2)]
    vt = {}
    stok = {}
    for m in range(16):
        if m >= 2:
            b.wait(b.sy, vt[m - 2])
        tok = b.dma(b.sy, f"lnx{m % 2}", x_ld[m % 2][:, :], x_in[m])
        b.wait(b.ve, tok)
        if m >= 2:
            b.wait(b.ve, stok[m - 2])
        vt[m] = b.inc("ve", b.ve.tensor_tensor(res[m % 2][:, :], y_acc[:, m, :],
                                               x_ld[m % 2][:, :], ALU.add))
        b.wait(b.po, vt[m])
        stok[m] = b.dma(b.po, f"st{m % 2}", x_out[m], res[m % 2][:, :])
    b.barrier()
    b.release()
    b.release()



def emit_attn(b, cons, L, x_in, x_out, dram):
    import os
    KDBG = int(os.environ.get("KDBG", "0"))
    nc_x = L["nch_xbc"]
    NH, P, N, G = 64, 64, L["N"], L["G"]
    NCH = 8
    b.mark()
    region = b.alloc(64 * 1024)
    DBG["region"] = region
    xnorm = b.view(region, (128, 16, T), BF16)
    y_buf = b.view(region, (128, 32, T), BF16)
    dt_raw = b.view(region + 32 * 1024, (64, T))
    dt_hm = b.view(region + 36 * 1024, (64, T))
    lndt_hm = b.view(region + 40 * 1024, (64, T))
    adt_hm = b.view(region + 44 * 1024, (64, T))
    nch_bc = nc_x - 32
    bc = b.tile((128, nch_bc, T), BF16)
    c_all = b.tile((64, T), BF16) if L["l0"] else None
    tail_st = b.tile((128, 256))
    gtail_lo = b.tile((128, 256))
    gtail_hi = b.tile((128, 256))
    diag = b.tile((128, 32, 128), BF16)
    colmisc = b.tile((64, 4))

    emit_ln(b, cons, x_in, xnorm, "eps5")

    tok = b.dma(b.sy, "misc0", colmisc[:, 0:1], cons[L["dtb"]][:, :])
    tok2 = b.dma(b.sy, "misc1", colmisc[:, 1:2], cons[L["alog"]][:, :])
    b.wait(b.se, tok2)
    b.se.drain()
    ts = b.inc("se", b.se.activation(colmisc[:, 1:2], colmisc[:, 1:2], AF.Exp))
    b.wait(b.ve, ts)
    b.ve.tensor_scalar(colmisc[:, 1:2], colmisc[:, 1:2], -1.0, None, ALU.mult)
    b.ve.memset(colmisc[:, 2:3], 1.0)
    b.ve.drain()
    b.inc("ve", b.ve.nop(nofuse=True))
    b.po.memset(tail_st[:, :], 0.0)
    b.po.drain()
    b.inc("po", b.po.nop(nofuse=True))
    dp = b.tile((128, 32))
    tok = b.dma(b.sy, "misc0", dp[:, :], cons[L["dpair"]][:, :])
    b.wait(b.ve, tok)
    for j in range(32):
        b.ve.tensor_scalar(diag[:, j, :], cons["ident_b"][:, :], dp[:, j:j + 1],
                           None, ALU.mult)
    b.ve.drain()
    b.inc("ve", b.ve.nop(nofuse=True))
    b.barrier()

    b.mark()
    z_stage = [b.tile((128, T), BF16) for _ in range(2)]
    x_stage = [b.tile((128, T), BF16) for _ in range(2)]
    st_z = {}
    st_x = {}

    def epi_in(m, t0, t1, ps, ps2, tok):
        if m < 32:
            sl = m % 2
            b.wait(b.se, tok)
            if m >= 2:
                b.wait(b.se, st_z[m - 2])
            b.se.drain()
            tsl = b.inc("se", b.se.activation(z_stage[sl][:, t0:t1], ps, AF.Silu))
            if t1 == T:
                b.wait(b.po, tsl)
                st_z[m] = b.dma(b.po, f"st{sl}", dram["z_spill"][m],
                                z_stage[sl][:, :])
            return tsl
        if m < 32 + nc_x:
            mx = m - 32
            sl = m % 2
            b.wait(b.se, tok)
            if m >= 34:
                b.wait(b.se, st_x[m - 2])
            b.se.drain()
            tsl = b.inc("se", b.se.copy(x_stage[sl][:, t0:t1], ps))
            b.wait(b.ve, tsl)
            if t0 == 0:
                b.ve.tensor_copy(tail_st[:, mx * 2:mx * 2 + 2], ps[:, 0:2])
            if t1 == T:
                b.ve.tensor_copy(tail_st[:, nc_x * 2 + mx * 3:nc_x * 2 + mx * 3 + 3],
                                 ps[:, 509:512])
            b.ve.drain()
            tv = b.inc("ve", b.ve.nop(nofuse=True))
            if t1 == T:
                b.wait(b.po, tv)
                st_x[m] = b.dma(b.po, f"st{2 + sl}", dram["xbc_spill"][mx],
                                x_stage[sl][:, :])
            return tv
        b.wait(b.se, tok)
        b.se.drain()
        return b.inc("se", b.se.copy(dt_raw[0:64, t0:t1], ps[0:64, :]))

    def rhs_in(kc, t0, t1):
        return xnorm[:, kc, t0:t1]

    emit_gemm(b, cons[L["win"]], L["M_in"], 16, rhs_in, epi_in)

    cin_t, cout_t = dram["cin_tail"], dram["cout_tail"]
    b.wait(b.po, b.tok("ve"))
    tst = b.dma(b.po, "st0", cin_t.ap(), tail_st[:, :])
    b.wait(b.po, tst)
    tcc = b.inc("cc", b.po.collective_compute(
        "AllGather", ALU.bypass, replica_groups=[[0, 1], [2, 3], [4, 5], [6, 7]],
        ins=[cin_t.ap().opt()], outs=[cout_t.ap().opt()]))
    b.wait(b.po, tcc)
    tg1 = b.dma(b.po, "st0", gtail_lo[:, :], cout_t.ap()[0:128, :])
    tg2 = b.dma(b.po, "st1", gtail_hi[:, :], cout_t.ap()[128:256, :])
    b.barrier()
    b.release()

    b.mark()
    cw = b.tile((128, nc_x, 4))
    cb = b.tile((128, nc_x))
    cbuf = [b.tile((128, 1032), BF16) for _ in range(3)]
    ctmp = [b.tile((128, T)) for _ in range(2)]
    hs_stage = [b.tile((128, T), BF16) for _ in range(2)]
    tok = b.dma(b.sy, "misc0", cw[:, :, :], cons[L["convw"]][:, :, :])
    tok2 = b.dma(b.sy, "misc1", cb[:, :], cons[L["convb"]][:, :])
    b.wait(b.ve, tok)
    b.wait(b.ve, tok2)
    pl = L["pad_left"]
    rdt = {}
    vet = {}
    set_ = {}
    stt = {}
    for m in range(nc_x):
        sl = m % 3
        if m >= 3:
            b.wait(b.sy, vet[m - 3])
        rdt[m] = b.dma(b.sy, f"cv{sl}", cbuf[sl][:, 3:1027], dram["xbc_spill"][m])
        b.wait(b.ve, rdt[m])
        if m >= 1:
            b.wait(b.ve, set_[m - 1])
        b.ve.tensor_scalar(ctmp[0][:, 0:3],
                           gtail_lo[:, nc_x * 2 + m * 3:nc_x * 2 + m * 3 + 3],
                           cons["flag_hi"][:, 0:1], None, ALU.mult)
        b.ve.drain()
        b.ve.tensor_copy(cbuf[sl][:, 0:3], ctmp[0][:, 0:3])
        b.ve.drain()
        if L["pad_left"] == 1:
            b.ve.tensor_scalar(ctmp[0][:, 0:2],
                               gtail_hi[:, m * 2:m * 2 + 2],
                               cons["flag_lo"][:, 0:1], None, ALU.mult)
            b.ve.drain()
            b.ve.tensor_copy(cbuf[sl][:, 1027:1029], ctmp[0][:, 0:2])
            b.ve.drain()
        else:
            b.ve.memset(cbuf[sl][:, 1027:1029], 0.0)
            b.ve.drain()
        off0 = 3 - pl
        b.ve.tensor_scalar(ctmp[0][:, :], cbuf[sl][:, off0:off0 + 1024],
                           cw[:, m, 0:1], None, ALU.mult)
        b.ve.drain()
        for k in range(1, 4):
            b.ve.scalar_tensor_tensor(ctmp[0][:, :],
                                      cbuf[sl][:, off0 + k:off0 + k + 1024],
                                      cw[:, m, k:k + 1], ctmp[0][:, :],
                                      ALU.mult, ALU.add)
            b.ve.drain()
        vet[m] = b.inc("ve", b.ve.nop(nofuse=True))
        if m >= 1:
            b.wait(b.pe, vet[m - 1])
        b.pe_warm(cons, 28)
        b.wait(b.se, vet[m])
        if m < 32:
            hsl = m % 2
            if m >= 2:
                b.wait(b.se, stt[m - 2])
            b.se.drain()
            tse = b.inc("se", b.se.activation(hs_stage[hsl][:, :], ctmp[0][:, :],
                                              AF.Silu, bias=cb[:, m:m + 1]))
            b.wait(b.po, tse)
            stt[m] = b.dma(b.po, f"st{hsl}", dram["hs_spill"][m],
                           hs_stage[hsl][:, :])
            set_[m] = tse
        else:
            b.se.drain()
            set_[m] = b.inc("se", b.se.activation(bc[:, m - 32, :], ctmp[0][:, :],
                                                  AF.Silu, bias=cb[:, m:m + 1]))
    b.barrier()
    if L["l0"]:
        tcp = b.dma(b.po, "st0", c_all[:, :], bc[64:128, 0, :])
        b.barrier()
    b.release()

    b.ve.tensor_scalar(dt_raw[0:64, :], dt_raw[0:64, :], colmisc[0:64, 0:1],
                       None, ALU.add)
    b.ve.drain()
    tv = b.inc("ve", b.ve.nop(nofuse=True))
    b.wait(b.se, tv)
    b.se.activation(lndt_hm[0:64, :], dt_raw[0:64, :], AF.Abs)
    b.se.drain()
    b.se.activation(lndt_hm[0:64, :], lndt_hm[0:64, :], AF.Exp, scale=-1.0)
    b.se.drain()
    ts = b.inc("se", b.se.activation(lndt_hm[0:64, :], lndt_hm[0:64, :], AF.Ln,
                                     bias=colmisc[0:64, 2:3]))
    b.wait(b.ve, ts)
    b.ve.tensor_scalar(dt_hm[0:64, :], dt_raw[0:64, :], 0.0, None, ALU.max)
    b.ve.drain()
    b.ve.tensor_tensor(dt_hm[0:64, :], dt_hm[0:64, :], lndt_hm[0:64, :], ALU.add)
    b.ve.drain()
    b.ve.tensor_scalar(dt_hm[0:64, :], dt_hm[0:64, :], 1e-30, None, ALU.max)
    b.ve.drain()
    b.ve.tensor_scalar(adt_hm[0:64, :], dt_hm[0:64, :], colmisc[0:64, 1:2],
                       None, ALU.mult)
    b.ve.drain()
    tv = b.inc("ve", b.ve.nop(nofuse=True))
    b.wait(b.se, tv)
    b.se.drain()
    b.inc("se", b.se.activation(lndt_hm[0:64, :], dt_hm[0:64, :], AF.Ln))
    b.barrier()

    b.mark()
    St = b.tile((N, 64 * P))
    stbf = b.tile((N, 64 * P), BF16)
    x_tm = b.tile((128, 32, 128), BF16)
    hswin = [b.tile((128, 32, 128), BF16) for _ in range(3)]
    b_tm = b.tile((128, G, L["N"]), BF16)
    acum = b.tile((128, 8, 64))
    vtile = b.tile((128, 8, 64))
    dtdec = b.tile((128, 8, 64))
    lamrep = b.tile((128, 8, 64))
    expacum = b.tile((128, 8, 64), BF16)
    dt_tm = b.tile((128, 64))
    scr = b.tile((128, 64))
    lam_raw = b.tile((64, 1))
    lam_rowt = b.tile((1, 64))
    xd2 = [b.tile((128, 512), BF16) for _ in range(2)]
    xd_st = xd2[0]
    cbt = b.tile((128, G, 128), BF16)
    mt2 = [b.tile((128, 1024), BF16) for _ in range(2)]
    e1 = b.tile((128, 1024))
    ea_s = b.tile((128, 512))
    t1t = b.tile((128, 512))
    t2t = t1t.rearrange("p (a c) -> p a c", a=4)
    zbuf = [b.tile((128, 4, 128), BF16) for _ in range(2)]

    dbg_scr = b.tile((128, 512)) if KDBG else None

    def dbg_dump(ap, idx, np_=128):
        nfree = 1
        for d in ap.shape[1:]:
            nfree *= d
        b.barrier()
        b.ve.tensor_copy(dbg_scr[0:np_, 0:nfree], ap)
        b.ve.drain()
        b.inc("ve", b.ve.nop(nofuse=True))
        b.barrier()
        b.dma(b.po, "st0", dram["dbg"][idx][0:np_, 0:nfree],
              dbg_scr[0:np_, 0:nfree])
        b.barrier()

    b.po.memset(St[:, :], 0.0)
    b.po.drain()
    b.inc("po", b.po.nop(nofuse=True))
    b.barrier()

    def chunk_prep(c):
        cs = slice(c * 128, (c + 1) * 128)
        pp = b.pE
        b.tr(pp[:, 0:64], adt_hm[0:64, cs], cons["ident_f"][0:64, 0:64])
        b.tr(pp[:, 64:128], dt_hm[0:64, cs], cons["ident_f"][0:64, 0:64])
        b.tr(pp[:, 128:192], lndt_hm[0:64, cs],
                       cons["ident_f"][0:64, 0:64])
        t = b.pe_tok()
        b.wait(b.se, t)
        b.se.copy(scr[:, :], pp[:, 0:64])
        b.se.copy(dt_tm[:, :], pp[:, 64:128])
        b.se.drain()
        ts = b.inc("se", b.se.copy(vtile[:, c, :], pp[:, 128:192]))
        b.wait(b.pe, ts)
        b.mm(pp[:, 192:256], cons["triu_f"][:, :], scr[:, :],
                    start=True, stop=True)
        t = b.pe_tok()
        b.wait(b.se, t)
        ts = b.inc("se", b.se.copy(acum[:, c, :], pp[:, 192:256]))
        b.wait(b.pe, ts)
        b.wait(b.sy, ts)
        trow = b.dma(b.sy, "lam0", lam_rowt[:, :], acum[127:128, c, :])
        b.wait(b.pe, trow)
        b.tr(pp[0:64, 256:257], lam_rowt[0:1, :],
                       cons["ident_f"][0:1, 0:1])
        t = b.pe_tok()
        b.wait(b.se, t)
        ts = b.inc("se", b.se.copy(lam_raw[:, :], pp[0:64, 256:257]))
        b.wait(b.pe, ts)
        b.tr(pp[:, 320:384],
                       lam_raw[:, 0:1].broadcast_to([64, 128]),
                       cons["ident_f"][0:64, 0:64])
        t = b.pe_tok()
        b.wait(b.ve, t)
        b.ve.tensor_tensor(scr[:, :], pp[:, 320:384], acum[:, c, :], ALU.subtract)
        b.ve.drain()
        tv = b.inc("ve", b.ve.tensor_tensor(vtile[:, c, :], vtile[:, c, :],
                                            acum[:, c, :], ALU.subtract))
        b.wait(b.se, tv)
        b.se.activation(scr[:, :], scr[:, :], AF.Exp)
        b.se.drain()
        b.se.activation(lamrep[:, c, :], pp[:, 320:384], AF.Exp)
        b.se.drain()
        ts = b.inc("se", b.se.activation(expacum[:, c, :], acum[:, c, :], AF.Exp))
        b.wait(b.ve, ts)
        tv = b.inc("ve", b.ve.tensor_tensor(dtdec[:, c, :], scr[:, :],
                                            dt_tm[:, :], ALU.mult))
        return tv

    def load_hswin(c, sl):
        b.wait(b.sy, b.tok("pe"))
        b.wait(b.po, b.tok("pe"))
        src = dram["hs_spill"][0:32, :, c * 128:(c + 1) * 128] \
            .rearrange("a p c -> p a c")
        b.dma(b.sy, f"hw{sl}", hswin[sl][:, 0:16, :], src[:, 0:16, :])
        return b.dma(b.po, f"hw{sl}", hswin[sl][:, 16:32, :], src[:, 16:32, :])

    def x_transposes(c, sl, want_xd):
        for batch in range(8):
            pp = b.pD if batch % 2 == 0 else b.pE
            ppb = pp.bitcast(BF16)
            for f4 in range(4):
                f = batch * 4 + f4
                b.tr(ppb[:, f4 * 128:(f4 + 1) * 128],
                               hswin[sl][:, f, :], cons["ident_b"][:, :])
            t = b.pe_tok()
            b.wait(b.ve, t)
            b.ve.tensor_copy(x_tm[:, batch * 4:batch * 4 + 4, :]
                             .rearrange("p a c -> p (a c)"), ppb[:, 0:512])
            b.ve.drain()
            tv = b.inc("ve", b.ve.nop(nofuse=True))
            b.wait(b.pe, tv)
        if L["l0"]:
            b.tr(b.pD.bitcast(BF16)[:, 0:64], bc[0:64, 0, c * 128:(c + 1) * 128],
                           cons["ident_b"][0:64, 0:64])
            t = b.pe_tok()
            b.wait(b.ve, t)
            b.ve.tensor_copy(b_tm[:, 0, :], b.pD.bitcast(BF16)[:, 0:64])
        else:
            for g in range(8):
                pp = b.pD if g % 2 == 0 else b.pE
                b.tr(pp.bitcast(BF16)[:, 0:128],
                               bc[:, g, c * 128:(c + 1) * 128], cons["ident_b"][:, :])
                t = b.pe_tok()
                b.wait(b.ve, t)
                b.ve.tensor_copy(b_tm[:, g, :], pp.bitcast(BF16)[:, 0:128])
                b.ve.drain()
                tv = b.inc("ve", b.ve.nop(nofuse=True))
                b.wait(b.pe, tv)
        b.ve.drain()
        return b.inc("ve", b.ve.nop(nofuse=True))

    def lam_scale_st(c):
        for h in range(64):
            b.ve.tensor_scalar(St[:, h * P:(h + 1) * P], St[:, h * P:(h + 1) * P],
                               lamrep[0:N, c, h:h + 1], None, ALU.mult)
        b.ve.drain()
        return b.inc("ve", b.ve.nop(nofuse=True))

    def states_pass(c, sl, seed_scale):
        tlam = lam_scale_st(c)
        for g in range(8):
            bg = g if G == 8 else 0
            for hh in range(8):
                h = g * 8 + hh
                b.ve.tensor_scalar(xd_st[:, hh * 64:(hh + 1) * 64],
                                   x_tm[:, h // 2, (h % 2) * 64:(h % 2) * 64 + 64],
                                   dtdec[:, c, h:h + 1], None, ALU.mult)
            b.ve.drain()
            tv = b.inc("ve", b.ve.nop(nofuse=True))
            b.wait(b.pe, tv)
            half = g % 2
            for hh in range(8):
                b.mm(b.pC[0:N, half * 512 + hh * 64:half * 512 + hh * 64 + 64],
                            b_tm[:, bg, :], xd_st[:, hh * 64:(hh + 1) * 64],
                            start=True, stop=True)
            t = b.pe_tok()
            b.wait(b.ve, t)
            b.ve.tensor_tensor(St[:, g * 512:(g + 1) * 512], St[:, g * 512:(g + 1) * 512],
                               b.pC[0:N, half * 512:half * 512 + 512], ALU.add)
            b.ve.drain()
            tv = b.inc("ve", b.ve.nop(nofuse=True))
            b.wait(b.pe, tv)

    hw_toks = {0: load_hswin(0, 0), 1: load_hswin(1, 1)}
    for c in range(NCH):
        chunk_prep(c)
        b.wait(b.ve, hw_toks[c])
        b.wait(b.pe, hw_toks[c])
        if c + 2 < NCH:
            hw_toks[c + 2] = load_hswin(c + 2, (c + 2) % 3)
        x_transposes(c, c % 3, True)
        states_pass(c, c % 3, None)
    b.barrier()

    cin_s = dram["cin_st64"] if L["l0"] else dram["cin_st128"]
    cout_s = dram["cout_st64"] if L["l0"] else dram["cout_st128"]
    tst = b.dma(b.po, "st0", cin_s.ap()[0:N, 0:64 * P], St[:, :])
    b.wait(b.po, tst)
    tcc = b.inc("cc", b.po.collective_compute(
        "AllGather", ALU.bypass, replica_groups=[[0, 1], [2, 3], [4, 5], [6, 7]],
        ins=[cin_s.ap().opt()], outs=[cout_s.ap().opt()]))
    b.wait(b.po, tcc)
    tg = b.dma(b.po, "st0", St[:, :], cout_s.ap()[0:N, 0:64 * P])
    b.wait(b.ve, tg)
    for h in range(64):
        b.ve.tensor_scalar(St[:, h * P:(h + 1) * P], St[:, h * P:(h + 1) * P],
                           cons["flag_hi"][0:N, 0:1], None, ALU.mult)
    b.ve.drain()
    b.inc("ve", b.ve.nop(nofuse=True))
    b.barrier()

    zr = {}

    def load_z(c, g, sl):
        b.wait(b.sy, b.tok("ve"))
        src = dram["z_spill"][4 * g:4 * g + 4, :, c * 128:(c + 1) * 128] \
            .rearrange("a p c -> p a c")
        return b.dma(b.sy, f"zr{sl}", zbuf[sl][:, :, :], src)

    hw_toks = {0: load_hswin(0, 0), 1: load_hswin(1, 1)}
    zr[(0, 0)] = load_z(0, 0, 0)
    zslot = {(0, 0): 0}
    state = {"mt_tok": None, "ea_tok": None, "stadd_tok": None, "xd_tok": None,
             "e1_tok": None, "ydiag_tok": {}, "states_tok": {},
             "comb_tok2": {}}

    def prep_slot(c, g):
        cs = slice(c * 128, (c + 1) * 128)
        if state["e1_tok"] is not None:
            b.wait(b.pe, state["e1_tok"])
        for j8 in range(8):
            h = g * 8 + j8
            b.tr(b.pA[:, j8 * 128:(j8 + 1) * 128],
                 acum[:, c, h:h + 1].broadcast_to([128, 128]),
                 cons["ident_f"][:, :])
        t1tok = b.pe_tok()
        b.wait(b.ve, t1tok)
        vview = vtile[:, c, g * 8:(g + 1) * 8].unsqueeze(2) \
            .broadcast_to([128, 8, 128])
        b.ve.tensor_tensor(e1.rearrange("p (a d) -> p a d", a=8),
                           b.pA.rearrange("p (a d) -> p a d", a=8),
                           vview, ALU.add)
        b.ve.drain()
        state["e1_tok"] = b.inc("ve", b.ve.tensor_scalar(
            e1[:, :], e1[:, :], 30.0, None, ALU.min))
        b.wait(b.se, state["e1_tok"])
        b.se.drain()
        texp = b.inc("se", b.se.activation(e1[:, :], e1[:, :], AF.Exp))
        b.wait(b.ve, texp)
        par = (c * 8 + g) % 2
        if state["ydiag_tok"].get(par) is not None:
            b.wait(b.ve, state["ydiag_tok"][par])
        bg = g if G == 8 else 0
        cview = cbt[:, bg, :].unsqueeze(1).broadcast_to([128, 8, 128])
        b.ve.tensor_tensor(mt2[par].rearrange("p (a d) -> p a d", a=8),
                           e1.rearrange("p (a d) -> p a d", a=8),
                           cview, ALU.mult)
        b.ve.drain()
        state["mt_tok"] = b.inc("ve", b.ve.nop(nofuse=True))
        if state["states_tok"].get(par) is not None:
            b.wait(b.ve, state["states_tok"][par])
        for j8 in range(8):
            h = g * 8 + j8
            b.ve.tensor_scalar(xd2[par][:, j8 * 64:(j8 + 1) * 64],
                               x_tm[:, h // 2, (h % 2) * 64:(h % 2) * 64 + 64],
                               dtdec[:, c, h:h + 1], None, ALU.mult)
        b.ve.drain()
        state["xd_tok"] = b.inc("ve", b.ve.nop(nofuse=True))

    def ywork_slot(c, g):
        cs = slice(c * 128, (c + 1) * 128)
        bg = g if G == 8 else 0
        par = (c * 8 + g) % 2
        pY = b.pB if par == 0 else b.pC
        b.wait(b.pe, state["mt_tok"])
        if state["comb_tok2"].get(par) is not None:
            b.wait(b.pe, state["comb_tok2"][par])
        mt8 = mt2[par]
        first = True
        for j8 in range(8):
            h = g * 8 + j8
            jp = j8 // 2
            b.mm(pY[(h % 2) * 64:(h % 2) * 64 + 64, jp * 128:(jp + 1) * 128],
                 x_tm[:, h // 2, (h % 2) * 64:(h % 2) * 64 + 64],
                 mt8[:, j8 * 128:(j8 + 1) * 128],
                 start=(j8 <= 1), stop=False, skip_group_check=True,
                 tile_position=(0, (h % 2) * 64))
        ydiag_tok = b.pe_tok()
        state["ydiag_tok"][par] = ydiag_tok
        for j in range(4):
            jj = 4 * g + j
            b.mm(pY[:, j * 128:(j + 1) * 128], diag[:, jj, :],
                 hswin[c % 3][:, jj, :], start=False, stop=(j == 3),
                 skip_group_check=True)
        for j8 in range(8):
            h = g * 8 + j8
            jp = j8 // 2
            if L["l0"]:
                crhs = c_all[:, cs]
                lhs = stbf[0:64, h * P:(h + 1) * P]
            else:
                crhs = bc[:, 8 + bg, cs]
                lhs = stbf[:, h * P:(h + 1) * P]
            b.mm(pY[(h % 2) * 64:(h % 2) * 64 + 64,
                    512 + jp * 128:512 + (jp + 1) * 128],
                 lhs, crhs, start=(j8 <= 1), stop=(j8 == 7),
                 skip_group_check=True, tile_position=(0, (h % 2) * 64))
        if state["ea_tok"] is not None:
            b.wait(b.pe, state["ea_tok"])
        pdb = b.pD.bitcast(BF16)
        for j in range(4):
            for parh in range(2):
                h = g * 8 + 2 * j + parh
                eview = expacum[:, c, h:h + 1].broadcast_to([128, 64])
                b.tr(pdb[parh * 64:parh * 64 + 64, j * 128:(j + 1) * 128],
                     eview, cons["ident_b"][:, :])
        b.wait(b.pe, state["xd_tok"])
        if state["stadd_tok"] is not None:
            b.wait(b.pe, state["stadd_tok"])
        for j8 in range(8):
            b.mm(b.pE[0:N, j8 * 64:(j8 + 1) * 64], b_tm[:, bg, :],
                 xd2[par][:, j8 * 64:(j8 + 1) * 64], start=True, stop=True)
        tall = b.pe_tok()
        state["states_tok"][par] = tall
        b.wait(b.se, tall)
        b.se.drain()
        tea = b.inc("se", b.se.copy(ea_s[:, :], pdb[:, 0:512]))
        state["ea_tok"] = tea
        b.wait(b.ve, tall)
        b.ve.tensor_tensor(St[:, g * 512:(g + 1) * 512],
                           St[:, g * 512:(g + 1) * 512],
                           b.pE[0:N, 0:512], ALU.add)
        b.ve.drain()
        state["stadd_tok"] = b.inc("ve", b.ve.nop(nofuse=True))
        b.wait(b.ve, tea)
        b.ve.tensor_tensor(t1t[:, :], pY[:, 512:1024], ea_s[:, :], ALU.mult)
        b.ve.drain()
        b.ve.scalar_tensor_tensor(t1t[:, :], t1t[:, :], 0.0, pY[:, 0:512],
                                  ALU.add, ALU.add)
        b.ve.drain()
        b.wait(b.ve, zr[(c, g)])
        ng = (c, g + 1) if g < 7 else (c + 1, 0)
        if ng[0] < NCH:
            zsl = (zslot[(c, g)] + 1) % 2
            zslot[ng] = zsl
            zr[ng] = load_z(ng[0], ng[1], zsl)
        b.ve.tensor_tensor(y_buf[:, 4 * g:4 * g + 4, cs],
                           t1t.rearrange("p (a d) -> p a d", a=4),
                           zbuf[zslot[(c, g)]][:, :, :], ALU.mult)
        b.ve.drain()
        state["comb_tok2"][(c * 8 + g) % 2] = b.inc("ve", b.ve.nop(nofuse=True))

    for c in range(NCH):
        b.wait(b.ve, hw_toks[c])
        b.wait(b.pe, hw_toks[c])
        if c + 2 < NCH:
            hw_toks[c + 2] = load_hswin(c + 2, (c + 2) % 3)
        if state["ea_tok"] is not None:
            b.wait(b.pe, state["ea_tok"])
        if state["stadd_tok"] is not None:
            b.wait(b.pe, state["stadd_tok"])
        x_transposes(c, c % 3, False)
        cs = slice(c * 128, (c + 1) * 128)
        b.wait(b.pe, b.tok("ve"))
        ncb = 8 if G == 8 else 1
        for bg in range(ncb):
            pp = b.pD if bg < 4 else b.pE
            off = (bg % 4) * 128
            if L["l0"]:
                b.mm(pp[:, off:off + 128], bc[0:64, 0, cs], c_all[:, cs],
                     start=True, stop=True)
            else:
                b.mm(pp[:, off:off + 128], bc[:, bg, cs], bc[:, 8 + bg, cs],
                     start=True, stop=True)
        tcb = b.pe_tok()
        b.wait(b.ve, tcb)
        for bg in range(ncb):
            pp = b.pD if bg < 4 else b.pE
            off = (bg % 4) * 128
            b.ve.tensor_tensor(cbt[:, bg, :], pp[:, off:off + 128],
                               cons["triu_f"][:, :], ALU.mult)
        b.ve.drain()
        b.wait(b.ve, b.tok("pe"))
        b.inc("ve", b.ve.tensor_copy(stbf[:, :], St[:, :]))
        b.ve.drain()
        lam_scale_st(c)
        tbnd = b.tok("ve")
        b.wait(b.pe, tbnd)
        state["ea_tok"] = None
        state["stadd_tok"] = None
        for g in range(8):
            prep_slot(c, g)
            if g >= 1:
                ywork_slot(c, g - 1)
        ywork_slot(c, 7)
    b.barrier()

    normw = b.tile((128, 32))
    rstd = b.tile((128, T))
    sqb = b.tile((128, T), BF16)
    tok = b.dma(b.sy, "misc0", normw[:, :], cons[L["normw"]][:, :])
    pet = {}
    for f in range(32):
        b.wait(b.se, tok)
        if f >= 1:
            b.wait(b.se, pet[f - 1])
        b.se.drain()
        tsq = b.inc("se", b.se.activation(sqb[:, :], y_buf[:, f, :], AF.Square))
        b.wait(b.pe, tsq)
        b.mm(b.pA[:, 0:512], cons["onesb"][:, :], sqb[:, 0:512],
                    start=(f == 0), stop=(f == 31))
        b.mm(b.pA[:, 512:1024], cons["onesb"][:, :], sqb[:, 512:1024],
                    start=(f == 0), stop=(f == 31))
        pet[f] = b.pe_tok()
    b.wait(b.ve, pet[31])
    b.ve.tensor_scalar(rstd[:, :], b.pA[:, :], 1.0 / 4096.0, None, ALU.mult)
    b.ve.drain()
    tv = b.inc("ve", b.ve.nop(nofuse=True))
    b.wait(b.se, tv)
    b.se.activation(rstd[:, :], rstd[:, :], AF.Ln, bias=cons[L["eps_rms"]][:, 0:1])
    b.se.drain()
    ts = b.inc("se", b.se.activation(rstd[:, :], rstd[:, :], AF.Exp, scale=-0.5))
    b.wait(b.ve, ts)
    if KDBG == 2:
        dbg_dump(rstd[:, 0:512], 10)
    for f in range(32):
        b.ve.scalar_tensor_tensor(y_buf[:, f, :], y_buf[:, f, :],
                                  normw[:, f:f + 1], rstd[:, :],
                                  ALU.mult, ALU.mult)
    b.ve.drain()
    b.inc("ve", b.ve.nop(nofuse=True))
    if KDBG == 2:
        dbg_dump(y_buf[:, 0:4, 0:128], 11)
    b.barrier()
    b.release()

    b.mark()
    x_ld = [b.tile((128, T)) for _ in range(2)]
    res = [b.tile((128, T)) for _ in range(2)]
    vt = {}
    stok = {}

    def rhs_out(kc, t0, t1):
        return y_buf[:, kc, t0:t1]

    last_tv = {}

    def epi_out(m, t0, t1, ps, ps2, tok):
        if t0 == 0:
            if m >= 2:
                b.wait(b.sy, last_tv[m - 2])
            vt[m] = b.dma(b.sy, f"lnx{m % 2}", x_ld[m % 2][:, :], x_in[m])
        b.wait(b.ve, tok)
        b.wait(b.ve, vt[m])
        if m >= 2 and t0 == 0:
            b.wait(b.ve, stok[m - 2])
        tv = b.inc("ve", b.ve.tensor_tensor(res[m % 2][:, t0:t1], ps,
                                            x_ld[m % 2][:, t0:t1], ALU.add))
        if t1 == T:
            last_tv[m] = tv
            b.wait(b.po, tv)
            stok[m] = b.dma(b.po, f"st{m % 2}", x_out[m], res[m % 2][:, :])
        return tv

    emit_gemm(b, cons[L["wout"]], 16, 32, rhs_out, epi_out)
    b.release()
    b.release()


L0_ATTN = dict(l0=True, nch_xbc=33, M_in=66, N=64, G=1, pad_left=3,
               win="l0_win", wout="l0_wout", convw="l0_convw", convb="l0_convb",
               dtb="l0_dtb", alog="l0_alog", dpair="l0_dpair", normw="l0_normw",
               eps_rms="eps5")
L1_ATTN = dict(l0=False, nch_xbc=48, M_in=81, N=128, G=8, pad_left=1,
               win="l1_win", wout="l1_wout", convw="l1_convw", convb="l1_convb",
               dtb="l1_dtb", alog="l1_alog", dpair="l1_dpair", normw="l1_normw",
               eps_rms="eps6")



L0_FFN = dict(eps_ln="eps5", gwT="l0_gwT", gbrep="l0_gbrep", ngcol=8, nsig=0,
              renorm=False, E=8, EI=1024, MI=8, MSH=8, sh_split=1,
              wgt="l0_w1", wup="l0_w3", wgup=None, wdn="l0_w2",
              wsg="l0_ws1", wsu="l0_ws3", wsd="l0_ws2")
L1_FFN = dict(eps_ln="eps5", gwT="l1_gwT", gbrep=None, ngcol=9, nsig=1,
              renorm=True, E=8, EI=768, MI=12, MSH=32, sh_split=2,
              wgt=None, wup=None, wgup="l1_gup", wdn="l1_down",
              wsg="l1_sg", wsu="l1_su", wsd="l1_sd")

GLOBAL_KEYS = ["ident_f", "ident_b", "triu_f", "onesb", "i8e_b",
               "l0_win", "l0_wout", "l0_convw", "l0_convb", "l0_dtb", "l0_alog",
               "l0_dpair", "l0_normw",
               "l0_gwT", "l0_gbrep", "l0_w1", "l0_w3", "l0_w2", "l0_ws1",
               "l0_ws3", "l0_ws2",
               "l1_win", "l1_wout", "l1_convw", "l1_convb", "l1_dtb", "l1_alog",
               "l1_dpair", "l1_normw",
               "l1_gwT", "l1_gup", "l1_down", "l1_sg", "l1_su", "l1_sd"]


def _np_dtype_to_mybir(a):
    import ml_dtypes
    if a.dtype == np.float32:
        return F32
    if a.dtype == ml_dtypes.bfloat16:
        return BF16
    raise AssertionError(str(a.dtype))


def build_program(blocks, g):
    nc = bass.Bass("TRN2", target_bir_lowering=False)
    cons = {}
    for k in GLOBAL_KEYS:
        a = g[k]
        t = nc.dram_tensor(k, list(a.shape), _np_dtype_to_mybir(a),
                           kind="ExternalInput")
        cons[k] = t.ap()
    x0 = nc.dram_tensor("x0", [16, 128, T], F32, kind="ExternalInput").ap()
    y = nc.dram_tensor("y", [16, 128, T], F32, kind="ExternalOutput").ap()
    flag_hi = nc.dram_tensor("flag_hi", [128, 1], F32, kind="ExternalInput").ap()
    flag_lo = nc.dram_tensor("flag_lo", [128, 1], F32, kind="ExternalInput").ap()
    resA = nc.dram_tensor("resA", [16, 128, T], F32).ap()
    resB = nc.dram_tensor("resB", [16, 128, T], F32).ap()
    dram = dict(flag_hi=flag_hi, flag_lo=flag_lo,
                hs_spill=nc.dram_tensor("hs_spill", [32, 128, T], BF16).ap(),
                z_spill=nc.dram_tensor("z_spill", [32, 128, T], BF16).ap(),
                xbc_spill=nc.dram_tensor("xbc_spill", [48, 128, T], BF16).ap(),
                cin_tail=nc.dram_tensor("cin_tail", [128, 256], F32),
                cout_tail=nc.dram_tensor("cout_tail", [256, 256], F32),
                dbg=nc.dram_tensor("dbg", [16, 128, 512], F32),
                cin_st64=nc.dram_tensor("cin_st64", [64, 4096], F32),
                cout_st64=nc.dram_tensor("cout_st64", [128, 4096], F32),
                cin_st128=nc.dram_tensor("cin_st128", [128, 4096], F32),
                cout_st128=nc.dram_tensor("cout_st128", [256, 4096], F32))

    with ExitStack() as ctx:
        b = Builder(nc, ctx)
        blk = ctx.enter_context(nc.Block())
        e5 = b.tile((128, 1))
        e6 = b.tile((128, 1))
        b.po.memset(e5[:, :], 1e-5)
        b.po.memset(e6[:, :], 1e-6)
        b.po.drain()
        b.inc("po", b.po.nop(nofuse=True))
        b.barrier()
        cons["eps5"] = e5
        cons["eps6"] = e6

        for k, shape, dt in [("ident_f", (128, 128), F32),
                             ("ident_b", (128, 128), BF16),
                             ("triu_f", (128, 128), F32),
                             ("onesb", (128, 128), BF16)]:
            tile = b.tile(shape, dt)
            tok = b.inc("ld", b.sy.dma_start(out=tile[:, :], in_=cons[k][:, :]), 16)
            cons[k] = tile
        i8 = []
        for e in range(9):
            tile = b.tile((9, 128), BF16)
            b.inc("ld", b.sy.dma_start(out=tile[:, :], in_=cons["i8e_b"][e]), 16)
            i8.append(tile)
        cons["i8e_b"] = i8
        for nmf, src in [("flag_hi", flag_hi), ("flag_lo", flag_lo)]:
            tile = b.tile((128, 1))
            b.inc("ld", b.sy.dma_start(out=tile[:, :], in_=src[:, :]), 16)
            cons[nmf] = tile
        b.barrier()

        cur = x0
        nxt = [resA, resB]
        from_attn = globals().get("emit_attn")
        for i, name in enumerate(blocks):
            out = y if i == len(blocks) - 1 else nxt[i % 2]
            if name == "ffn0":
                emit_ffn(b, cons, L0_FFN, cur, out, dram)
            elif name == "ffn1":
                emit_ffn(b, cons, L1_FFN, cur, out, dram)
            elif name == "attn0":
                from_attn(b, cons, L0_ATTN, cur, out, dram)
            elif name == "attn1":
                from_attn(b, cons, L1_ATTN, cur, out, dram)
            cur = out
        b.wait(b.po, b.tok("gd"))
        b.wait(b.po, b.tok("ld"))
        print(f"[build] arena peak {b.peak/1024:.1f} KB/partition; "
              f"sems {b.cnt}")
    return nc



def run_blocks(inputs, blocks):
    g, percore = stage_inputs(inputs)
    nc = build_program(blocks, g)
    in_maps = []
    for c in range(NCORES):
        m = {k: g[k] for k in GLOBAL_KEYS}
        m.update(percore[c])
        in_maps.append(m)
    res = run_bass_kernel_spmd(nc, in_maps, core_ids=list(range(NCORES)))
    out = np.empty((4, 2048, 2048), np.float32)
    for c in range(NCORES):
        bb, h = c // 2, c % 2
        yc = np.asarray(res.results[c]["y"])
        out[bb, h * T:(h + 1) * T, :] = yc.reshape(2048, T).T
    return out


def kernel(**inputs):
    return run_blocks(inputs, ["attn0", "ffn0", "attn1", "ffn1"])


# version 14
# speedup vs baseline: 1.1772x; 1.1772x over previous
import numpy as np
from contextlib import ExitStack

import concourse.bass as bass
import concourse.mybir as mybir
from concourse.bass_utils import run_bass_kernel_spmd

F32 = mybir.dt.float32
BF16 = mybir.dt.bfloat16
AF = mybir.ActivationFunctionType
ALU = mybir.AluOpType
AX = mybir.AxisListType

T = 1024
NCORES = 8
DBG = {}



def _tile_w(w, transpose):
    w = np.asarray(w, np.float32)
    if transpose:
        w = np.ascontiguousarray(w.T)
    in_f, out_f = w.shape
    assert in_f % 128 == 0
    kc = in_f // 128
    m = (out_f + 127) // 128
    wp = np.zeros((in_f, m * 128), np.float32)
    wp[:, :out_f] = w
    out = wp.reshape(kc, 128, m, 128).transpose(2, 1, 0, 3)
    return np.ascontiguousarray(out)


def stage_inputs(inputs):
    import ml_dtypes
    g = {}
    g["ident_f"] = np.eye(128, dtype=np.float32)
    g["ident_b"] = np.eye(128, dtype=ml_dtypes.bfloat16)
    g["triu_f"] = np.triu(np.ones((128, 128), np.float32))
    g["onesb"] = np.ones((128, 128), dtype=ml_dtypes.bfloat16)
    i8e = np.zeros((9, 9, 128), dtype=ml_dtypes.bfloat16)
    for e in range(9):
        i8e[e, e, :] = 1.0
    g["i8e_b"] = i8e

    def dpair_of(dp):
        dp = np.asarray(dp, np.float32)
        out = np.zeros((128, 32), np.float32)
        out[0:64, :] = dp[0::2][None, :]
        out[64:128, :] = dp[1::2][None, :]
        return out

    g["l0_win"] = _tile_w(inputs["L0_attn_W_in"], True)
    g["l0_wout"] = _tile_w(inputs["L0_attn_W_out"], True)
    g["l0_convw"] = np.ascontiguousarray(
        np.asarray(inputs["L0_attn_conv_w"], np.float32).reshape(33, 128, 4)
        .transpose(1, 0, 2))
    g["l0_convb"] = np.ascontiguousarray(
        np.asarray(inputs["L0_attn_conv_b"], np.float32).reshape(33, 128)
        .transpose(1, 0))
    g["l0_dtb"] = np.asarray(inputs["L0_attn_dt_bias"], np.float32).reshape(64, 1)
    g["l0_alog"] = np.asarray(inputs["L0_attn_A_log"], np.float32).reshape(64, 1)
    g["l0_dpair"] = dpair_of(inputs["L0_attn_D_param"])
    g["l0_normw"] = np.ascontiguousarray(
        np.asarray(inputs["L0_attn_norm_w"], np.float32).reshape(32, 128)
        .transpose(1, 0))
    g["l0_gwT"] = np.ascontiguousarray(
        np.asarray(inputs["L0_ffn_gate_w"], np.float32).T.reshape(16, 128, 8)
        .transpose(1, 0, 2))
    g["l0_gbrep"] = np.ascontiguousarray(np.broadcast_to(
        np.asarray(inputs["L0_ffn_gate_b"], np.float32), (128, 8)))
    g["l0_w1"] = np.stack([_tile_w(inputs["L0_ffn_W1"][e], True) for e in range(8)])
    g["l0_w3"] = np.stack([_tile_w(inputs["L0_ffn_W3"][e], True) for e in range(8)])
    g["l0_w2"] = np.stack([_tile_w(inputs["L0_ffn_W2"][e], True) for e in range(8)])
    g["l0_ws1"] = _tile_w(inputs["L0_ffn_Ws1"], True)
    g["l0_ws3"] = _tile_w(inputs["L0_ffn_Ws3"], True)
    g["l0_ws2"] = _tile_w(inputs["L0_ffn_Ws2"], True)
    g["l1_win"] = _tile_w(inputs["L1_attn_in_proj_w"], False)
    g["l1_wout"] = _tile_w(inputs["L1_attn_out_proj_w"], False)
    cw1 = np.ascontiguousarray(np.asarray(inputs["L1_attn_conv_w"], np.float32)[:, 0, :].T)
    g["l1_convw"] = np.ascontiguousarray(cw1.reshape(48, 128, 4).transpose(1, 0, 2))
    g["l1_convb"] = np.ascontiguousarray(
        np.asarray(inputs["L1_attn_conv_b"], np.float32).reshape(48, 128)
        .transpose(1, 0))
    g["l1_dtb"] = np.asarray(inputs["L1_attn_dt_bias"], np.float32).reshape(64, 1)
    g["l1_alog"] = np.asarray(inputs["L1_attn_A_log"], np.float32).reshape(64, 1)
    g["l1_dpair"] = dpair_of(inputs["L1_attn_D"])
    g["l1_normw"] = np.ascontiguousarray(
        np.asarray(inputs["L1_attn_norm_w"], np.float32).reshape(32, 128)
        .transpose(1, 0))
    gw9 = np.concatenate([np.asarray(inputs["L1_ffn_gate_w"], np.float32),
                          np.asarray(inputs["L1_ffn_shared_expert_gate_w"], np.float32)],
                         axis=0)
    g["l1_gwT"] = np.ascontiguousarray(gw9.T.reshape(16, 128, 9).transpose(1, 0, 2))
    g["l1_gup"] = np.stack([_tile_w(inputs["L1_ffn_gate_up_proj"][e], True)
                            for e in range(8)])
    g["l1_down"] = np.stack([_tile_w(inputs["L1_ffn_down_proj"][e], True)
                             for e in range(8)])
    g["l1_sg"] = _tile_w(inputs["L1_ffn_shared_gate_w"], True)
    g["l1_su"] = _tile_w(inputs["L1_ffn_shared_up_w"], True)
    g["l1_sd"] = _tile_w(inputs["L1_ffn_shared_down_w"], True)

    u = np.asarray(inputs["L0_attn_u"], np.float32)
    percore = []
    for c in range(NCORES):
        bb, h = c // 2, c % 2
        sl = u[bb, h * T:(h + 1) * T, :]
        x0 = np.ascontiguousarray(sl.T).reshape(16, 128, T)
        percore.append({"x0": x0,
                        "flag_hi": np.full((128, 1), float(h), np.float32),
                        "flag_lo": np.full((128, 1), float(1 - h), np.float32)})
    return g, percore



class Tok:
    __slots__ = ("sem", "v")

    def __init__(self, sem, v):
        self.sem = sem
        self.v = v


class Builder:
    def __init__(self, nc, ctx):
        self.nc = nc
        self.ctx = ctx
        self.pe = nc.tensor
        self.ve = nc.vector
        self.se = nc.scalar
        self.po = nc.gpsimd
        self.sy = nc.sync
        self.sems = {}
        for name in ["ld", "gd", "pe", "ve", "se", "po", "cc", "sy"]:
            self.sems[name] = ctx.enter_context(nc.semaphore(name))
        self.cnt = {name: 0 for name in self.sems}
        self.pA = ctx.enter_context(nc.psum_tensor("pA", [128, 1024], F32))
        self.pB = ctx.enter_context(nc.psum_tensor("pB", [128, 1024], F32))
        self.pC = ctx.enter_context(nc.psum_tensor("pC", [128, 1024], F32))
        self.pD = ctx.enter_context(nc.psum_tensor("pD", [128, 512], F32))
        self.pE = ctx.enter_context(nc.psum_tensor("pE", [128, 512], F32))
        self.CAP = 206 * 1024
        self.arena = ctx.enter_context(
            nc.sbuf_tensor("arena", [128, self.CAP // 4], F32))
        self.off = 0
        self.marks = []
        self.peak = 0
        self.last_pe = None
        self.psum_tok = {"A": None, "B": None}

    def inc(self, name, inst, n=1):
        inst.then_inc(self.sems[name], n)
        self.cnt[name] += n
        return Tok(name, self.cnt[name])

    def dslot(self, name):
        if name not in self.sems:
            self.sems[name] = self.ctx.enter_context(self.nc.semaphore(name))
            self.cnt[name] = 0
        return name

    def dma(self, q, slot, out, in_):
        self.dslot(slot)
        return self.inc(slot, q.dma_start(out=out, in_=in_), 16)

    def tok(self, name):
        return Tok(name, self.cnt[name])

    def pe_tok(self, inst=None):
        if inst is None:
            inst = self.last_pe
        if inst is not None:
            return self.inc("pe", inst)
        return self.inc("pe", self.pe.drain())

    def mm(self, *a, **k):
        self.last_pe = self.pe.matmul(*a, **k)
        return self.last_pe

    def tr(self, *a, **k):
        self.last_pe = self.pe.transpose(*a, **k)
        return self.last_pe

    def pe_warm(self, cons, n):
        for _ in range(n):
            self.last_pe = self.pe.matmul(self.pA[:, 0:128], cons["onesb"][:, :],
                                          cons["ident_b"][:, :],
                                          start=True, stop=True)

    def wait(self, eng, tok):
        if tok is not None:
            eng.wait_ge(self.sems[tok.sem], tok.v)

    def barrier(self):
        b = self
        for name, c in list(b.cnt.items()):
            if name in ("pe", "ve", "se", "po", "cc", "sy", "gd") or c == 0:
                continue
            if name.startswith("st"):
                continue
            b.sy.wait_ge(b.sems[name], c)
        tsy = b.inc("sy", b.sy.nop(nofuse=True))
        for name, c in list(b.cnt.items()):
            if c and (name == "gd" or name.startswith("st")):
                b.po.wait_ge(b.sems[name], c)
        b.ve.drain()
        tv = b.inc("ve", b.ve.nop(nofuse=True))
        b.se.drain()
        ts = b.inc("se", b.se.nop(nofuse=True))
        b.po.drain()
        tp = b.inc("po", b.po.nop(nofuse=True))
        tt = b.inc("pe", b.pe.drain())
        for eng in [b.pe, b.ve, b.se, b.po, b.sy]:
            for t in [tv, ts, tp, tt, tsy]:
                b.wait(eng, t)

    def alloc(self, nbytes, align=64):
        nbytes = -(-nbytes // align) * align
        o = -(-self.off // align) * align
        self.off = o + nbytes
        self.peak = max(self.peak, self.off)
        assert self.off <= self.CAP, f"SBUF arena overflow: {self.off}"
        return o

    def mark(self):
        self.marks.append(self.off)

    def release(self):
        self.off = self.marks.pop()

    def view(self, off, shape, dtype=F32):
        p = shape[0]
        free = 1
        for s in shape[1:]:
            free *= s
        dsz = 2 if dtype == BF16 else 4
        assert off % 4 == 0
        ap = self.arena[0:p, off // 4:(off + free * dsz + 3) // 4]
        if dtype != F32:
            ap = ap.bitcast(dtype)
            ap = ap[:, 0:free]
        if len(shape) == 2:
            return ap
        if len(shape) == 3:
            return ap.rearrange("p (a c) -> p a c", a=shape[1])
        if len(shape) == 4:
            return ap.rearrange("p (a c d) -> p a c d", a=shape[1], c=shape[2])
        raise AssertionError

    def tile(self, shape, dtype=F32):
        dsz = 2 if dtype == BF16 else 4
        free = 1
        for s in shape[1:]:
            free *= s
        return self.view(self.alloc(free * dsz), shape, dtype)



def emit_ln(b, cons, x_dram, xnorm, eps_name):
    b.mark()
    x_ld = [b.tile((128, T)) for _ in range(2)]
    sq = b.tile((128, T), BF16)
    mrep = b.tile((128, T))
    rstd = b.tile((128, T))
    c0 = b.tile((128, T))
    casttok = {}
    pet = {}
    for m in range(16):
        buf = x_ld[m % 2]
        if m >= 2:
            b.wait(b.sy, casttok[m - 2])
        tok = b.dma(b.sy, f"lnx{m % 2}", buf[:, :], x_dram[m])
        b.wait(b.ve, tok)
        casttok[m] = b.inc("ve", b.ve.tensor_copy(xnorm[:, m, :], buf[:, :]))
        b.wait(b.se, casttok[m])
        if m >= 1:
            b.wait(b.se, pet[m - 1])
        tsq = b.inc("se", b.se.activation(sq[:, :], xnorm[:, m, :], AF.Square))
        b.wait(b.pe, casttok[m])
        b.mm(b.pA[:, 0:512], cons["onesb"][:, :], xnorm[:, m, 0:512],
                    start=(m == 0), stop=(m == 15))
        b.mm(b.pA[:, 512:1024], cons["onesb"][:, :], xnorm[:, m, 512:1024],
                    start=(m == 0), stop=(m == 15))
        b.wait(b.pe, tsq)
        b.mm(b.pB[:, 0:512], cons["onesb"][:, :], sq[:, 0:512],
                    start=(m == 0), stop=(m == 15))
        lastmm = b.mm(b.pB[:, 512:1024], cons["onesb"][:, :], sq[:, 512:1024],
                             start=(m == 0), stop=(m == 15))
        pet[m] = b.pe_tok(lastmm)
    tstats = pet[15]
    inv = 1.0 / 2048.0
    b.wait(b.ve, tstats)
    b.ve.tensor_scalar(mrep[:, :], b.pA[:, :], inv, None, ALU.mult)
    b.ve.drain()
    b.ve.tensor_tensor(c0[:, :], mrep[:, :], mrep[:, :], ALU.mult)
    b.ve.drain()
    tv = b.inc("ve", b.ve.scalar_tensor_tensor(
        rstd[:, :], b.pB[:, :], inv, c0[:, :], ALU.mult, ALU.subtract))
    b.wait(b.se, tv)
    b.se.activation(rstd[:, :], rstd[:, :], AF.Ln, bias=cons[eps_name][:, 0:1])
    b.se.drain()
    ts = b.inc("se", b.se.activation(rstd[:, :], rstd[:, :], AF.Exp, scale=-0.5))
    b.wait(b.ve, ts)
    b.ve.drain()
    b.inc("ve", b.ve.scalar_tensor_tensor(
        c0[:, :], mrep[:, :], -1.0, rstd[:, :], ALU.mult, ALU.mult))
    b.ve.drain()
    vt = {-2: casttok[14], -1: casttok[15]}
    for m in range(16):
        buf = x_ld[m % 2]
        b.wait(b.sy, vt[m - 2])
        tok = b.dma(b.sy, f"lnx{m % 2}", buf[:, :], x_dram[m])
        b.wait(b.ve, tok)
        b.ve.tensor_tensor(buf[:, :], buf[:, :], rstd[:, :], ALU.mult)
        b.ve.drain()
        vt[m] = b.inc("ve", b.ve.tensor_tensor(
            xnorm[:, m, :], buf[:, :], c0[:, :], ALU.add))
    b.barrier()
    b.release()



def mk_gemm_bufs(b, KC, pair, nbuf=2):
    bufs = dict(KC=KC, pair=pair, nbuf=nbuf,
                wf=[b.tile((128, KC, 128)) for _ in range(nbuf)],
                wb=[b.tile((128, KC, 128), BF16) for _ in range(nbuf)],
                tok={}, n=0)
    if pair:
        bufs["wf2"] = [b.tile((128, KC, 128)) for _ in range(nbuf)]
        bufs["wb2"] = [b.tile((128, KC, 128), BF16) for _ in range(nbuf)]
    return bufs


def emit_gemm(b, w_dram, M, KC, rhs_fn, epi_fn, w2_dram=None, n_tok=T,
              nbuf=3, chain=None, rhs_tok=None):
    own = chain is None
    if KC > 16 or w2_dram is not None:
        nbuf = 2
    if own:
        b.mark()
        chain = mk_gemm_bufs(b, KC, w2_dram is not None, nbuf)
    else:
        assert chain["KC"] == KC and chain["pair"] == (w2_dram is not None)
        nbuf = chain["nbuf"]
    wf, wb = chain["wf"], chain["wb"]
    wf2 = chain.get("wf2")
    wb2 = chain.get("wb2")
    NT = -(-n_tok // 512)
    base = chain["n"]
    ldt, pot, epit, pet = chain.setdefault("ldt", {}), chain.setdefault("pot", {}), {}, {}

    def issue_load(i):
        gi = base + i
        if gi >= nbuf:
            b.wait(b.sy, pot[gi - nbuf])
        t = b.dma(b.sy, f"gw{chain['KC']}{'p' if chain['pair'] else ''}{gi % nbuf}",
                  wf[gi % nbuf][:, :, :], w_dram[i])
        if w2_dram is not None:
            t = b.dma(b.sy, f"gw{chain['KC']}{'p' if chain['pair'] else ''}{gi % nbuf}",
                      wf2[gi % nbuf][:, :, :], w2_dram[i])
        ldt[gi] = t

    def issue_cast(i):
        gi = base + i
        b.wait(b.po, ldt[gi])
        if gi >= nbuf:
            b.wait(b.po, chain.setdefault("pet", {})[gi - nbuf])
        c = b.po.tensor_copy(wb[gi % nbuf][:, :, :], wf[gi % nbuf][:, :, :])
        if w2_dram is not None:
            c = b.po.tensor_copy(wb2[gi % nbuf][:, :, :], wf2[gi % nbuf][:, :, :])
        pot[gi] = b.inc("po", c)

    for i in range(min(nbuf, M)):
        issue_load(i)
        issue_cast(i)
    for i in range(M):
        gi = base + i
        if i + nbuf < M:
            issue_load(i + nbuf)
        b.wait(b.pe, pot[gi])
        if i == 0 and rhs_tok is not None:
            b.wait(b.pe, rhs_tok)
        if w2_dram is not None:
            ps, ps2 = b.pA, b.pB
            if i == 0:
                for tkn in (b.psum_tok["A"], b.psum_tok["B"]):
                    b.wait(b.pe, tkn)
            elif epit.get(i - 1) is not None:
                b.wait(b.pe, epit[i - 1])
        else:
            ps, ps2 = (b.pA, None) if i % 2 == 0 else (b.pB, None)
            if i == 0:
                b.wait(b.pe, b.psum_tok["A"])
            elif i == 1:
                b.wait(b.pe, b.psum_tok["B"])
            elif epit.get(i - 2) is not None:
                b.wait(b.pe, epit[i - 2])
        last = None
        for nt in range(NT):
            t0, t1 = nt * 512, min((nt + 1) * 512, n_tok)
            for kc in range(KC):
                last = b.mm(ps[:, t0:t1], wb[gi % nbuf][:, kc, :],
                            rhs_fn(kc, t0, t1),
                            start=(kc == 0), stop=(kc == KC - 1))
            if w2_dram is not None:
                for kc in range(KC):
                    last = b.mm(ps2[:, t0:t1], wb2[gi % nbuf][:, kc, :],
                                rhs_fn(kc, t0, t1),
                                start=(kc == 0), stop=(kc == KC - 1))
        pet[i] = chain.setdefault("pet", {})
        pet[i] = b.pe_tok(last)
        chain["pet"][gi] = pet[i]
        if i + nbuf < M:
            issue_cast(i + nbuf)
        tk = None
        for nt in range(NT):
            t0, t1 = nt * 512, min((nt + 1) * 512, n_tok)
            r = epi_fn(i, t0, t1, ps[:, t0:t1],
                       None if ps2 is None else ps2[:, t0:t1], pet[i])
            if r is not None:
                tk = r
        epit[i] = tk
    chain["n"] = base + M
    lastA = lastB = None
    for i in range(M):
        if epit.get(i) is None:
            continue
        if w2_dram is not None or i % 2 == 0:
            lastA = epit[i]
        if w2_dram is not None or i % 2 == 1:
            lastB = epit[i]
    b.psum_tok["A"] = lastA if lastA is not None else b.psum_tok["A"]
    b.psum_tok["B"] = lastB if lastB is not None else b.psum_tok["B"]
    if own:
        b.barrier()
        b.release()
    return epit.get(M - 1)



def emit_gating(b, cons, L, xnorm, dwT, renorm):
    ncol = L["ngcol"]
    b.mark()
    gw = b.tile((128, 16, ncol))
    gwb = b.tile((128, 16, ncol), BF16)
    logits = b.tile((128, 8, ncol))
    p = b.tile((128, 8, 8))
    msk = b.tile((128, 8, 8))
    tmp = b.tile((128, 8, 8))
    red = b.tile((128, 8))
    dwtm = b.tile((128, 8, 8), BF16)
    sgtm = b.tile((128, 8), BF16) if L.get("nsig") else None
    tok = b.dma(b.sy, "misc0", gw[:, :, :], cons[L["gwT"]])
    b.wait(b.po, tok)
    tc_ = b.inc("po", b.po.tensor_copy(gwb[:, :, :], gw[:, :, :]))
    b.wait(b.pe, tc_)
    lastmm = None
    for tc in range(8):
        for kc in range(16):
            lastmm = b.mm(b.pD[:, tc * ncol:(tc + 1) * ncol],
                                 xnorm[:, kc, tc * 128:(tc + 1) * 128],
                                 gwb[:, kc, :], start=(kc == 0), stop=(kc == 15))
    tmm = b.pe_tok()
    b.wait(b.ve, tmm)
    b.ve.tensor_copy(logits[:, :, :],
                     b.pD[:, 0:8 * ncol].rearrange("p (a c) -> p a c", a=8))
    b.ve.drain()
    lg8 = logits[:, :, 0:8]
    b.ve.tensor_reduce(red[:, :], lg8, axis=AX.X, op=ALU.max)
    b.ve.drain()
    for tc in range(8):
        b.ve.tensor_scalar(p[:, tc, :], lg8[:, tc, :], red[:, tc:tc + 1], None,
                           ALU.subtract)
    b.ve.drain()
    tv = b.inc("ve", b.ve.nop(nofuse=True))
    b.wait(b.se, tv)
    ts = b.inc("se", b.se.activation(p[:, :, :], p[:, :, :], AF.Exp))
    b.wait(b.ve, ts)
    b.ve.tensor_reduce(red[:, :], p[:, :, :], axis=AX.X, op=ALU.add)
    b.ve.drain()
    b.ve.reciprocal(red[:, :], red[:, :])
    b.ve.drain()
    for tc in range(8):
        b.ve.tensor_scalar(p[:, tc, :], p[:, tc, :], red[:, tc:tc + 1], None,
                           ALU.mult)
    b.ve.drain()
    if L.get("gbrep"):
        gb = b.tile((128, 8))
        tok = b.dma(b.sy, "misc1", gb[:, :], cons[L["gbrep"]])
        b.wait(b.ve, tok)
        bview = gb[:, :].unsqueeze(1).broadcast_to([128, 8, 8])
        b.ve.tensor_tensor(tmp[:, :, :], p[:, :, :], bview, ALU.add)
    else:
        b.ve.tensor_copy(tmp[:, :, :], p[:, :, :])
    b.ve.drain()
    b.ve.tensor_reduce(red[:, :], tmp[:, :, :], axis=AX.X, op=ALU.max)
    b.ve.drain()
    for tc in range(8):
        b.ve.tensor_scalar(msk[:, tc, :], tmp[:, tc, :], red[:, tc:tc + 1], None,
                           ALU.is_equal)
    b.ve.drain()
    b.ve.scalar_tensor_tensor(tmp[:, :, :], msk[:, :, :], -1e30, tmp[:, :, :],
                              ALU.mult, ALU.add)
    b.ve.drain()
    b.ve.tensor_reduce(red[:, :], tmp[:, :, :], axis=AX.X, op=ALU.max)
    b.ve.drain()
    for tc in range(8):
        b.ve.tensor_scalar(tmp[:, tc, :], tmp[:, tc, :], red[:, tc:tc + 1], None,
                           ALU.is_equal)
    b.ve.drain()
    b.ve.tensor_tensor(msk[:, :, :], msk[:, :, :], tmp[:, :, :], ALU.add)
    b.ve.drain()
    b.ve.tensor_tensor(p[:, :, :], p[:, :, :], msk[:, :, :], ALU.mult)
    b.ve.drain()
    if renorm:
        b.ve.tensor_reduce(red[:, :], p[:, :, :], axis=AX.X, op=ALU.add)
        b.ve.drain()
        b.ve.reciprocal(red[:, :], red[:, :])
        b.ve.drain()
        for tc in range(8):
            b.ve.tensor_scalar(p[:, tc, :], p[:, tc, :], red[:, tc:tc + 1], None,
                               ALU.mult)
        b.ve.drain()
    tv = b.inc("ve", b.ve.tensor_copy(dwtm[:, :, :], p[:, :, :]))
    tv2 = None
    if sgtm is not None:
        b.wait(b.se, tmm)
        b.se.drain()
        tv2 = b.inc("se", b.se.activation(sgtm[:, :], logits[:, :, 8], AF.Sigmoid))
    b.wait(b.pe, tv)
    for tc in range(8):
        nrow = 8 + (1 if sgtm is not None else 0)
        if sgtm is not None:
            src = b.tile((128, 9), BF16)
            b.wait(b.ve, tv2)
            b.ve.tensor_copy(src[:, 0:8], dwtm[:, tc, :])
            b.ve.drain()
            tvv = b.inc("ve", b.ve.tensor_copy(src[:, 8:9], sgtm[:, tc:tc + 1]))
            b.wait(b.pe, tvv)
            b.tr(b.pE.bitcast(BF16)[0:nrow, tc * 128:(tc + 1) * 128],
                           src[:, :], cons["ident_b"][:, :])
        else:
            b.tr(b.pE.bitcast(BF16)[0:8, tc * 128:(tc + 1) * 128],
                           dwtm[:, tc, :], cons["ident_b"][:, :])
    tpe = b.pe_tok()
    b.wait(b.ve, tpe)
    b.inc("ve", b.ve.tensor_copy(
        dwT[:, :], b.pE.bitcast(BF16)[0:dwT.shape[0], 0:1024]))
    b.barrier()
    b.release()


def emit_bcast_rows(b, cons, rowsrc, nrows, out_bf, srow=0):
    for e in range(nrows):
        ei = min(srow + e, 8)
        nr = rowsrc.shape[0]
        last = None
        for half in range(2):
            last = b.mm(b.pC[:, half * 512:(half + 1) * 512],
                               cons["i8e_b"][ei][0:nr, :],
                               rowsrc[0:nr, half * 512:(half + 1) * 512],
                               start=True, stop=True)
        t = b.pe_tok()
        b.wait(b.ve, t)
        b.inc("ve", b.ve.tensor_copy(out_bf[:, e, :], b.pC[:, 0:1024]))
        b.ve.drain()
        b.wait(b.pe, b.tok("ve"))



def emit_ffn(b, cons, L, x_in, x_out, dram):
    b.mark()
    xnorm = b.tile((128, 16, T), BF16)
    y_acc = b.tile((128, 16, T))
    emit_ln(b, cons, x_in, xnorm, L["eps_ln"])

    nsig = L.get("nsig", 0)
    sigb = b.tile((128, 1, 1024), BF16) if nsig else None

    for m in range(16):
        b.po.memset(y_acc[:, m, :], 0.0)
    b.po.drain()
    b.inc("po", b.po.nop(nofuse=True))

    def rhs(kc, t0, t1):
        return xnorm[:, kc, t0:t1]

    b.mark()
    dwT = b.tile((8 + nsig, 1024), BF16)
    emit_gating(b, cons, L, xnorm, dwT, renorm=L["renorm"])
    dwb = b.tile((128, 8, 1024), BF16)
    emit_bcast_rows(b, cons, dwT, 8, dwb)
    if nsig:
        emit_bcast_rows(b, cons, dwT, 1, sigb, srow=8)
    b.barrier()
    upset = mk_gemm_bufs(b, 16, L["wgt"] is not None)
    dnset = mk_gemm_bufs(b, L["EI"] // 128, False)
    nhc = L["EI"] // 128
    hbuf = b.tile((128, nhc, T), BF16)
    tmpb = [b.tile((128, 512), BF16) for _ in range(2)]
    gu = b.tile((128, L["MI"], T), BF16) if L["wgt"] is None else None
    cell = {"prev_ve": None, "slot": 0}

    for e in range(L["E"]):
        if L["wgt"] is not None:
            gt_d = cons[L["wgt"]][e]
            up_d = cons[L["wup"]][e]

            def epi_h(m, t0, t1, ps, ps2, tok, e=e):
                sl = cell["slot"] % 2
                cell["slot"] += 1
                b.wait(b.se, tok)
                if m == 0 and t0 == 0 and cell.get("dn_pe") is not None:
                    b.wait(b.se, cell["dn_pe"])
                    b.wait(b.ve, cell["dn_pe"])
                if cell["prev_ve"] is not None:
                    b.wait(b.se, cell["prev_ve"])
                b.se.drain()
                tsl = b.inc("se", b.se.activation(tmpb[sl][:, 0:t1 - t0], ps,
                                                  AF.Silu))
                b.wait(b.ve, tsl)
                b.ve.tensor_tensor(hbuf[:, m, t0:t1], tmpb[sl][:, 0:t1 - t0],
                                   ps2, ALU.mult)
                b.ve.drain()
                b.ve.tensor_tensor(hbuf[:, m, t0:t1], hbuf[:, m, t0:t1],
                                   dwb[:, e, t0:t1], ALU.mult)
                b.ve.drain()
                cell["prev_ve"] = b.inc("ve", b.ve.nop(nofuse=True))
                return cell["prev_ve"]

            emit_gemm(b, gt_d, L["MI"], 16, rhs, epi_h, w2_dram=up_d,
                      chain=upset)
        else:

            def epi_gu(m, t0, t1, ps, ps2, tok):
                b.wait(b.se, tok)
                b.se.drain()
                b.inc("se", b.se.copy(gu[:, m, t0:t1], ps))
                return b.tok("se")

            emit_gemm(b, cons[L["wgup"]][e], L["MI"], 16, rhs, epi_gu,
                      chain=upset)
            half = L["MI"] // 2
            if cell.get("dn_pe") is not None:
                b.wait(b.ve, cell["dn_pe"])
                b.wait(b.se, cell["dn_pe"])
            for mi in range(half):
                for nt in range(2):
                    t0, t1 = nt * 512, (nt + 1) * 512
                    sl = cell["slot"] % 2
                    cell["slot"] += 1
                    if cell["prev_ve"] is not None:
                        b.wait(b.se, cell["prev_ve"])
                    b.se.drain()
                    tsl = b.inc("se", b.se.activation(
                        tmpb[sl][:, 0:512], gu[:, mi, t0:t1], AF.Silu))
                    b.wait(b.ve, tsl)
                    b.ve.tensor_tensor(hbuf[:, mi, t0:t1], tmpb[sl][:, 0:512],
                                       gu[:, half + mi, t0:t1], ALU.mult)
                    b.ve.drain()
                    b.ve.tensor_tensor(hbuf[:, mi, t0:t1], hbuf[:, mi, t0:t1],
                                       dwb[:, e, t0:t1], ALU.mult)
                    b.ve.drain()
                    cell["prev_ve"] = b.inc("ve", b.ve.nop(nofuse=True))

        def rhs_h(kc, t0, t1):
            return hbuf[:, kc, t0:t1]

        def epi_y(m, t0, t1, ps, ps2, tok):
            b.wait(b.ve, tok)
            b.ve.tensor_tensor(y_acc[:, m, t0:t1], y_acc[:, m, t0:t1], ps,
                               ALU.add)
            b.ve.drain()
            return b.inc("ve", b.ve.nop(nofuse=True))

        emit_gemm(b, cons[L["wdn"]][e], 16, nhc, rhs_h, epi_y,
                  chain=dnset, rhs_tok=cell["prev_ve"])
        cell["dn_pe"] = b.tok("pe")
    b.barrier()
    b.release()

    b.mark()
    msh = L["MSH"]
    nsplit = L["sh_split"]
    hsh = b.tile((128, msh, T // nsplit), BF16)
    tmpb2 = [b.tile((128, 512), BF16) for _ in range(2)]
    cell2 = {"prev_ve": None, "slot": 0}
    for sp in range(nsplit):
        tt0 = sp * (T // nsplit)

        def rhs_s(kc, t0, t1, tt0=tt0):
            return xnorm[:, kc, tt0 + t0:tt0 + t1]

        def epi_hs(m, t0, t1, ps, ps2, tok):
            sl = cell2["slot"] % 2
            cell2["slot"] += 1
            b.wait(b.se, tok)
            if cell2["prev_ve"] is not None:
                b.wait(b.se, cell2["prev_ve"])
            b.se.drain()
            tsl = b.inc("se", b.se.activation(tmpb2[sl][:, 0:t1 - t0], ps,
                                              AF.Silu))
            b.wait(b.ve, tsl)
            b.ve.tensor_tensor(hsh[:, m, t0:t1], tmpb2[sl][:, 0:t1 - t0], ps2,
                               ALU.mult)
            b.ve.drain()
            cell2["prev_ve"] = b.inc("ve", b.ve.nop(nofuse=True))
            return cell2["prev_ve"]

        emit_gemm(b, cons[L["wsg"]], msh, 16, rhs_s, epi_hs,
                  w2_dram=cons[L["wsu"]], n_tok=T // nsplit)

        def rhs_hs(kc, t0, t1):
            return hsh[:, kc, t0:t1]

        def epi_ys(m, t0, t1, ps, ps2, tok, tt0=tt0):
            b.wait(b.ve, tok)
            if nsig:
                b.ve.tensor_tensor(ps, ps, sigb[:, 0, tt0 + t0:tt0 + t1], ALU.mult)
                b.ve.drain()
            b.ve.tensor_tensor(y_acc[:, m, tt0 + t0:tt0 + t1],
                               y_acc[:, m, tt0 + t0:tt0 + t1], ps, ALU.add)
            b.ve.drain()
            return b.inc("ve", b.ve.nop(nofuse=True))

        emit_gemm(b, cons[L["wsd"]], 16, msh, rhs_hs, epi_ys, n_tok=T // nsplit)
    b.release()

    b.mark()
    x_ld = [b.tile((128, T)) for _ in range(2)]
    res = [b.tile((128, T)) for _ in range(2)]
    vt = {}
    stok = {}
    for m in range(16):
        if m >= 2:
            b.wait(b.sy, vt[m - 2])
        tok = b.dma(b.sy, f"lnx{m % 2}", x_ld[m % 2][:, :], x_in[m])
        b.wait(b.ve, tok)
        if m >= 2:
            b.wait(b.ve, stok[m - 2])
        vt[m] = b.inc("ve", b.ve.tensor_tensor(res[m % 2][:, :], y_acc[:, m, :],
                                               x_ld[m % 2][:, :], ALU.add))
        b.wait(b.po, vt[m])
        stok[m] = b.dma(b.po, f"st{m % 2}", x_out[m], res[m % 2][:, :])
    b.barrier()
    b.release()
    b.release()



def emit_attn(b, cons, L, x_in, x_out, dram):
    import os
    KDBG = int(os.environ.get("KDBG", "0"))
    nc_x = L["nch_xbc"]
    NH, P, N, G = 64, 64, L["N"], L["G"]
    NCH = 8
    b.mark()
    region = b.alloc(64 * 1024)
    DBG["region"] = region
    xnorm = b.view(region, (128, 16, T), BF16)
    y_buf = b.view(region, (128, 32, T), BF16)
    dt_raw = b.view(region + 32 * 1024, (64, T))
    dt_hm = b.view(region + 36 * 1024, (64, T))
    lndt_hm = b.view(region + 40 * 1024, (64, T))
    adt_hm = b.view(region + 44 * 1024, (64, T))
    nch_bc = nc_x - 32
    bc = b.tile((128, nch_bc, T), BF16)
    c_all = b.tile((64, T), BF16) if L["l0"] else None
    tail_st = b.tile((128, 256))
    gtail_lo = b.tile((128, 256))
    gtail_hi = b.tile((128, 256))
    diag = b.tile((128, 32, 128), BF16)
    colmisc = b.tile((64, 4))

    emit_ln(b, cons, x_in, xnorm, "eps5")

    tok = b.dma(b.sy, "misc0", colmisc[:, 0:1], cons[L["dtb"]][:, :])
    tok2 = b.dma(b.sy, "misc1", colmisc[:, 1:2], cons[L["alog"]][:, :])
    b.wait(b.se, tok2)
    b.se.drain()
    ts = b.inc("se", b.se.activation(colmisc[:, 1:2], colmisc[:, 1:2], AF.Exp))
    b.wait(b.ve, ts)
    b.ve.tensor_scalar(colmisc[:, 1:2], colmisc[:, 1:2], -1.0, None, ALU.mult)
    b.ve.memset(colmisc[:, 2:3], 1.0)
    b.ve.drain()
    b.inc("ve", b.ve.nop(nofuse=True))
    b.po.memset(tail_st[:, :], 0.0)
    b.po.drain()
    b.inc("po", b.po.nop(nofuse=True))
    dp = b.tile((128, 32))
    tok = b.dma(b.sy, "misc0", dp[:, :], cons[L["dpair"]][:, :])
    b.wait(b.ve, tok)
    for j in range(32):
        b.ve.tensor_scalar(diag[:, j, :], cons["ident_b"][:, :], dp[:, j:j + 1],
                           None, ALU.mult)
    b.ve.drain()
    b.inc("ve", b.ve.nop(nofuse=True))
    b.barrier()

    b.mark()
    z_stage = [b.tile((128, T), BF16) for _ in range(2)]
    x_stage = [b.tile((128, T), BF16) for _ in range(2)]
    st_z = {}
    st_x = {}

    def epi_in(m, t0, t1, ps, ps2, tok):
        if m < 32:
            sl = m % 2
            b.wait(b.se, tok)
            if m >= 2:
                b.wait(b.se, st_z[m - 2])
            b.se.drain()
            tsl = b.inc("se", b.se.activation(z_stage[sl][:, t0:t1], ps, AF.Silu))
            if t1 == T:
                b.wait(b.po, tsl)
                st_z[m] = b.dma(b.po, f"st{sl}", dram["z_spill"][m],
                                z_stage[sl][:, :])
            return tsl
        if m < 32 + nc_x:
            mx = m - 32
            sl = m % 2
            b.wait(b.se, tok)
            if m >= 34:
                b.wait(b.se, st_x[m - 2])
            b.se.drain()
            tsl = b.inc("se", b.se.copy(x_stage[sl][:, t0:t1], ps))
            b.wait(b.ve, tsl)
            if t0 == 0:
                b.ve.tensor_copy(tail_st[:, mx * 2:mx * 2 + 2], ps[:, 0:2])
            if t1 == T:
                b.ve.tensor_copy(tail_st[:, nc_x * 2 + mx * 3:nc_x * 2 + mx * 3 + 3],
                                 ps[:, 509:512])
            b.ve.drain()
            tv = b.inc("ve", b.ve.nop(nofuse=True))
            if t1 == T:
                b.wait(b.po, tv)
                st_x[m] = b.dma(b.po, f"st{2 + sl}", dram["xbc_spill"][mx],
                                x_stage[sl][:, :])
            return tv
        b.wait(b.se, tok)
        b.se.drain()
        return b.inc("se", b.se.copy(dt_raw[0:64, t0:t1], ps[0:64, :]))

    def rhs_in(kc, t0, t1):
        return xnorm[:, kc, t0:t1]

    emit_gemm(b, cons[L["win"]], L["M_in"], 16, rhs_in, epi_in)

    cin_t, cout_t = dram["cin_tail"], dram["cout_tail"]
    b.wait(b.po, b.tok("ve"))
    tst = b.dma(b.po, "st0", cin_t.ap(), tail_st[:, :])
    b.wait(b.po, tst)
    tcc = b.inc("cc", b.po.collective_compute(
        "AllGather", ALU.bypass, replica_groups=[[0, 1], [2, 3], [4, 5], [6, 7]],
        ins=[cin_t.ap().opt()], outs=[cout_t.ap().opt()]))
    b.wait(b.po, tcc)
    tg1 = b.dma(b.po, "st0", gtail_lo[:, :], cout_t.ap()[0:128, :])
    tg2 = b.dma(b.po, "st1", gtail_hi[:, :], cout_t.ap()[128:256, :])
    b.barrier()
    b.release()

    b.mark()
    cw = b.tile((128, nc_x, 4))
    cb = b.tile((128, nc_x))
    cbuf = [b.tile((128, 1032), BF16) for _ in range(3)]
    ctmp = [b.tile((128, T)) for _ in range(2)]
    hs_stage = [b.tile((128, T), BF16) for _ in range(2)]
    tok = b.dma(b.sy, "misc0", cw[:, :, :], cons[L["convw"]][:, :, :])
    tok2 = b.dma(b.sy, "misc1", cb[:, :], cons[L["convb"]][:, :])
    b.wait(b.ve, tok)
    b.wait(b.ve, tok2)
    pl = L["pad_left"]
    rdt = {}
    vet = {}
    set_ = {}
    stt = {}
    for m in range(nc_x):
        sl = m % 3
        if m >= 3:
            b.wait(b.sy, vet[m - 3])
        rdt[m] = b.dma(b.sy, f"cv{sl}", cbuf[sl][:, 3:1027], dram["xbc_spill"][m])
        b.wait(b.ve, rdt[m])
        if m >= 1:
            b.wait(b.ve, set_[m - 1])
        b.ve.tensor_scalar(ctmp[0][:, 0:3],
                           gtail_lo[:, nc_x * 2 + m * 3:nc_x * 2 + m * 3 + 3],
                           cons["flag_hi"][:, 0:1], None, ALU.mult)
        b.ve.drain()
        b.ve.tensor_copy(cbuf[sl][:, 0:3], ctmp[0][:, 0:3])
        b.ve.drain()
        if L["pad_left"] == 1:
            b.ve.tensor_scalar(ctmp[0][:, 0:2],
                               gtail_hi[:, m * 2:m * 2 + 2],
                               cons["flag_lo"][:, 0:1], None, ALU.mult)
            b.ve.drain()
            b.ve.tensor_copy(cbuf[sl][:, 1027:1029], ctmp[0][:, 0:2])
            b.ve.drain()
        else:
            b.ve.memset(cbuf[sl][:, 1027:1029], 0.0)
            b.ve.drain()
        off0 = 3 - pl
        b.ve.tensor_scalar(ctmp[0][:, :], cbuf[sl][:, off0:off0 + 1024],
                           cw[:, m, 0:1], None, ALU.mult)
        b.ve.drain()
        for k in range(1, 4):
            b.ve.scalar_tensor_tensor(ctmp[0][:, :],
                                      cbuf[sl][:, off0 + k:off0 + k + 1024],
                                      cw[:, m, k:k + 1], ctmp[0][:, :],
                                      ALU.mult, ALU.add)
            b.ve.drain()
        vet[m] = b.inc("ve", b.ve.nop(nofuse=True))
        if m >= 1:
            b.wait(b.pe, vet[m - 1])
        b.pe_warm(cons, 28)
        b.wait(b.se, vet[m])
        if m < 32:
            hsl = m % 2
            if m >= 2:
                b.wait(b.se, stt[m - 2])
            b.se.drain()
            tse = b.inc("se", b.se.activation(hs_stage[hsl][:, :], ctmp[0][:, :],
                                              AF.Silu, bias=cb[:, m:m + 1]))
            b.wait(b.po, tse)
            stt[m] = b.dma(b.po, f"st{hsl}", dram["hs_spill"][m],
                           hs_stage[hsl][:, :])
            set_[m] = tse
        else:
            b.se.drain()
            set_[m] = b.inc("se", b.se.activation(bc[:, m - 32, :], ctmp[0][:, :],
                                                  AF.Silu, bias=cb[:, m:m + 1]))
    b.barrier()
    if L["l0"]:
        tcp = b.dma(b.po, "st0", c_all[:, :], bc[64:128, 0, :])
        b.barrier()
    b.release()

    b.ve.tensor_scalar(dt_raw[0:64, :], dt_raw[0:64, :], colmisc[0:64, 0:1],
                       None, ALU.add)
    b.ve.drain()
    tv = b.inc("ve", b.ve.nop(nofuse=True))
    b.wait(b.se, tv)
    b.se.activation(lndt_hm[0:64, :], dt_raw[0:64, :], AF.Abs)
    b.se.drain()
    b.se.activation(lndt_hm[0:64, :], lndt_hm[0:64, :], AF.Exp, scale=-1.0)
    b.se.drain()
    ts = b.inc("se", b.se.activation(lndt_hm[0:64, :], lndt_hm[0:64, :], AF.Ln,
                                     bias=colmisc[0:64, 2:3]))
    b.wait(b.ve, ts)
    b.ve.tensor_scalar(dt_hm[0:64, :], dt_raw[0:64, :], 0.0, None, ALU.max)
    b.ve.drain()
    b.ve.tensor_tensor(dt_hm[0:64, :], dt_hm[0:64, :], lndt_hm[0:64, :], ALU.add)
    b.ve.drain()
    b.ve.tensor_scalar(dt_hm[0:64, :], dt_hm[0:64, :], 1e-30, None, ALU.max)
    b.ve.drain()
    b.ve.tensor_scalar(adt_hm[0:64, :], dt_hm[0:64, :], colmisc[0:64, 1:2],
                       None, ALU.mult)
    b.ve.drain()
    tv = b.inc("ve", b.ve.nop(nofuse=True))
    b.wait(b.se, tv)
    b.se.drain()
    b.inc("se", b.se.activation(lndt_hm[0:64, :], dt_hm[0:64, :], AF.Ln))
    b.barrier()

    b.mark()
    St = b.tile((N, 64 * P))
    stbf = b.tile((N, 64 * P), BF16)
    x_tm = b.tile((128, 32, 128), BF16)
    hswin = [b.tile((128, 32, 128), BF16) for _ in range(3)]
    b_tm = b.tile((128, G, L["N"]), BF16)
    acum = b.tile((128, 8, 64))
    vtile = b.tile((128, 8, 64))
    dtdec = b.tile((128, 8, 64))
    lamrep = b.tile((128, 8, 64))
    expacum = b.tile((128, 8, 64), BF16)
    dt_tm = b.tile((128, 64))
    scr = b.tile((128, 64))
    lam_raw = b.tile((64, 1))
    lam_rowt = b.tile((1, 64))
    xd2 = [b.tile((128, 512), BF16) for _ in range(2)]
    xd_st = xd2[0]
    cbt = b.tile((128, G, 128), BF16)
    mt2 = [b.tile((128, 1024), BF16) for _ in range(2)]
    e1 = b.tile((128, 1024))
    ea_s = b.tile((128, 512))
    t1t = b.tile((128, 512))
    t2t = t1t.rearrange("p (a c) -> p a c", a=4)
    zbuf = [b.tile((128, 4, 128), BF16) for _ in range(2)]

    dbg_scr = b.tile((128, 512)) if KDBG else None

    def dbg_dump(ap, idx, np_=128):
        nfree = 1
        for d in ap.shape[1:]:
            nfree *= d
        b.barrier()
        b.ve.tensor_copy(dbg_scr[0:np_, 0:nfree], ap)
        b.ve.drain()
        b.inc("ve", b.ve.nop(nofuse=True))
        b.barrier()
        b.dma(b.po, "st0", dram["dbg"][idx][0:np_, 0:nfree],
              dbg_scr[0:np_, 0:nfree])
        b.barrier()

    b.po.memset(St[:, :], 0.0)
    b.po.drain()
    b.inc("po", b.po.nop(nofuse=True))
    b.barrier()

    def chunk_prep(c):
        cs = slice(c * 128, (c + 1) * 128)
        pp = b.pE
        b.tr(pp[:, 0:64], adt_hm[0:64, cs], cons["ident_f"][0:64, 0:64])
        b.tr(pp[:, 64:128], dt_hm[0:64, cs], cons["ident_f"][0:64, 0:64])
        b.tr(pp[:, 128:192], lndt_hm[0:64, cs],
                       cons["ident_f"][0:64, 0:64])
        t = b.pe_tok()
        b.wait(b.se, t)
        b.se.copy(scr[:, :], pp[:, 0:64])
        b.se.copy(dt_tm[:, :], pp[:, 64:128])
        b.se.drain()
        ts = b.inc("se", b.se.copy(vtile[:, c, :], pp[:, 128:192]))
        b.wait(b.pe, ts)
        b.mm(pp[:, 192:256], cons["triu_f"][:, :], scr[:, :],
                    start=True, stop=True)
        t = b.pe_tok()
        b.wait(b.se, t)
        ts = b.inc("se", b.se.copy(acum[:, c, :], pp[:, 192:256]))
        b.wait(b.pe, ts)
        b.wait(b.sy, ts)
        trow = b.dma(b.sy, "lam0", lam_rowt[:, :], acum[127:128, c, :])
        b.wait(b.pe, trow)
        b.tr(pp[0:64, 256:257], lam_rowt[0:1, :],
                       cons["ident_f"][0:1, 0:1])
        t = b.pe_tok()
        b.wait(b.se, t)
        ts = b.inc("se", b.se.copy(lam_raw[:, :], pp[0:64, 256:257]))
        b.wait(b.pe, ts)
        b.tr(pp[:, 320:384],
                       lam_raw[:, 0:1].broadcast_to([64, 128]),
                       cons["ident_f"][0:64, 0:64])
        t = b.pe_tok()
        b.wait(b.ve, t)
        b.ve.tensor_tensor(scr[:, :], pp[:, 320:384], acum[:, c, :], ALU.subtract)
        b.ve.drain()
        tv = b.inc("ve", b.ve.tensor_tensor(vtile[:, c, :], vtile[:, c, :],
                                            acum[:, c, :], ALU.subtract))
        b.wait(b.se, tv)
        b.se.activation(scr[:, :], scr[:, :], AF.Exp)
        b.se.drain()
        b.se.activation(lamrep[:, c, :], pp[:, 320:384], AF.Exp)
        b.se.drain()
        ts = b.inc("se", b.se.activation(expacum[:, c, :], acum[:, c, :], AF.Exp))
        b.wait(b.ve, ts)
        tv = b.inc("ve", b.ve.tensor_tensor(dtdec[:, c, :], scr[:, :],
                                            dt_tm[:, :], ALU.mult))
        return tv

    def load_hswin(c, sl):
        b.wait(b.sy, b.tok("pe"))
        b.wait(b.po, b.tok("pe"))
        src = dram["hs_spill"][0:32, :, c * 128:(c + 1) * 128] \
            .rearrange("a p c -> p a c")
        b.dma(b.sy, f"hw{sl}", hswin[sl][:, 0:16, :], src[:, 0:16, :])
        return b.dma(b.po, f"hw{sl}", hswin[sl][:, 16:32, :], src[:, 16:32, :])

    def x_transposes(c, sl, want_xd):
        for batch in range(8):
            pp = b.pD if batch % 2 == 0 else b.pE
            ppb = pp.bitcast(BF16)
            for f4 in range(4):
                f = batch * 4 + f4
                b.tr(ppb[:, f4 * 128:(f4 + 1) * 128],
                               hswin[sl][:, f, :], cons["ident_b"][:, :])
            t = b.pe_tok()
            b.wait(b.ve, t)
            b.ve.tensor_copy(x_tm[:, batch * 4:batch * 4 + 4, :]
                             .rearrange("p a c -> p (a c)"), ppb[:, 0:512])
            b.ve.drain()
            tv = b.inc("ve", b.ve.nop(nofuse=True))
            b.wait(b.pe, tv)
        if L["l0"]:
            b.tr(b.pD.bitcast(BF16)[:, 0:64], bc[0:64, 0, c * 128:(c + 1) * 128],
                           cons["ident_b"][0:64, 0:64])
            t = b.pe_tok()
            b.wait(b.ve, t)
            b.ve.tensor_copy(b_tm[:, 0, :], b.pD.bitcast(BF16)[:, 0:64])
        else:
            for g in range(8):
                pp = b.pD if g % 2 == 0 else b.pE
                b.tr(pp.bitcast(BF16)[:, 0:128],
                               bc[:, g, c * 128:(c + 1) * 128], cons["ident_b"][:, :])
                t = b.pe_tok()
                b.wait(b.ve, t)
                b.ve.tensor_copy(b_tm[:, g, :], pp.bitcast(BF16)[:, 0:128])
                b.ve.drain()
                tv = b.inc("ve", b.ve.nop(nofuse=True))
                b.wait(b.pe, tv)
        b.ve.drain()
        return b.inc("ve", b.ve.nop(nofuse=True))

    def lam_scale_st(c):
        for h in range(64):
            b.ve.tensor_scalar(St[:, h * P:(h + 1) * P], St[:, h * P:(h + 1) * P],
                               lamrep[0:N, c, h:h + 1], None, ALU.mult)
        b.ve.drain()
        return b.inc("ve", b.ve.nop(nofuse=True))

    def states_pass(c, sl, seed_scale):
        tlam = lam_scale_st(c)
        for g in range(8):
            bg = g if G == 8 else 0
            for hh in range(8):
                h = g * 8 + hh
                b.ve.tensor_scalar(xd_st[:, hh * 64:(hh + 1) * 64],
                                   x_tm[:, h // 2, (h % 2) * 64:(h % 2) * 64 + 64],
                                   dtdec[:, c, h:h + 1], None, ALU.mult)
            b.ve.drain()
            tv = b.inc("ve", b.ve.nop(nofuse=True))
            b.wait(b.pe, tv)
            half = g % 2
            for hh in range(8):
                b.mm(b.pC[0:N, half * 512 + hh * 64:half * 512 + hh * 64 + 64],
                            b_tm[:, bg, :], xd_st[:, hh * 64:(hh + 1) * 64],
                            start=True, stop=True)
            t = b.pe_tok()
            b.wait(b.ve, t)
            b.ve.tensor_tensor(St[:, g * 512:(g + 1) * 512], St[:, g * 512:(g + 1) * 512],
                               b.pC[0:N, half * 512:half * 512 + 512], ALU.add)
            b.ve.drain()
            tv = b.inc("ve", b.ve.nop(nofuse=True))
            b.wait(b.pe, tv)

    hw_toks = {0: load_hswin(0, 0), 1: load_hswin(1, 1)}
    for c in range(NCH):
        chunk_prep(c)
        b.wait(b.ve, hw_toks[c])
        b.wait(b.pe, hw_toks[c])
        if c + 2 < NCH:
            hw_toks[c + 2] = load_hswin(c + 2, (c + 2) % 3)
        x_transposes(c, c % 3, True)
        states_pass(c, c % 3, None)
    b.barrier()

    cin_s = dram["cin_st64"] if L["l0"] else dram["cin_st128"]
    cout_s = dram["cout_st64"] if L["l0"] else dram["cout_st128"]
    tst = b.dma(b.po, "st0", cin_s.ap()[0:N, 0:64 * P], St[:, :])
    b.wait(b.po, tst)
    tcc = b.inc("cc", b.po.collective_compute(
        "AllGather", ALU.bypass, replica_groups=[[0, 1], [2, 3], [4, 5], [6, 7]],
        ins=[cin_s.ap().opt()], outs=[cout_s.ap().opt()]))
    b.wait(b.po, tcc)
    tg = b.dma(b.po, "st0", St[:, :], cout_s.ap()[0:N, 0:64 * P])
    b.wait(b.ve, tg)
    for h in range(64):
        b.ve.tensor_scalar(St[:, h * P:(h + 1) * P], St[:, h * P:(h + 1) * P],
                           cons["flag_hi"][0:N, 0:1], None, ALU.mult)
    b.ve.drain()
    b.inc("ve", b.ve.nop(nofuse=True))
    b.barrier()

    zr = {}

    def load_z(c, g, sl):
        b.wait(b.sy, b.tok("ve"))
        src = dram["z_spill"][4 * g:4 * g + 4, :, c * 128:(c + 1) * 128] \
            .rearrange("a p c -> p a c")
        return b.dma(b.sy, f"zr{sl}", zbuf[sl][:, :, :], src)

    hw_toks = {0: load_hswin(0, 0), 1: load_hswin(1, 1)}
    zr[(0, 0)] = load_z(0, 0, 0)
    zslot = {(0, 0): 0}
    state = {"mt_tok": None, "ea_tok": None, "stadd_tok": None, "xd_tok": None,
             "e1_tok": None, "ydiag_tok": {}, "states_tok": {},
             "comb_tok2": {}}

    def prep_slot(c, g):
        cs = slice(c * 128, (c + 1) * 128)
        if state["e1_tok"] is not None:
            b.wait(b.pe, state["e1_tok"])
        for j8 in range(8):
            h = g * 8 + j8
            b.tr(b.pA[:, j8 * 128:(j8 + 1) * 128],
                 acum[:, c, h:h + 1].broadcast_to([128, 128]),
                 cons["ident_f"][:, :])
        t1tok = b.pe_tok()
        b.wait(b.ve, t1tok)
        vview = vtile[:, c, g * 8:(g + 1) * 8].unsqueeze(2) \
            .broadcast_to([128, 8, 128])
        b.ve.tensor_tensor(e1.rearrange("p (a d) -> p a d", a=8),
                           b.pA.rearrange("p (a d) -> p a d", a=8),
                           vview, ALU.add)
        b.ve.drain()
        state["e1_tok"] = b.inc("ve", b.ve.tensor_scalar(
            e1[:, :], e1[:, :], 30.0, None, ALU.min))
        b.wait(b.se, state["e1_tok"])
        b.se.drain()
        texp = b.inc("se", b.se.activation(e1[:, :], e1[:, :], AF.Exp))
        b.wait(b.ve, texp)
        par = (c * 8 + g) % 2
        if state["ydiag_tok"].get(par) is not None:
            b.wait(b.ve, state["ydiag_tok"][par])
        bg = g if G == 8 else 0
        cview = cbt[:, bg, :].unsqueeze(1).broadcast_to([128, 8, 128])
        b.ve.tensor_tensor(mt2[par].rearrange("p (a d) -> p a d", a=8),
                           e1.rearrange("p (a d) -> p a d", a=8),
                           cview, ALU.mult)
        b.ve.drain()
        state["mt_tok"] = b.inc("ve", b.ve.nop(nofuse=True))
        if state["states_tok"].get(par) is not None:
            b.wait(b.ve, state["states_tok"][par])
        for j8 in range(8):
            h = g * 8 + j8
            b.ve.tensor_scalar(xd2[par][:, j8 * 64:(j8 + 1) * 64],
                               x_tm[:, h // 2, (h % 2) * 64:(h % 2) * 64 + 64],
                               dtdec[:, c, h:h + 1], None, ALU.mult)
        b.ve.drain()
        state["xd_tok"] = b.inc("ve", b.ve.nop(nofuse=True))

    def ywork_slot(c, g):
        cs = slice(c * 128, (c + 1) * 128)
        bg = g if G == 8 else 0
        par = (c * 8 + g) % 2
        pY = b.pB if par == 0 else b.pC
        b.wait(b.pe, state["mt_tok"])
        if state["comb_tok2"].get(par) is not None:
            b.wait(b.pe, state["comb_tok2"][par])
        mt8 = mt2[par]
        first = True
        for j8 in range(8):
            h = g * 8 + j8
            jp = j8 // 2
            b.mm(pY[(h % 2) * 64:(h % 2) * 64 + 64, jp * 128:(jp + 1) * 128],
                 x_tm[:, h // 2, (h % 2) * 64:(h % 2) * 64 + 64],
                 mt8[:, j8 * 128:(j8 + 1) * 128],
                 start=(j8 <= 1), stop=False, skip_group_check=True,
                 tile_position=(0, (h % 2) * 64))
        ydiag_tok = b.pe_tok()
        state["ydiag_tok"][par] = ydiag_tok
        for j in range(4):
            jj = 4 * g + j
            b.mm(pY[:, j * 128:(j + 1) * 128], diag[:, jj, :],
                 hswin[c % 3][:, jj, :], start=False, stop=(j == 3),
                 skip_group_check=True)
        for j8 in range(8):
            h = g * 8 + j8
            jp = j8 // 2
            if L["l0"]:
                crhs = c_all[:, cs]
                lhs = stbf[0:64, h * P:(h + 1) * P]
            else:
                crhs = bc[:, 8 + bg, cs]
                lhs = stbf[:, h * P:(h + 1) * P]
            b.mm(pY[(h % 2) * 64:(h % 2) * 64 + 64,
                    512 + jp * 128:512 + (jp + 1) * 128],
                 lhs, crhs, start=(j8 <= 1), stop=(j8 == 7),
                 skip_group_check=True, tile_position=(0, (h % 2) * 64))
        if state["ea_tok"] is not None:
            b.wait(b.pe, state["ea_tok"])
        pdb = b.pD.bitcast(BF16)
        for j in range(4):
            for parh in range(2):
                h = g * 8 + 2 * j + parh
                eview = expacum[:, c, h:h + 1].broadcast_to([128, 64])
                b.tr(pdb[parh * 64:parh * 64 + 64, j * 128:(j + 1) * 128],
                     eview, cons["ident_b"][:, :])
        b.wait(b.pe, state["xd_tok"])
        if state["stadd_tok"] is not None:
            b.wait(b.pe, state["stadd_tok"])
        for j8 in range(8):
            b.mm(b.pE[0:N, j8 * 64:(j8 + 1) * 64], b_tm[:, bg, :],
                 xd2[par][:, j8 * 64:(j8 + 1) * 64], start=True, stop=True)
        tall = b.pe_tok()
        state["states_tok"][par] = tall
        b.wait(b.se, tall)
        b.se.drain()
        tea = b.inc("se", b.se.copy(ea_s[:, :], pdb[:, 0:512]))
        state["ea_tok"] = tea
        b.wait(b.ve, tall)
        b.ve.tensor_tensor(St[:, g * 512:(g + 1) * 512],
                           St[:, g * 512:(g + 1) * 512],
                           b.pE[0:N, 0:512], ALU.add)
        b.ve.drain()
        state["stadd_tok"] = b.inc("ve", b.ve.nop(nofuse=True))
        b.wait(b.ve, tea)
        b.ve.tensor_tensor(t1t[:, :], pY[:, 512:1024], ea_s[:, :], ALU.mult)
        b.ve.drain()
        b.ve.scalar_tensor_tensor(t1t[:, :], t1t[:, :], 0.0, pY[:, 0:512],
                                  ALU.add, ALU.add)
        b.ve.drain()
        b.wait(b.ve, zr[(c, g)])
        ng = (c, g + 1) if g < 7 else (c + 1, 0)
        if ng[0] < NCH:
            zsl = (zslot[(c, g)] + 1) % 2
            zslot[ng] = zsl
            zr[ng] = load_z(ng[0], ng[1], zsl)
        b.ve.tensor_tensor(y_buf[:, 4 * g:4 * g + 4, cs],
                           t1t.rearrange("p (a d) -> p a d", a=4),
                           zbuf[zslot[(c, g)]][:, :, :], ALU.mult)
        b.ve.drain()
        state["comb_tok2"][(c * 8 + g) % 2] = b.inc("ve", b.ve.nop(nofuse=True))

    for c in range(NCH):
        b.wait(b.ve, hw_toks[c])
        b.wait(b.pe, hw_toks[c])
        if c + 2 < NCH:
            hw_toks[c + 2] = load_hswin(c + 2, (c + 2) % 3)
        if state["ea_tok"] is not None:
            b.wait(b.pe, state["ea_tok"])
        if state["stadd_tok"] is not None:
            b.wait(b.pe, state["stadd_tok"])
        x_transposes(c, c % 3, False)
        cs = slice(c * 128, (c + 1) * 128)
        b.wait(b.pe, b.tok("ve"))
        ncb = 8 if G == 8 else 1
        for bg in range(ncb):
            pp = b.pD if bg < 4 else b.pE
            off = (bg % 4) * 128
            if L["l0"]:
                b.mm(pp[:, off:off + 128], bc[0:64, 0, cs], c_all[:, cs],
                     start=True, stop=True)
            else:
                b.mm(pp[:, off:off + 128], bc[:, bg, cs], bc[:, 8 + bg, cs],
                     start=True, stop=True)
        tcb = b.pe_tok()
        b.wait(b.ve, tcb)
        for bg in range(ncb):
            pp = b.pD if bg < 4 else b.pE
            off = (bg % 4) * 128
            b.ve.tensor_tensor(cbt[:, bg, :], pp[:, off:off + 128],
                               cons["triu_f"][:, :], ALU.mult)
        b.ve.drain()
        b.wait(b.ve, b.tok("pe"))
        b.inc("ve", b.ve.tensor_copy(stbf[:, :], St[:, :]))
        b.ve.drain()
        lam_scale_st(c)
        tbnd = b.tok("ve")
        b.wait(b.pe, tbnd)
        state["ea_tok"] = None
        state["stadd_tok"] = None
        for g in range(8):
            prep_slot(c, g)
            if g >= 1:
                ywork_slot(c, g - 1)
        ywork_slot(c, 7)
    b.barrier()

    normw = b.tile((128, 32))
    rstd = b.tile((128, T))
    sqb = b.tile((128, T), BF16)
    tok = b.dma(b.sy, "misc0", normw[:, :], cons[L["normw"]][:, :])
    pet = {}
    for f in range(32):
        b.wait(b.se, tok)
        if f >= 1:
            b.wait(b.se, pet[f - 1])
        b.se.drain()
        tsq = b.inc("se", b.se.activation(sqb[:, :], y_buf[:, f, :], AF.Square))
        b.wait(b.pe, tsq)
        b.mm(b.pA[:, 0:512], cons["onesb"][:, :], sqb[:, 0:512],
                    start=(f == 0), stop=(f == 31))
        b.mm(b.pA[:, 512:1024], cons["onesb"][:, :], sqb[:, 512:1024],
                    start=(f == 0), stop=(f == 31))
        pet[f] = b.pe_tok()
    b.wait(b.ve, pet[31])
    b.ve.tensor_scalar(rstd[:, :], b.pA[:, :], 1.0 / 4096.0, None, ALU.mult)
    b.ve.drain()
    tv = b.inc("ve", b.ve.nop(nofuse=True))
    b.wait(b.se, tv)
    b.se.activation(rstd[:, :], rstd[:, :], AF.Ln, bias=cons[L["eps_rms"]][:, 0:1])
    b.se.drain()
    ts = b.inc("se", b.se.activation(rstd[:, :], rstd[:, :], AF.Exp, scale=-0.5))
    b.wait(b.ve, ts)
    if KDBG == 2:
        dbg_dump(rstd[:, 0:512], 10)
    for f in range(32):
        b.ve.scalar_tensor_tensor(y_buf[:, f, :], y_buf[:, f, :],
                                  normw[:, f:f + 1], rstd[:, :],
                                  ALU.mult, ALU.mult)
    b.ve.drain()
    b.inc("ve", b.ve.nop(nofuse=True))
    if KDBG == 2:
        dbg_dump(y_buf[:, 0:4, 0:128], 11)
    b.barrier()
    b.release()

    b.mark()
    x_ld = [b.tile((128, T)) for _ in range(2)]
    res = [b.tile((128, T)) for _ in range(2)]
    vt = {}
    stok = {}

    def rhs_out(kc, t0, t1):
        return y_buf[:, kc, t0:t1]

    last_tv = {}

    def epi_out(m, t0, t1, ps, ps2, tok):
        if t0 == 0:
            if m >= 2:
                b.wait(b.sy, last_tv[m - 2])
            vt[m] = b.dma(b.sy, f"lnx{m % 2}", x_ld[m % 2][:, :], x_in[m])
        b.wait(b.ve, tok)
        b.wait(b.ve, vt[m])
        if m >= 2 and t0 == 0:
            b.wait(b.ve, stok[m - 2])
        tv = b.inc("ve", b.ve.tensor_tensor(res[m % 2][:, t0:t1], ps,
                                            x_ld[m % 2][:, t0:t1], ALU.add))
        if t1 == T:
            last_tv[m] = tv
            b.wait(b.po, tv)
            stok[m] = b.dma(b.po, f"st{m % 2}", x_out[m], res[m % 2][:, :])
        return tv

    emit_gemm(b, cons[L["wout"]], 16, 32, rhs_out, epi_out)
    b.release()
    b.release()


L0_ATTN = dict(l0=True, nch_xbc=33, M_in=66, N=64, G=1, pad_left=3,
               win="l0_win", wout="l0_wout", convw="l0_convw", convb="l0_convb",
               dtb="l0_dtb", alog="l0_alog", dpair="l0_dpair", normw="l0_normw",
               eps_rms="eps5")
L1_ATTN = dict(l0=False, nch_xbc=48, M_in=81, N=128, G=8, pad_left=1,
               win="l1_win", wout="l1_wout", convw="l1_convw", convb="l1_convb",
               dtb="l1_dtb", alog="l1_alog", dpair="l1_dpair", normw="l1_normw",
               eps_rms="eps6")



L0_FFN = dict(eps_ln="eps5", gwT="l0_gwT", gbrep="l0_gbrep", ngcol=8, nsig=0,
              renorm=False, E=8, EI=1024, MI=8, MSH=8, sh_split=1,
              wgt="l0_w1", wup="l0_w3", wgup=None, wdn="l0_w2",
              wsg="l0_ws1", wsu="l0_ws3", wsd="l0_ws2")
L1_FFN = dict(eps_ln="eps5", gwT="l1_gwT", gbrep=None, ngcol=9, nsig=1,
              renorm=True, E=8, EI=768, MI=12, MSH=32, sh_split=2,
              wgt=None, wup=None, wgup="l1_gup", wdn="l1_down",
              wsg="l1_sg", wsu="l1_su", wsd="l1_sd")

GLOBAL_KEYS = ["ident_f", "ident_b", "triu_f", "onesb", "i8e_b",
               "l0_win", "l0_wout", "l0_convw", "l0_convb", "l0_dtb", "l0_alog",
               "l0_dpair", "l0_normw",
               "l0_gwT", "l0_gbrep", "l0_w1", "l0_w3", "l0_w2", "l0_ws1",
               "l0_ws3", "l0_ws2",
               "l1_win", "l1_wout", "l1_convw", "l1_convb", "l1_dtb", "l1_alog",
               "l1_dpair", "l1_normw",
               "l1_gwT", "l1_gup", "l1_down", "l1_sg", "l1_su", "l1_sd"]


def _np_dtype_to_mybir(a):
    import ml_dtypes
    if a.dtype == np.float32:
        return F32
    if a.dtype == ml_dtypes.bfloat16:
        return BF16
    raise AssertionError(str(a.dtype))


def build_program(blocks, g):
    nc = bass.Bass("TRN2", target_bir_lowering=False)
    cons = {}
    for k in GLOBAL_KEYS:
        a = g[k]
        t = nc.dram_tensor(k, list(a.shape), _np_dtype_to_mybir(a),
                           kind="ExternalInput")
        cons[k] = t.ap()
    x0 = nc.dram_tensor("x0", [16, 128, T], F32, kind="ExternalInput").ap()
    y = nc.dram_tensor("y", [16, 128, T], F32, kind="ExternalOutput").ap()
    flag_hi = nc.dram_tensor("flag_hi", [128, 1], F32, kind="ExternalInput").ap()
    flag_lo = nc.dram_tensor("flag_lo", [128, 1], F32, kind="ExternalInput").ap()
    resA = nc.dram_tensor("resA", [16, 128, T], F32).ap()
    resB = nc.dram_tensor("resB", [16, 128, T], F32).ap()
    dram = dict(flag_hi=flag_hi, flag_lo=flag_lo,
                hs_spill=nc.dram_tensor("hs_spill", [32, 128, T], BF16).ap(),
                z_spill=nc.dram_tensor("z_spill", [32, 128, T], BF16).ap(),
                xbc_spill=nc.dram_tensor("xbc_spill", [48, 128, T], BF16).ap(),
                cin_tail=nc.dram_tensor("cin_tail", [128, 256], F32),
                cout_tail=nc.dram_tensor("cout_tail", [256, 256], F32),
                dbg=nc.dram_tensor("dbg", [16, 128, 512], F32),
                cin_st64=nc.dram_tensor("cin_st64", [64, 4096], F32),
                cout_st64=nc.dram_tensor("cout_st64", [128, 4096], F32),
                cin_st128=nc.dram_tensor("cin_st128", [128, 4096], F32),
                cout_st128=nc.dram_tensor("cout_st128", [256, 4096], F32))

    with ExitStack() as ctx:
        b = Builder(nc, ctx)
        blk = ctx.enter_context(nc.Block())
        e5 = b.tile((128, 1))
        e6 = b.tile((128, 1))
        b.po.memset(e5[:, :], 1e-5)
        b.po.memset(e6[:, :], 1e-6)
        b.po.drain()
        b.inc("po", b.po.nop(nofuse=True))
        b.barrier()
        cons["eps5"] = e5
        cons["eps6"] = e6

        for k, shape, dt in [("ident_f", (128, 128), F32),
                             ("ident_b", (128, 128), BF16),
                             ("triu_f", (128, 128), F32),
                             ("onesb", (128, 128), BF16)]:
            tile = b.tile(shape, dt)
            tok = b.inc("ld", b.sy.dma_start(out=tile[:, :], in_=cons[k][:, :]), 16)
            cons[k] = tile
        i8 = []
        for e in range(9):
            tile = b.tile((9, 128), BF16)
            b.inc("ld", b.sy.dma_start(out=tile[:, :], in_=cons["i8e_b"][e]), 16)
            i8.append(tile)
        cons["i8e_b"] = i8
        for nmf, src in [("flag_hi", flag_hi), ("flag_lo", flag_lo)]:
            tile = b.tile((128, 1))
            b.inc("ld", b.sy.dma_start(out=tile[:, :], in_=src[:, :]), 16)
            cons[nmf] = tile
        b.barrier()

        cur = x0
        nxt = [resA, resB]
        from_attn = globals().get("emit_attn")
        for i, name in enumerate(blocks):
            out = y if i == len(blocks) - 1 else nxt[i % 2]
            if name == "ffn0":
                emit_ffn(b, cons, L0_FFN, cur, out, dram)
            elif name == "ffn1":
                emit_ffn(b, cons, L1_FFN, cur, out, dram)
            elif name == "attn0":
                from_attn(b, cons, L0_ATTN, cur, out, dram)
            elif name == "attn1":
                from_attn(b, cons, L1_ATTN, cur, out, dram)
            cur = out
        b.wait(b.po, b.tok("gd"))
        b.wait(b.po, b.tok("ld"))
        print(f"[build] arena peak {b.peak/1024:.1f} KB/partition; "
              f"sems {b.cnt}")
    return nc



def run_blocks(inputs, blocks):
    g, percore = stage_inputs(inputs)
    nc = build_program(blocks, g)
    in_maps = []
    for c in range(NCORES):
        m = {k: g[k] for k in GLOBAL_KEYS}
        m.update(percore[c])
        in_maps.append(m)
    res = run_bass_kernel_spmd(nc, in_maps, core_ids=list(range(NCORES)))
    out = np.empty((4, 2048, 2048), np.float32)
    for c in range(NCORES):
        bb, h = c // 2, c % 2
        yc = np.asarray(res.results[c]["y"])
        out[bb, h * T:(h + 1) * T, :] = yc.reshape(2048, T).T
    return out


def kernel(**inputs):
    return run_blocks(inputs, ["attn0", "ffn0", "attn1", "ffn1"])


# version 16
# speedup vs baseline: 1.2101x; 1.0279x over previous
import numpy as np
from contextlib import ExitStack

import concourse.bass as bass
import concourse.mybir as mybir
from concourse.bass_utils import run_bass_kernel_spmd

F32 = mybir.dt.float32
BF16 = mybir.dt.bfloat16
AF = mybir.ActivationFunctionType
ALU = mybir.AluOpType
AX = mybir.AxisListType

T = 1024
NCORES = 8
DBG = {}



def _tile_w(w, transpose):
    w = np.asarray(w, np.float32)
    if transpose:
        w = np.ascontiguousarray(w.T)
    in_f, out_f = w.shape
    assert in_f % 128 == 0
    kc = in_f // 128
    m = (out_f + 127) // 128
    wp = np.zeros((in_f, m * 128), np.float32)
    wp[:, :out_f] = w
    out = wp.reshape(kc, 128, m, 128).transpose(2, 1, 0, 3)
    return np.ascontiguousarray(out)


def stage_inputs(inputs):
    import ml_dtypes
    g = {}
    g["ident_f"] = np.eye(128, dtype=np.float32)
    g["ident_b"] = np.eye(128, dtype=ml_dtypes.bfloat16)
    g["triu_f"] = np.triu(np.ones((128, 128), np.float32))
    g["onesb"] = np.ones((128, 128), dtype=ml_dtypes.bfloat16)
    i8e = np.zeros((9, 9, 128), dtype=ml_dtypes.bfloat16)
    for e in range(9):
        i8e[e, e, :] = 1.0
    g["i8e_b"] = i8e

    def dpair_of(dp):
        dp = np.asarray(dp, np.float32)
        out = np.zeros((128, 32), np.float32)
        out[0:64, :] = dp[0::2][None, :]
        out[64:128, :] = dp[1::2][None, :]
        return out

    g["l0_win"] = _tile_w(inputs["L0_attn_W_in"], True)
    g["l0_wout"] = _tile_w(inputs["L0_attn_W_out"], True)
    g["l0_convw"] = np.ascontiguousarray(
        np.asarray(inputs["L0_attn_conv_w"], np.float32).reshape(33, 128, 4)
        .transpose(1, 0, 2))
    g["l0_convb"] = np.ascontiguousarray(
        np.asarray(inputs["L0_attn_conv_b"], np.float32).reshape(33, 128)
        .transpose(1, 0))
    g["l0_dtb"] = np.asarray(inputs["L0_attn_dt_bias"], np.float32).reshape(64, 1)
    g["l0_alog"] = np.asarray(inputs["L0_attn_A_log"], np.float32).reshape(64, 1)
    g["l0_dpair"] = dpair_of(inputs["L0_attn_D_param"])
    g["l0_normw"] = np.ascontiguousarray(
        np.asarray(inputs["L0_attn_norm_w"], np.float32).reshape(32, 128)
        .transpose(1, 0))
    g["l0_gwT"] = np.ascontiguousarray(
        np.asarray(inputs["L0_ffn_gate_w"], np.float32).T.reshape(16, 128, 8)
        .transpose(1, 0, 2))
    g["l0_gbrep"] = np.ascontiguousarray(np.broadcast_to(
        np.asarray(inputs["L0_ffn_gate_b"], np.float32), (128, 8)))
    g["l0_w1"] = np.stack([_tile_w(inputs["L0_ffn_W1"][e], True) for e in range(8)])
    g["l0_w3"] = np.stack([_tile_w(inputs["L0_ffn_W3"][e], True) for e in range(8)])
    g["l0_w2"] = np.stack([_tile_w(inputs["L0_ffn_W2"][e], True) for e in range(8)])
    g["l0_ws1"] = _tile_w(inputs["L0_ffn_Ws1"], True)
    g["l0_ws3"] = _tile_w(inputs["L0_ffn_Ws3"], True)
    g["l0_ws2"] = _tile_w(inputs["L0_ffn_Ws2"], True)
    g["l1_win"] = _tile_w(inputs["L1_attn_in_proj_w"], False)
    g["l1_wout"] = _tile_w(inputs["L1_attn_out_proj_w"], False)
    cw1 = np.ascontiguousarray(np.asarray(inputs["L1_attn_conv_w"], np.float32)[:, 0, :].T)
    g["l1_convw"] = np.ascontiguousarray(cw1.reshape(48, 128, 4).transpose(1, 0, 2))
    g["l1_convb"] = np.ascontiguousarray(
        np.asarray(inputs["L1_attn_conv_b"], np.float32).reshape(48, 128)
        .transpose(1, 0))
    g["l1_dtb"] = np.asarray(inputs["L1_attn_dt_bias"], np.float32).reshape(64, 1)
    g["l1_alog"] = np.asarray(inputs["L1_attn_A_log"], np.float32).reshape(64, 1)
    g["l1_dpair"] = dpair_of(inputs["L1_attn_D"])
    g["l1_normw"] = np.ascontiguousarray(
        np.asarray(inputs["L1_attn_norm_w"], np.float32).reshape(32, 128)
        .transpose(1, 0))
    gw9 = np.concatenate([np.asarray(inputs["L1_ffn_gate_w"], np.float32),
                          np.asarray(inputs["L1_ffn_shared_expert_gate_w"], np.float32)],
                         axis=0)
    g["l1_gwT"] = np.ascontiguousarray(gw9.T.reshape(16, 128, 9).transpose(1, 0, 2))
    g["l1_gup"] = np.stack([_tile_w(inputs["L1_ffn_gate_up_proj"][e], True)
                            for e in range(8)])
    g["l1_down"] = np.stack([_tile_w(inputs["L1_ffn_down_proj"][e], True)
                             for e in range(8)])
    g["l1_sg"] = _tile_w(inputs["L1_ffn_shared_gate_w"], True)
    g["l1_su"] = _tile_w(inputs["L1_ffn_shared_up_w"], True)
    g["l1_sd"] = _tile_w(inputs["L1_ffn_shared_down_w"], True)

    u = np.asarray(inputs["L0_attn_u"], np.float32)
    percore = []
    for c in range(NCORES):
        bb, h = c // 2, c % 2
        sl = u[bb, h * T:(h + 1) * T, :]
        x0 = np.ascontiguousarray(sl.T).reshape(16, 128, T)
        percore.append({"x0": x0,
                        "flag_hi": np.full((128, 1), float(h), np.float32),
                        "flag_lo": np.full((128, 1), float(1 - h), np.float32)})
    return g, percore



class Tok:
    __slots__ = ("sem", "v")

    def __init__(self, sem, v):
        self.sem = sem
        self.v = v


class Builder:
    def __init__(self, nc, ctx):
        self.nc = nc
        self.ctx = ctx
        self.pe = nc.tensor
        self.ve = nc.vector
        self.se = nc.scalar
        self.po = nc.gpsimd
        self.sy = nc.sync
        self.sems = {}
        for name in ["ld", "gd", "pe", "ve", "se", "po", "cc", "sy"]:
            self.sems[name] = ctx.enter_context(nc.semaphore(name))
        self.cnt = {name: 0 for name in self.sems}
        self.pA = ctx.enter_context(nc.psum_tensor("pA", [128, 1024], F32))
        self.pB = ctx.enter_context(nc.psum_tensor("pB", [128, 1024], F32))
        self.pC = ctx.enter_context(nc.psum_tensor("pC", [128, 1024], F32))
        self.pD = ctx.enter_context(nc.psum_tensor("pD", [128, 512], F32))
        self.pE = ctx.enter_context(nc.psum_tensor("pE", [128, 512], F32))
        self.CAP = 206 * 1024
        self.arena = ctx.enter_context(
            nc.sbuf_tensor("arena", [128, self.CAP // 4], F32))
        self.off = 0
        self.marks = []
        self.peak = 0
        self.last_pe = None
        self.psum_tok = {"A": None, "B": None}

    def inc(self, name, inst, n=1):
        inst.then_inc(self.sems[name], n)
        self.cnt[name] += n
        return Tok(name, self.cnt[name])

    def dslot(self, name):
        if name not in self.sems:
            self.sems[name] = self.ctx.enter_context(self.nc.semaphore(name))
            self.cnt[name] = 0
        return name

    def dma(self, q, slot, out, in_):
        self.dslot(slot)
        return self.inc(slot, q.dma_start(out=out, in_=in_), 16)

    def tok(self, name):
        return Tok(name, self.cnt[name])

    def pe_tok(self, inst=None):
        if inst is None:
            inst = self.last_pe
        if inst is not None:
            return self.inc("pe", inst)
        return self.inc("pe", self.pe.drain())

    def mm(self, *a, **k):
        self.last_pe = self.pe.matmul(*a, **k)
        return self.last_pe

    def tr(self, *a, **k):
        self.last_pe = self.pe.transpose(*a, **k)
        return self.last_pe

    def pe_warm(self, cons, n, ps=None):
        if ps is None:
            ps = self.pA
        for _ in range(n):
            self.last_pe = self.pe.matmul(ps[:, 0:128], cons["onesb"][:, :],
                                          cons["ident_b"][:, :],
                                          start=True, stop=True)

    def wait(self, eng, tok):
        if tok is not None:
            eng.wait_ge(self.sems[tok.sem], tok.v)

    def barrier(self):
        b = self
        for name, c in list(b.cnt.items()):
            if name in ("pe", "ve", "se", "po", "cc", "sy", "gd") or c == 0:
                continue
            if name.startswith("st"):
                continue
            b.sy.wait_ge(b.sems[name], c)
        tsy = b.inc("sy", b.sy.nop(nofuse=True))
        for name, c in list(b.cnt.items()):
            if c and (name == "gd" or name.startswith("st")):
                b.po.wait_ge(b.sems[name], c)
        b.ve.drain()
        tv = b.inc("ve", b.ve.nop(nofuse=True))
        b.se.drain()
        ts = b.inc("se", b.se.nop(nofuse=True))
        b.po.drain()
        tp = b.inc("po", b.po.nop(nofuse=True))
        tt = b.inc("pe", b.pe.drain())
        for eng in [b.pe, b.ve, b.se, b.po, b.sy]:
            for t in [tv, ts, tp, tt, tsy]:
                b.wait(eng, t)

    def alloc(self, nbytes, align=64):
        nbytes = -(-nbytes // align) * align
        o = -(-self.off // align) * align
        self.off = o + nbytes
        self.peak = max(self.peak, self.off)
        assert self.off <= self.CAP, f"SBUF arena overflow: {self.off}"
        return o

    def mark(self):
        self.marks.append(self.off)

    def release(self):
        self.off = self.marks.pop()

    def view(self, off, shape, dtype=F32):
        p = shape[0]
        free = 1
        for s in shape[1:]:
            free *= s
        dsz = 2 if dtype == BF16 else 4
        assert off % 4 == 0
        ap = self.arena[0:p, off // 4:(off + free * dsz + 3) // 4]
        if dtype != F32:
            ap = ap.bitcast(dtype)
            ap = ap[:, 0:free]
        if len(shape) == 2:
            return ap
        if len(shape) == 3:
            return ap.rearrange("p (a c) -> p a c", a=shape[1])
        if len(shape) == 4:
            return ap.rearrange("p (a c d) -> p a c d", a=shape[1], c=shape[2])
        raise AssertionError

    def tile(self, shape, dtype=F32):
        dsz = 2 if dtype == BF16 else 4
        free = 1
        for s in shape[1:]:
            free *= s
        return self.view(self.alloc(free * dsz), shape, dtype)



def emit_ln(b, cons, x_dram, xnorm, eps_name):
    b.mark()
    x_ld = [b.tile((128, T)) for _ in range(2)]
    sq = b.tile((128, T), BF16)
    mrep = b.tile((128, T))
    rstd = b.tile((128, T))
    c0 = b.tile((128, T))
    casttok = {}
    pet = {}
    for m in range(16):
        buf = x_ld[m % 2]
        if m >= 2:
            b.wait(b.sy, casttok[m - 2])
        tok = b.dma(b.sy, f"lnx{m % 2}", buf[:, :], x_dram[m])
        b.wait(b.ve, tok)
        casttok[m] = b.inc("ve", b.ve.tensor_copy(xnorm[:, m, :], buf[:, :]))
        b.wait(b.se, casttok[m])
        if m >= 1:
            b.wait(b.se, pet[m - 1])
        tsq = b.inc("se", b.se.activation(sq[:, :], xnorm[:, m, :], AF.Square))
        b.wait(b.pe, casttok[m])
        b.mm(b.pA[:, 0:512], cons["onesb"][:, :], xnorm[:, m, 0:512],
                    start=(m == 0), stop=(m == 15))
        b.mm(b.pA[:, 512:1024], cons["onesb"][:, :], xnorm[:, m, 512:1024],
                    start=(m == 0), stop=(m == 15))
        b.wait(b.pe, tsq)
        b.mm(b.pB[:, 0:512], cons["onesb"][:, :], sq[:, 0:512],
                    start=(m == 0), stop=(m == 15))
        lastmm = b.mm(b.pB[:, 512:1024], cons["onesb"][:, :], sq[:, 512:1024],
                             start=(m == 0), stop=(m == 15))
        pet[m] = b.pe_tok(lastmm)
    tstats = pet[15]
    inv = 1.0 / 2048.0
    b.wait(b.ve, tstats)
    b.ve.tensor_scalar(mrep[:, :], b.pA[:, :], inv, None, ALU.mult)
    b.ve.drain()
    b.ve.tensor_tensor(c0[:, :], mrep[:, :], mrep[:, :], ALU.mult)
    b.ve.drain()
    tv = b.inc("ve", b.ve.scalar_tensor_tensor(
        rstd[:, :], b.pB[:, :], inv, c0[:, :], ALU.mult, ALU.subtract))
    b.wait(b.se, tv)
    b.se.activation(rstd[:, :], rstd[:, :], AF.Ln, bias=cons[eps_name][:, 0:1])
    b.se.drain()
    ts = b.inc("se", b.se.activation(rstd[:, :], rstd[:, :], AF.Exp, scale=-0.5))
    b.wait(b.ve, ts)
    b.ve.drain()
    b.inc("ve", b.ve.scalar_tensor_tensor(
        c0[:, :], mrep[:, :], -1.0, rstd[:, :], ALU.mult, ALU.mult))
    b.ve.drain()
    vt = {-2: casttok[14], -1: casttok[15]}
    for m in range(16):
        buf = x_ld[m % 2]
        b.wait(b.sy, vt[m - 2])
        tok = b.dma(b.sy, f"lnx{m % 2}", buf[:, :], x_dram[m])
        b.wait(b.ve, tok)
        b.ve.tensor_tensor(buf[:, :], buf[:, :], rstd[:, :], ALU.mult)
        vt[m] = b.inc("ve", b.ve.tensor_tensor(
            xnorm[:, m, :], buf[:, :], c0[:, :], ALU.add))
        if m >= 1:
            b.wait(b.pe, vt[m - 1])
        b.pe_warm(cons, 12, b.pC)
    b.barrier()
    b.release()



def mk_gemm_bufs(b, KC, pair, nbuf=2):
    bufs = dict(KC=KC, pair=pair, nbuf=nbuf,
                wf=[b.tile((128, KC, 128)) for _ in range(nbuf)],
                wb=[b.tile((128, KC, 128), BF16) for _ in range(nbuf)],
                tok={}, n=0)
    if pair:
        bufs["wf2"] = [b.tile((128, KC, 128)) for _ in range(nbuf)]
        bufs["wb2"] = [b.tile((128, KC, 128), BF16) for _ in range(nbuf)]
    return bufs


def emit_gemm(b, w_dram, M, KC, rhs_fn, epi_fn, w2_dram=None, n_tok=T,
              nbuf=3, chain=None, rhs_tok=None):
    own = chain is None
    if KC > 16 or w2_dram is not None:
        nbuf = 2
    if own:
        b.mark()
        chain = mk_gemm_bufs(b, KC, w2_dram is not None, nbuf)
    else:
        assert chain["KC"] == KC and chain["pair"] == (w2_dram is not None)
        nbuf = chain["nbuf"]
    wf, wb = chain["wf"], chain["wb"]
    wf2 = chain.get("wf2")
    wb2 = chain.get("wb2")
    NT = -(-n_tok // 512)
    base = chain["n"]
    ldt, pot, epit, pet = chain.setdefault("ldt", {}), chain.setdefault("pot", {}), {}, {}

    def issue_load(i):
        gi = base + i
        if gi >= nbuf:
            b.wait(b.sy, pot[gi - nbuf])
        t = b.dma(b.sy, f"gw{chain['KC']}{'p' if chain['pair'] else ''}{gi % nbuf}",
                  wf[gi % nbuf][:, :, :], w_dram[i])
        if w2_dram is not None:
            t = b.dma(b.sy, f"gw{chain['KC']}{'p' if chain['pair'] else ''}{gi % nbuf}",
                      wf2[gi % nbuf][:, :, :], w2_dram[i])
        ldt[gi] = t

    def issue_cast(i):
        gi = base + i
        b.wait(b.po, ldt[gi])
        if gi >= nbuf:
            b.wait(b.po, chain.setdefault("pet", {})[gi - nbuf])
        c = b.po.tensor_copy(wb[gi % nbuf][:, :, :], wf[gi % nbuf][:, :, :])
        if w2_dram is not None:
            c = b.po.tensor_copy(wb2[gi % nbuf][:, :, :], wf2[gi % nbuf][:, :, :])
        pot[gi] = b.inc("po", c)

    for i in range(min(nbuf, M)):
        issue_load(i)
        issue_cast(i)
    for i in range(M):
        gi = base + i
        if i + nbuf < M:
            issue_load(i + nbuf)
        b.wait(b.pe, pot[gi])
        if i == 0 and rhs_tok is not None:
            b.wait(b.pe, rhs_tok)
        if w2_dram is not None:
            ps, ps2 = b.pA, b.pB
            if i == 0:
                for tkn in (b.psum_tok["A"], b.psum_tok["B"]):
                    b.wait(b.pe, tkn)
            elif epit.get(i - 1) is not None:
                b.wait(b.pe, epit[i - 1])
        else:
            ps, ps2 = (b.pA, None) if i % 2 == 0 else (b.pB, None)
            if i == 0:
                b.wait(b.pe, b.psum_tok["A"])
            elif i == 1:
                b.wait(b.pe, b.psum_tok["B"])
            elif epit.get(i - 2) is not None:
                b.wait(b.pe, epit[i - 2])
        last = None
        for nt in range(NT):
            t0, t1 = nt * 512, min((nt + 1) * 512, n_tok)
            for kc in range(KC):
                last = b.mm(ps[:, t0:t1], wb[gi % nbuf][:, kc, :],
                            rhs_fn(kc, t0, t1),
                            start=(kc == 0), stop=(kc == KC - 1))
            if w2_dram is not None:
                for kc in range(KC):
                    last = b.mm(ps2[:, t0:t1], wb2[gi % nbuf][:, kc, :],
                                rhs_fn(kc, t0, t1),
                                start=(kc == 0), stop=(kc == KC - 1))
        pet[i] = chain.setdefault("pet", {})
        pet[i] = b.pe_tok(last)
        chain["pet"][gi] = pet[i]
        if i + nbuf < M:
            issue_cast(i + nbuf)
        tk = None
        for nt in range(NT):
            t0, t1 = nt * 512, min((nt + 1) * 512, n_tok)
            r = epi_fn(i, t0, t1, ps[:, t0:t1],
                       None if ps2 is None else ps2[:, t0:t1], pet[i])
            if r is not None:
                tk = r
        epit[i] = tk
    chain["n"] = base + M
    lastA = lastB = None
    for i in range(M):
        if epit.get(i) is None:
            continue
        if w2_dram is not None or i % 2 == 0:
            lastA = epit[i]
        if w2_dram is not None or i % 2 == 1:
            lastB = epit[i]
    b.psum_tok["A"] = lastA if lastA is not None else b.psum_tok["A"]
    b.psum_tok["B"] = lastB if lastB is not None else b.psum_tok["B"]
    if own:
        b.barrier()
        b.release()
    return epit.get(M - 1)



def emit_gating(b, cons, L, xnorm, dwT, renorm):
    ncol = L["ngcol"]
    b.mark()
    gw = b.tile((128, 16, ncol))
    gwb = b.tile((128, 16, ncol), BF16)
    logits = b.tile((128, 8, ncol))
    p = b.tile((128, 8, 8))
    msk = b.tile((128, 8, 8))
    tmp = b.tile((128, 8, 8))
    red = b.tile((128, 8))
    dwtm = b.tile((128, 8, 8), BF16)
    sgtm = b.tile((128, 8), BF16) if L.get("nsig") else None
    tok = b.dma(b.sy, "misc0", gw[:, :, :], cons[L["gwT"]])
    b.wait(b.po, tok)
    tc_ = b.inc("po", b.po.tensor_copy(gwb[:, :, :], gw[:, :, :]))
    b.wait(b.pe, tc_)
    lastmm = None
    for tc in range(8):
        for kc in range(16):
            lastmm = b.mm(b.pD[:, tc * ncol:(tc + 1) * ncol],
                                 xnorm[:, kc, tc * 128:(tc + 1) * 128],
                                 gwb[:, kc, :], start=(kc == 0), stop=(kc == 15))
    tmm = b.pe_tok()
    b.wait(b.ve, tmm)
    b.ve.tensor_copy(logits[:, :, :],
                     b.pD[:, 0:8 * ncol].rearrange("p (a c) -> p a c", a=8))
    b.ve.drain()
    lg8 = logits[:, :, 0:8]
    b.ve.tensor_reduce(red[:, :], lg8, axis=AX.X, op=ALU.max)
    b.ve.drain()
    for tc in range(8):
        b.ve.tensor_scalar(p[:, tc, :], lg8[:, tc, :], red[:, tc:tc + 1], None,
                           ALU.subtract)
    b.ve.drain()
    tv = b.inc("ve", b.ve.nop(nofuse=True))
    b.wait(b.se, tv)
    ts = b.inc("se", b.se.activation(p[:, :, :], p[:, :, :], AF.Exp))
    b.wait(b.ve, ts)
    b.ve.tensor_reduce(red[:, :], p[:, :, :], axis=AX.X, op=ALU.add)
    b.ve.drain()
    b.ve.reciprocal(red[:, :], red[:, :])
    b.ve.drain()
    for tc in range(8):
        b.ve.tensor_scalar(p[:, tc, :], p[:, tc, :], red[:, tc:tc + 1], None,
                           ALU.mult)
    b.ve.drain()
    if L.get("gbrep"):
        gb = b.tile((128, 8))
        tok = b.dma(b.sy, "misc1", gb[:, :], cons[L["gbrep"]])
        b.wait(b.ve, tok)
        bview = gb[:, :].unsqueeze(1).broadcast_to([128, 8, 8])
        b.ve.tensor_tensor(tmp[:, :, :], p[:, :, :], bview, ALU.add)
    else:
        b.ve.tensor_copy(tmp[:, :, :], p[:, :, :])
    b.ve.drain()
    b.ve.tensor_reduce(red[:, :], tmp[:, :, :], axis=AX.X, op=ALU.max)
    b.ve.drain()
    for tc in range(8):
        b.ve.tensor_scalar(msk[:, tc, :], tmp[:, tc, :], red[:, tc:tc + 1], None,
                           ALU.is_equal)
    b.ve.drain()
    b.ve.scalar_tensor_tensor(tmp[:, :, :], msk[:, :, :], -1e30, tmp[:, :, :],
                              ALU.mult, ALU.add)
    b.ve.drain()
    b.ve.tensor_reduce(red[:, :], tmp[:, :, :], axis=AX.X, op=ALU.max)
    b.ve.drain()
    for tc in range(8):
        b.ve.tensor_scalar(tmp[:, tc, :], tmp[:, tc, :], red[:, tc:tc + 1], None,
                           ALU.is_equal)
    b.ve.drain()
    b.ve.tensor_tensor(msk[:, :, :], msk[:, :, :], tmp[:, :, :], ALU.add)
    b.ve.drain()
    b.ve.tensor_tensor(p[:, :, :], p[:, :, :], msk[:, :, :], ALU.mult)
    b.ve.drain()
    if renorm:
        b.ve.tensor_reduce(red[:, :], p[:, :, :], axis=AX.X, op=ALU.add)
        b.ve.drain()
        b.ve.reciprocal(red[:, :], red[:, :])
        b.ve.drain()
        for tc in range(8):
            b.ve.tensor_scalar(p[:, tc, :], p[:, tc, :], red[:, tc:tc + 1], None,
                               ALU.mult)
        b.ve.drain()
    tv = b.inc("ve", b.ve.tensor_copy(dwtm[:, :, :], p[:, :, :]))
    tv2 = None
    if sgtm is not None:
        b.wait(b.se, tmm)
        b.se.drain()
        tv2 = b.inc("se", b.se.activation(sgtm[:, :], logits[:, :, 8], AF.Sigmoid))
    b.wait(b.pe, tv)
    for tc in range(8):
        nrow = 8 + (1 if sgtm is not None else 0)
        if sgtm is not None:
            src = b.tile((128, 9), BF16)
            b.wait(b.ve, tv2)
            b.ve.tensor_copy(src[:, 0:8], dwtm[:, tc, :])
            b.ve.drain()
            tvv = b.inc("ve", b.ve.tensor_copy(src[:, 8:9], sgtm[:, tc:tc + 1]))
            b.wait(b.pe, tvv)
            b.tr(b.pE.bitcast(BF16)[0:nrow, tc * 128:(tc + 1) * 128],
                           src[:, :], cons["ident_b"][:, :])
        else:
            b.tr(b.pE.bitcast(BF16)[0:8, tc * 128:(tc + 1) * 128],
                           dwtm[:, tc, :], cons["ident_b"][:, :])
    tpe = b.pe_tok()
    b.wait(b.ve, tpe)
    b.inc("ve", b.ve.tensor_copy(
        dwT[:, :], b.pE.bitcast(BF16)[0:dwT.shape[0], 0:1024]))
    b.barrier()
    b.release()


def emit_bcast_rows(b, cons, rowsrc, nrows, out_bf, srow=0):
    for e in range(nrows):
        ei = min(srow + e, 8)
        nr = rowsrc.shape[0]
        last = None
        for half in range(2):
            last = b.mm(b.pC[:, half * 512:(half + 1) * 512],
                               cons["i8e_b"][ei][0:nr, :],
                               rowsrc[0:nr, half * 512:(half + 1) * 512],
                               start=True, stop=True)
        t = b.pe_tok()
        b.wait(b.ve, t)
        b.inc("ve", b.ve.tensor_copy(out_bf[:, e, :], b.pC[:, 0:1024]))
        b.ve.drain()
        b.wait(b.pe, b.tok("ve"))



def emit_ffn(b, cons, L, x_in, x_out, dram):
    b.mark()
    xnorm = b.tile((128, 16, T), BF16)
    y_acc = b.tile((128, 16, T))
    emit_ln(b, cons, x_in, xnorm, L["eps_ln"])

    nsig = L.get("nsig", 0)
    sigb = b.tile((128, 1, 1024), BF16) if nsig else None

    for m in range(16):
        b.po.memset(y_acc[:, m, :], 0.0)
    b.po.drain()
    b.inc("po", b.po.nop(nofuse=True))

    def rhs(kc, t0, t1):
        return xnorm[:, kc, t0:t1]

    b.mark()
    dwT = b.tile((8 + nsig, 1024), BF16)
    emit_gating(b, cons, L, xnorm, dwT, renorm=L["renorm"])
    dwb = b.tile((128, 8, 1024), BF16)
    emit_bcast_rows(b, cons, dwT, 8, dwb)
    if nsig:
        emit_bcast_rows(b, cons, dwT, 1, sigb, srow=8)
    b.barrier()
    upset = mk_gemm_bufs(b, 16, L["wgt"] is not None)
    dnset = mk_gemm_bufs(b, L["EI"] // 128, False)
    nhc = L["EI"] // 128
    hbuf = b.tile((128, nhc, T), BF16)
    tmpb = [b.tile((128, 512), BF16) for _ in range(2)]
    gu = b.tile((128, L["MI"], T), BF16) if L["wgt"] is None else None
    cell = {"prev_ve": None, "slot": 0}

    for e in range(L["E"]):
        if L["wgt"] is not None:
            gt_d = cons[L["wgt"]][e]
            up_d = cons[L["wup"]][e]

            def epi_h(m, t0, t1, ps, ps2, tok, e=e):
                sl = cell["slot"] % 2
                cell["slot"] += 1
                b.wait(b.se, tok)
                if m == 0 and t0 == 0 and cell.get("dn_pe") is not None:
                    b.wait(b.se, cell["dn_pe"])
                    b.wait(b.ve, cell["dn_pe"])
                if cell["prev_ve"] is not None:
                    b.wait(b.se, cell["prev_ve"])
                b.se.drain()
                tsl = b.inc("se", b.se.activation(tmpb[sl][:, 0:t1 - t0], ps,
                                                  AF.Silu))
                b.wait(b.ve, tsl)
                b.ve.tensor_tensor(hbuf[:, m, t0:t1], tmpb[sl][:, 0:t1 - t0],
                                   ps2, ALU.mult)
                cell["prev_ve"] = b.inc("ve", b.ve.tensor_tensor(
                    hbuf[:, m, t0:t1], hbuf[:, m, t0:t1],
                    dwb[:, e, t0:t1], ALU.mult))
                return cell["prev_ve"]

            emit_gemm(b, gt_d, L["MI"], 16, rhs, epi_h, w2_dram=up_d,
                      chain=upset)
        else:

            def epi_gu(m, t0, t1, ps, ps2, tok):
                b.wait(b.se, tok)
                b.se.drain()
                b.inc("se", b.se.copy(gu[:, m, t0:t1], ps))
                return b.tok("se")

            emit_gemm(b, cons[L["wgup"]][e], L["MI"], 16, rhs, epi_gu,
                      chain=upset)
            half = L["MI"] // 2
            if cell.get("dn_pe") is not None:
                b.wait(b.ve, cell["dn_pe"])
                b.wait(b.se, cell["dn_pe"])
            for mi in range(half):
                for nt in range(2):
                    t0, t1 = nt * 512, (nt + 1) * 512
                    sl = cell["slot"] % 2
                    cell["slot"] += 1
                    if cell["prev_ve"] is not None:
                        b.wait(b.se, cell["prev_ve"])
                    b.se.drain()
                    tsl = b.inc("se", b.se.activation(
                        tmpb[sl][:, 0:512], gu[:, mi, t0:t1], AF.Silu))
                    b.wait(b.ve, tsl)
                    b.ve.tensor_tensor(hbuf[:, mi, t0:t1], tmpb[sl][:, 0:512],
                                       gu[:, half + mi, t0:t1], ALU.mult)
                    cell["prev_ve"] = b.inc("ve", b.ve.tensor_tensor(
                        hbuf[:, mi, t0:t1], hbuf[:, mi, t0:t1],
                        dwb[:, e, t0:t1], ALU.mult))

        def rhs_h(kc, t0, t1):
            return hbuf[:, kc, t0:t1]

        def epi_y(m, t0, t1, ps, ps2, tok):
            b.wait(b.ve, tok)
            return b.inc("ve", b.ve.tensor_tensor(
                y_acc[:, m, t0:t1], y_acc[:, m, t0:t1], ps, ALU.add))

        emit_gemm(b, cons[L["wdn"]][e], 16, nhc, rhs_h, epi_y,
                  chain=dnset, rhs_tok=cell["prev_ve"])
        cell["dn_pe"] = b.tok("pe")
    b.barrier()
    b.release()

    b.mark()
    msh = L["MSH"]
    nsplit = L["sh_split"]
    hsh = b.tile((128, msh, T // nsplit), BF16)
    tmpb2 = [b.tile((128, 512), BF16) for _ in range(2)]
    cell2 = {"prev_ve": None, "slot": 0}
    for sp in range(nsplit):
        tt0 = sp * (T // nsplit)

        def rhs_s(kc, t0, t1, tt0=tt0):
            return xnorm[:, kc, tt0 + t0:tt0 + t1]

        def epi_hs(m, t0, t1, ps, ps2, tok):
            sl = cell2["slot"] % 2
            cell2["slot"] += 1
            b.wait(b.se, tok)
            if cell2["prev_ve"] is not None:
                b.wait(b.se, cell2["prev_ve"])
            b.se.drain()
            tsl = b.inc("se", b.se.activation(tmpb2[sl][:, 0:t1 - t0], ps,
                                              AF.Silu))
            b.wait(b.ve, tsl)
            cell2["prev_ve"] = b.inc("ve", b.ve.tensor_tensor(
                hsh[:, m, t0:t1], tmpb2[sl][:, 0:t1 - t0], ps2, ALU.mult))
            return cell2["prev_ve"]

        emit_gemm(b, cons[L["wsg"]], msh, 16, rhs_s, epi_hs,
                  w2_dram=cons[L["wsu"]], n_tok=T // nsplit)

        def rhs_hs(kc, t0, t1):
            return hsh[:, kc, t0:t1]

        def epi_ys(m, t0, t1, ps, ps2, tok, tt0=tt0):
            b.wait(b.ve, tok)
            if nsig:
                b.ve.tensor_tensor(ps, ps, sigb[:, 0, tt0 + t0:tt0 + t1], ALU.mult)
                b.ve.drain()
            return b.inc("ve", b.ve.tensor_tensor(
                y_acc[:, m, tt0 + t0:tt0 + t1],
                y_acc[:, m, tt0 + t0:tt0 + t1], ps, ALU.add))

        emit_gemm(b, cons[L["wsd"]], 16, msh, rhs_hs, epi_ys, n_tok=T // nsplit)
    b.release()

    b.mark()
    x_ld = [b.tile((128, T)) for _ in range(2)]
    res = [b.tile((128, T)) for _ in range(2)]
    vt = {}
    stok = {}
    for m in range(16):
        if m >= 2:
            b.wait(b.sy, vt[m - 2])
        tok = b.dma(b.sy, f"lnx{m % 2}", x_ld[m % 2][:, :], x_in[m])
        b.wait(b.ve, tok)
        if m >= 2:
            b.wait(b.ve, stok[m - 2])
        vt[m] = b.inc("ve", b.ve.tensor_tensor(res[m % 2][:, :], y_acc[:, m, :],
                                               x_ld[m % 2][:, :], ALU.add))
        b.wait(b.po, vt[m])
        stok[m] = b.dma(b.po, f"st{m % 2}", x_out[m], res[m % 2][:, :])
    b.barrier()
    b.release()
    b.release()



def emit_attn(b, cons, L, x_in, x_out, dram):
    import os
    KDBG = int(os.environ.get("KDBG", "0"))
    nc_x = L["nch_xbc"]
    NH, P, N, G = 64, 64, L["N"], L["G"]
    NCH = 8
    b.mark()
    region = b.alloc(64 * 1024)
    DBG["region"] = region
    xnorm = b.view(region, (128, 16, T), BF16)
    y_buf = b.view(region, (128, 32, T), BF16)
    dt_raw = b.view(region + 32 * 1024, (64, T))
    dt_hm = b.view(region + 36 * 1024, (64, T))
    lndt_hm = b.view(region + 40 * 1024, (64, T))
    adt_hm = b.view(region + 44 * 1024, (64, T))
    nch_bc = nc_x - 32
    bc = b.tile((128, nch_bc, T), BF16)
    c_all = b.tile((64, T), BF16) if L["l0"] else None
    tail_st = b.tile((128, 256))
    gtail_lo = b.tile((128, 256))
    gtail_hi = b.tile((128, 256))
    diag = b.tile((128, 32, 128), BF16)
    colmisc = b.tile((64, 4))

    emit_ln(b, cons, x_in, xnorm, "eps5")

    tok = b.dma(b.sy, "misc0", colmisc[:, 0:1], cons[L["dtb"]][:, :])
    tok2 = b.dma(b.sy, "misc1", colmisc[:, 1:2], cons[L["alog"]][:, :])
    b.wait(b.se, tok2)
    b.se.drain()
    ts = b.inc("se", b.se.activation(colmisc[:, 1:2], colmisc[:, 1:2], AF.Exp))
    b.wait(b.ve, ts)
    b.ve.tensor_scalar(colmisc[:, 1:2], colmisc[:, 1:2], -1.0, None, ALU.mult)
    b.ve.memset(colmisc[:, 2:3], 1.0)
    b.ve.drain()
    b.inc("ve", b.ve.nop(nofuse=True))
    b.po.memset(tail_st[:, :], 0.0)
    b.po.drain()
    b.inc("po", b.po.nop(nofuse=True))
    dp = b.tile((128, 32))
    tok = b.dma(b.sy, "misc0", dp[:, :], cons[L["dpair"]][:, :])
    b.wait(b.ve, tok)
    for j in range(32):
        b.ve.tensor_scalar(diag[:, j, :], cons["ident_b"][:, :], dp[:, j:j + 1],
                           None, ALU.mult)
    b.ve.drain()
    b.inc("ve", b.ve.nop(nofuse=True))
    b.barrier()

    b.mark()
    z_stage = [b.tile((128, T), BF16) for _ in range(2)]
    x_stage = [b.tile((128, T), BF16) for _ in range(2)]
    st_z = {}
    st_x = {}

    def epi_in(m, t0, t1, ps, ps2, tok):
        if m < 32:
            sl = m % 2
            b.wait(b.se, tok)
            if m >= 2:
                b.wait(b.se, st_z[m - 2])
            b.se.drain()
            tsl = b.inc("se", b.se.activation(z_stage[sl][:, t0:t1], ps, AF.Silu))
            if t1 == T:
                b.wait(b.po, tsl)
                st_z[m] = b.dma(b.po, f"st{sl}", dram["z_spill"][m],
                                z_stage[sl][:, :])
            return tsl
        if m < 32 + nc_x:
            mx = m - 32
            sl = m % 2
            b.wait(b.se, tok)
            if m >= 34:
                b.wait(b.se, st_x[m - 2])
            b.se.drain()
            tsl = b.inc("se", b.se.copy(x_stage[sl][:, t0:t1], ps))
            b.wait(b.ve, tsl)
            if t0 == 0:
                b.ve.tensor_copy(tail_st[:, mx * 2:mx * 2 + 2], ps[:, 0:2])
            if t1 == T:
                b.ve.tensor_copy(tail_st[:, nc_x * 2 + mx * 3:nc_x * 2 + mx * 3 + 3],
                                 ps[:, 509:512])
            b.ve.drain()
            tv = b.inc("ve", b.ve.nop(nofuse=True))
            if t1 == T:
                b.wait(b.po, tv)
                st_x[m] = b.dma(b.po, f"st{2 + sl}", dram["xbc_spill"][mx],
                                x_stage[sl][:, :])
            return tv
        b.wait(b.se, tok)
        b.se.drain()
        return b.inc("se", b.se.copy(dt_raw[0:64, t0:t1], ps[0:64, :]))

    def rhs_in(kc, t0, t1):
        return xnorm[:, kc, t0:t1]

    emit_gemm(b, cons[L["win"]], L["M_in"], 16, rhs_in, epi_in)

    cin_t, cout_t = dram["cin_tail"], dram["cout_tail"]
    b.wait(b.po, b.tok("ve"))
    tst = b.dma(b.po, "st0", cin_t.ap(), tail_st[:, :])
    b.wait(b.po, tst)
    tcc = b.inc("cc", b.po.collective_compute(
        "AllGather", ALU.bypass, replica_groups=[[0, 1], [2, 3], [4, 5], [6, 7]],
        ins=[cin_t.ap().opt()], outs=[cout_t.ap().opt()]))
    b.wait(b.po, tcc)
    tg1 = b.dma(b.po, "st0", gtail_lo[:, :], cout_t.ap()[0:128, :])
    tg2 = b.dma(b.po, "st1", gtail_hi[:, :], cout_t.ap()[128:256, :])
    b.barrier()
    b.release()

    b.mark()
    cw = b.tile((128, nc_x, 4))
    cb = b.tile((128, nc_x))
    cbuf = [b.tile((128, 1032), BF16) for _ in range(3)]
    ctmp = [b.tile((128, T)) for _ in range(2)]
    hs_stage = [b.tile((128, T), BF16) for _ in range(2)]
    tok = b.dma(b.sy, "misc0", cw[:, :, :], cons[L["convw"]][:, :, :])
    tok2 = b.dma(b.sy, "misc1", cb[:, :], cons[L["convb"]][:, :])
    b.wait(b.ve, tok)
    b.wait(b.ve, tok2)
    pl = L["pad_left"]
    rdt = {}
    vet = {}
    set_ = {}
    stt = {}
    for m in range(nc_x):
        sl = m % 3
        if m >= 3:
            b.wait(b.sy, vet[m - 3])
        rdt[m] = b.dma(b.sy, f"cv{sl}", cbuf[sl][:, 3:1027], dram["xbc_spill"][m])
        b.wait(b.ve, rdt[m])
        if m >= 1:
            b.wait(b.ve, set_[m - 1])
        b.ve.tensor_scalar(ctmp[1][:, 0:3],
                           gtail_lo[:, nc_x * 2 + m * 3:nc_x * 2 + m * 3 + 3],
                           cons["flag_hi"][:, 0:1], None, ALU.mult)
        b.ve.drain()
        b.ve.tensor_copy(cbuf[sl][:, 0:3], ctmp[1][:, 0:3])
        if L["pad_left"] == 1:
            b.ve.tensor_scalar(ctmp[1][:, 4:6],
                               gtail_hi[:, m * 2:m * 2 + 2],
                               cons["flag_lo"][:, 0:1], None, ALU.mult)
            b.ve.drain()
            b.ve.tensor_copy(cbuf[sl][:, 1027:1029], ctmp[1][:, 4:6])
        else:
            b.ve.memset(cbuf[sl][:, 1027:1029], 0.0)
        b.ve.drain()
        off0 = 3 - pl
        b.ve.tensor_scalar(ctmp[0][:, :], cbuf[sl][:, off0:off0 + 1024],
                           cw[:, m, 0:1], None, ALU.mult)
        lastv = None
        for k in range(1, 4):
            lastv = b.ve.scalar_tensor_tensor(ctmp[0][:, :],
                                              cbuf[sl][:, off0 + k:off0 + k + 1024],
                                              cw[:, m, k:k + 1], ctmp[0][:, :],
                                              ALU.mult, ALU.add)
        vet[m] = b.inc("ve", lastv)
        if m >= 1:
            b.wait(b.pe, vet[m - 1])
        b.pe_warm(cons, 28)
        b.wait(b.se, vet[m])
        if m < 32:
            hsl = m % 2
            if m >= 2:
                b.wait(b.se, stt[m - 2])
            b.se.drain()
            tse = b.inc("se", b.se.activation(hs_stage[hsl][:, :], ctmp[0][:, :],
                                              AF.Silu, bias=cb[:, m:m + 1]))
            b.wait(b.po, tse)
            stt[m] = b.dma(b.po, f"st{hsl}", dram["hs_spill"][m],
                           hs_stage[hsl][:, :])
            set_[m] = tse
        else:
            b.se.drain()
            set_[m] = b.inc("se", b.se.activation(bc[:, m - 32, :], ctmp[0][:, :],
                                                  AF.Silu, bias=cb[:, m:m + 1]))
    b.barrier()
    if L["l0"]:
        tcp = b.dma(b.po, "st0", c_all[:, :], bc[64:128, 0, :])
        b.barrier()
    b.release()

    b.ve.tensor_scalar(dt_raw[0:64, :], dt_raw[0:64, :], colmisc[0:64, 0:1],
                       None, ALU.add)
    b.ve.drain()
    tv = b.inc("ve", b.ve.nop(nofuse=True))
    b.wait(b.se, tv)
    b.se.activation(lndt_hm[0:64, :], dt_raw[0:64, :], AF.Abs)
    b.se.drain()
    b.se.activation(lndt_hm[0:64, :], lndt_hm[0:64, :], AF.Exp, scale=-1.0)
    b.se.drain()
    ts = b.inc("se", b.se.activation(lndt_hm[0:64, :], lndt_hm[0:64, :], AF.Ln,
                                     bias=colmisc[0:64, 2:3]))
    b.wait(b.ve, ts)
    b.ve.tensor_scalar(dt_hm[0:64, :], dt_raw[0:64, :], 0.0, None, ALU.max)
    b.ve.drain()
    b.ve.tensor_tensor(dt_hm[0:64, :], dt_hm[0:64, :], lndt_hm[0:64, :], ALU.add)
    b.ve.drain()
    b.ve.tensor_scalar(dt_hm[0:64, :], dt_hm[0:64, :], 1e-30, None, ALU.max)
    b.ve.drain()
    b.ve.tensor_scalar(adt_hm[0:64, :], dt_hm[0:64, :], colmisc[0:64, 1:2],
                       None, ALU.mult)
    b.ve.drain()
    tv = b.inc("ve", b.ve.nop(nofuse=True))
    b.wait(b.se, tv)
    b.se.drain()
    b.inc("se", b.se.activation(lndt_hm[0:64, :], dt_hm[0:64, :], AF.Ln))
    b.barrier()

    b.mark()
    St = b.tile((N, 64 * P))
    stbf = b.tile((N, 64 * P), BF16)
    x_tm = b.tile((128, 32, 128), BF16)
    hswin = [b.tile((128, 32, 128), BF16) for _ in range(3)]
    b_tm = b.tile((128, G, L["N"]), BF16)
    acum = b.tile((128, 8, 64))
    vtile = b.tile((128, 8, 64))
    dtdec = b.tile((128, 8, 64))
    lamrep = b.tile((128, 8, 64))
    expacum = b.tile((128, 8, 64), BF16)
    dt_tm = b.tile((128, 64))
    scr = b.tile((128, 64))
    lam_raw = b.tile((64, 1))
    lam_rowt = b.tile((1, 64))
    xd2 = [b.tile((128, 512), BF16) for _ in range(2)]
    xd_st = xd2[0]
    cbt = b.tile((128, G, 128), BF16)
    mt2 = [b.tile((128, 1024), BF16) for _ in range(2)]
    e1 = b.tile((128, 1024))
    ea_s = b.tile((128, 512))
    t1t = b.tile((128, 512))
    t2t = t1t.rearrange("p (a c) -> p a c", a=4)
    zbuf = [b.tile((128, 4, 128), BF16) for _ in range(2)]

    dbg_scr = b.tile((128, 512)) if KDBG else None

    def dbg_dump(ap, idx, np_=128):
        nfree = 1
        for d in ap.shape[1:]:
            nfree *= d
        b.barrier()
        b.ve.tensor_copy(dbg_scr[0:np_, 0:nfree], ap)
        b.ve.drain()
        b.inc("ve", b.ve.nop(nofuse=True))
        b.barrier()
        b.dma(b.po, "st0", dram["dbg"][idx][0:np_, 0:nfree],
              dbg_scr[0:np_, 0:nfree])
        b.barrier()

    b.po.memset(St[:, :], 0.0)
    b.po.drain()
    b.inc("po", b.po.nop(nofuse=True))
    b.barrier()

    def chunk_prep(c):
        cs = slice(c * 128, (c + 1) * 128)
        pp = b.pE
        b.tr(pp[:, 0:64], adt_hm[0:64, cs], cons["ident_f"][0:64, 0:64])
        b.tr(pp[:, 64:128], dt_hm[0:64, cs], cons["ident_f"][0:64, 0:64])
        b.tr(pp[:, 128:192], lndt_hm[0:64, cs],
                       cons["ident_f"][0:64, 0:64])
        t = b.pe_tok()
        b.wait(b.se, t)
        b.se.copy(scr[:, :], pp[:, 0:64])
        b.se.copy(dt_tm[:, :], pp[:, 64:128])
        b.se.drain()
        ts = b.inc("se", b.se.copy(vtile[:, c, :], pp[:, 128:192]))
        b.wait(b.pe, ts)
        b.mm(pp[:, 192:256], cons["triu_f"][:, :], scr[:, :],
                    start=True, stop=True)
        t = b.pe_tok()
        b.wait(b.se, t)
        ts = b.inc("se", b.se.copy(acum[:, c, :], pp[:, 192:256]))
        b.wait(b.pe, ts)
        b.wait(b.sy, ts)
        trow = b.dma(b.sy, "lam0", lam_rowt[:, :], acum[127:128, c, :])
        b.wait(b.pe, trow)
        b.tr(pp[0:64, 256:257], lam_rowt[0:1, :],
                       cons["ident_f"][0:1, 0:1])
        t = b.pe_tok()
        b.wait(b.se, t)
        ts = b.inc("se", b.se.copy(lam_raw[:, :], pp[0:64, 256:257]))
        b.wait(b.pe, ts)
        b.tr(pp[:, 320:384],
                       lam_raw[:, 0:1].broadcast_to([64, 128]),
                       cons["ident_f"][0:64, 0:64])
        t = b.pe_tok()
        b.wait(b.ve, t)
        b.ve.tensor_tensor(scr[:, :], pp[:, 320:384], acum[:, c, :], ALU.subtract)
        b.ve.drain()
        tv = b.inc("ve", b.ve.tensor_tensor(vtile[:, c, :], vtile[:, c, :],
                                            acum[:, c, :], ALU.subtract))
        b.wait(b.se, tv)
        b.se.activation(scr[:, :], scr[:, :], AF.Exp)
        b.se.drain()
        b.se.activation(lamrep[:, c, :], pp[:, 320:384], AF.Exp)
        b.se.drain()
        ts = b.inc("se", b.se.activation(expacum[:, c, :], acum[:, c, :], AF.Exp))
        b.wait(b.ve, ts)
        tv = b.inc("ve", b.ve.tensor_tensor(dtdec[:, c, :], scr[:, :],
                                            dt_tm[:, :], ALU.mult))
        return tv

    def load_hswin(c, sl):
        b.wait(b.sy, b.tok("pe"))
        b.wait(b.po, b.tok("pe"))
        src = dram["hs_spill"][0:32, :, c * 128:(c + 1) * 128] \
            .rearrange("a p c -> p a c")
        b.dma(b.sy, f"hw{sl}", hswin[sl][:, 0:16, :], src[:, 0:16, :])
        return b.dma(b.po, f"hw{sl}", hswin[sl][:, 16:32, :], src[:, 16:32, :])

    def x_transposes(c, sl, want_xd):
        for batch in range(8):
            pp = b.pD if batch % 2 == 0 else b.pE
            ppb = pp.bitcast(BF16)
            for f4 in range(4):
                f = batch * 4 + f4
                b.tr(ppb[:, f4 * 128:(f4 + 1) * 128],
                               hswin[sl][:, f, :], cons["ident_b"][:, :])
            t = b.pe_tok()
            b.wait(b.ve, t)
            tv = b.inc("ve", b.ve.tensor_copy(
                x_tm[:, batch * 4:batch * 4 + 4, :]
                .rearrange("p a c -> p (a c)"), ppb[:, 0:512]))
            b.wait(b.pe, tv)
        if L["l0"]:
            b.tr(b.pD.bitcast(BF16)[:, 0:64], bc[0:64, 0, c * 128:(c + 1) * 128],
                           cons["ident_b"][0:64, 0:64])
            t = b.pe_tok()
            b.wait(b.ve, t)
            b.ve.tensor_copy(b_tm[:, 0, :], b.pD.bitcast(BF16)[:, 0:64])
        else:
            for g in range(8):
                pp = b.pD if g % 2 == 0 else b.pE
                b.tr(pp.bitcast(BF16)[:, 0:128],
                               bc[:, g, c * 128:(c + 1) * 128], cons["ident_b"][:, :])
                t = b.pe_tok()
                b.wait(b.ve, t)
                tv = b.inc("ve", b.ve.tensor_copy(b_tm[:, g, :],
                                                  pp.bitcast(BF16)[:, 0:128]))
                b.wait(b.pe, tv)
        b.ve.drain()
        return b.inc("ve", b.ve.nop(nofuse=True))

    def lam_scale_st(c):
        for h in range(64):
            b.ve.tensor_scalar(St[:, h * P:(h + 1) * P], St[:, h * P:(h + 1) * P],
                               lamrep[0:N, c, h:h + 1], None, ALU.mult)
        b.ve.drain()
        return b.inc("ve", b.ve.nop(nofuse=True))

    def states_pass(c, sl, seed_scale):
        tlam = lam_scale_st(c)
        for g in range(8):
            bg = g if G == 8 else 0
            lastxd = None
            for hh in range(8):
                h = g * 8 + hh
                lastxd = b.ve.tensor_scalar(
                    xd_st[:, hh * 64:(hh + 1) * 64],
                    x_tm[:, h // 2, (h % 2) * 64:(h % 2) * 64 + 64],
                    dtdec[:, c, h:h + 1], None, ALU.mult)
            b.ve.drain()
            tv = b.inc("ve", b.ve.nop(nofuse=True))
            b.wait(b.pe, tv)
            half = g % 2
            for hh in range(8):
                b.mm(b.pC[0:N, half * 512 + hh * 64:half * 512 + hh * 64 + 64],
                            b_tm[:, bg, :], xd_st[:, hh * 64:(hh + 1) * 64],
                            start=True, stop=True)
            t = b.pe_tok()
            b.wait(b.ve, t)
            b.ve.tensor_tensor(St[:, g * 512:(g + 1) * 512], St[:, g * 512:(g + 1) * 512],
                               b.pC[0:N, half * 512:half * 512 + 512], ALU.add)
            b.ve.drain()
            tv = b.inc("ve", b.ve.nop(nofuse=True))
            b.wait(b.pe, tv)

    hw_toks = {0: load_hswin(0, 0), 1: load_hswin(1, 1)}
    for c in range(NCH):
        chunk_prep(c)
        b.wait(b.ve, hw_toks[c])
        b.wait(b.pe, hw_toks[c])
        if c + 2 < NCH:
            hw_toks[c + 2] = load_hswin(c + 2, (c + 2) % 3)
        x_transposes(c, c % 3, True)
        states_pass(c, c % 3, None)
    b.barrier()

    cin_s = dram["cin_st64"] if L["l0"] else dram["cin_st128"]
    cout_s = dram["cout_st64"] if L["l0"] else dram["cout_st128"]
    tst = b.dma(b.po, "st0", cin_s.ap()[0:N, 0:64 * P], St[:, :])
    b.wait(b.po, tst)
    tcc = b.inc("cc", b.po.collective_compute(
        "AllGather", ALU.bypass, replica_groups=[[0, 1], [2, 3], [4, 5], [6, 7]],
        ins=[cin_s.ap().opt()], outs=[cout_s.ap().opt()]))
    b.wait(b.po, tcc)
    tg = b.dma(b.po, "st0", St[:, :], cout_s.ap()[0:N, 0:64 * P])
    b.wait(b.ve, tg)
    for h in range(64):
        b.ve.tensor_scalar(St[:, h * P:(h + 1) * P], St[:, h * P:(h + 1) * P],
                           cons["flag_hi"][0:N, 0:1], None, ALU.mult)
    b.ve.drain()
    b.inc("ve", b.ve.nop(nofuse=True))
    b.barrier()

    zr = {}

    def load_z(c, g, sl):
        b.wait(b.sy, b.tok("ve"))
        src = dram["z_spill"][4 * g:4 * g + 4, :, c * 128:(c + 1) * 128] \
            .rearrange("a p c -> p a c")
        return b.dma(b.sy, f"zr{sl}", zbuf[sl][:, :, :], src)

    hw_toks = {0: load_hswin(0, 0), 1: load_hswin(1, 1)}
    zr[(0, 0)] = load_z(0, 0, 0)
    zslot = {(0, 0): 0}
    state = {"mt_tok": None, "ea_tok": None, "stadd_tok": None, "xd_tok": None,
             "e1_tok": None, "ydiag_tok": {}, "states_tok": {},
             "comb_tok2": {}}

    def prep_slot(c, g):
        cs = slice(c * 128, (c + 1) * 128)
        if state["e1_tok"] is not None:
            b.wait(b.pe, state["e1_tok"])
        for j8 in range(8):
            h = g * 8 + j8
            b.tr(b.pA[:, j8 * 128:(j8 + 1) * 128],
                 acum[:, c, h:h + 1].broadcast_to([128, 128]),
                 cons["ident_f"][:, :])
        t1tok = b.pe_tok()
        b.wait(b.ve, t1tok)
        vview = vtile[:, c, g * 8:(g + 1) * 8].unsqueeze(2) \
            .broadcast_to([128, 8, 128])
        b.ve.tensor_tensor(e1.rearrange("p (a d) -> p a d", a=8),
                           b.pA.rearrange("p (a d) -> p a d", a=8),
                           vview, ALU.add)
        state["e1_tok"] = b.inc("ve", b.ve.tensor_scalar(
            e1[:, :], e1[:, :], 30.0, None, ALU.min))
        b.wait(b.se, state["e1_tok"])
        b.se.drain()
        texp = b.inc("se", b.se.activation(e1[:, :], e1[:, :], AF.Exp))
        b.wait(b.ve, texp)
        par = (c * 8 + g) % 2
        if state["ydiag_tok"].get(par) is not None:
            b.wait(b.ve, state["ydiag_tok"][par])
        bg = g if G == 8 else 0
        cview = cbt[:, bg, :].unsqueeze(1).broadcast_to([128, 8, 128])
        state["mt_tok"] = b.inc("ve", b.ve.tensor_tensor(
            mt2[par].rearrange("p (a d) -> p a d", a=8),
            e1.rearrange("p (a d) -> p a d", a=8), cview, ALU.mult))
        if state["states_tok"].get(par) is not None:
            b.wait(b.ve, state["states_tok"][par])
        lastx = None
        for j8 in range(8):
            h = g * 8 + j8
            lastx = b.ve.tensor_scalar(xd2[par][:, j8 * 64:(j8 + 1) * 64],
                                       x_tm[:, h // 2, (h % 2) * 64:(h % 2) * 64 + 64],
                                       dtdec[:, c, h:h + 1], None, ALU.mult)
        state["xd_tok"] = b.inc("ve", lastx)

    def ywork_slot(c, g):
        cs = slice(c * 128, (c + 1) * 128)
        bg = g if G == 8 else 0
        par = (c * 8 + g) % 2
        pY = b.pB if par == 0 else b.pC
        b.wait(b.pe, state["mt_tok"])
        if state["comb_tok2"].get(par) is not None:
            b.wait(b.pe, state["comb_tok2"][par])
        mt8 = mt2[par]
        first = True
        for j8 in range(8):
            h = g * 8 + j8
            jp = j8 // 2
            b.mm(pY[(h % 2) * 64:(h % 2) * 64 + 64, jp * 128:(jp + 1) * 128],
                 x_tm[:, h // 2, (h % 2) * 64:(h % 2) * 64 + 64],
                 mt8[:, j8 * 128:(j8 + 1) * 128],
                 start=(j8 <= 1), stop=False, skip_group_check=True,
                 tile_position=(0, (h % 2) * 64))
        ydiag_tok = b.pe_tok()
        state["ydiag_tok"][par] = ydiag_tok
        for j in range(4):
            jj = 4 * g + j
            b.mm(pY[:, j * 128:(j + 1) * 128], diag[:, jj, :],
                 hswin[c % 3][:, jj, :], start=False, stop=(j == 3),
                 skip_group_check=True)
        for j8 in range(8):
            h = g * 8 + j8
            jp = j8 // 2
            if L["l0"]:
                crhs = c_all[:, cs]
                lhs = stbf[0:64, h * P:(h + 1) * P]
            else:
                crhs = bc[:, 8 + bg, cs]
                lhs = stbf[:, h * P:(h + 1) * P]
            b.mm(pY[(h % 2) * 64:(h % 2) * 64 + 64,
                    512 + jp * 128:512 + (jp + 1) * 128],
                 lhs, crhs, start=(j8 <= 1), stop=(j8 == 7),
                 skip_group_check=True, tile_position=(0, (h % 2) * 64))
        if state["ea_tok"] is not None:
            b.wait(b.pe, state["ea_tok"])
        pdb = b.pD.bitcast(BF16)
        for j in range(4):
            for parh in range(2):
                h = g * 8 + 2 * j + parh
                eview = expacum[:, c, h:h + 1].broadcast_to([128, 64])
                b.tr(pdb[parh * 64:parh * 64 + 64, j * 128:(j + 1) * 128],
                     eview, cons["ident_b"][:, :])
        b.wait(b.pe, state["xd_tok"])
        if state["stadd_tok"] is not None:
            b.wait(b.pe, state["stadd_tok"])
        for j8 in range(8):
            b.mm(b.pE[0:N, j8 * 64:(j8 + 1) * 64], b_tm[:, bg, :],
                 xd2[par][:, j8 * 64:(j8 + 1) * 64], start=True, stop=True)
        tall = b.pe_tok()
        state["states_tok"][par] = tall
        b.wait(b.se, tall)
        b.se.drain()
        tea = b.inc("se", b.se.copy(ea_s[:, :], pdb[:, 0:512]))
        state["ea_tok"] = tea
        b.wait(b.ve, tall)
        lasts = b.ve.tensor_tensor(St[:, g * 512:(g + 1) * 512],
                                   St[:, g * 512:(g + 1) * 512],
                                   b.pE[0:N, 0:512], ALU.add)
        if c + 1 < NCH:
            b.ve.drain()
            b.ve.tensor_copy(stbf[:, g * 512:(g + 1) * 512],
                             St[:, g * 512:(g + 1) * 512])
            b.ve.drain()
            for j8 in range(8):
                h = g * 8 + j8
                lasts = b.ve.tensor_scalar(St[:, h * P:(h + 1) * P],
                                           St[:, h * P:(h + 1) * P],
                                           lamrep[0:N, c + 1, h:h + 1], None,
                                           ALU.mult)
        state["stadd_tok"] = b.inc("ve", lasts)
        b.wait(b.ve, tea)
        b.ve.tensor_tensor(t1t[:, :], pY[:, 512:1024], ea_s[:, :], ALU.mult)
        b.ve.scalar_tensor_tensor(t1t[:, :], t1t[:, :], 0.0, pY[:, 0:512],
                                  ALU.add, ALU.add)
        b.wait(b.ve, zr[(c, g)])
        ng = (c, g + 1) if g < 7 else (c + 1, 0)
        if ng[0] < NCH:
            zsl = (zslot[(c, g)] + 1) % 2
            zslot[ng] = zsl
            zr[ng] = load_z(ng[0], ng[1], zsl)
        state["comb_tok2"][(c * 8 + g) % 2] = b.inc(
            "ve", b.ve.tensor_tensor(y_buf[:, 4 * g:4 * g + 4, cs],
                                     t1t.rearrange("p (a d) -> p a d", a=4),
                                     zbuf[zslot[(c, g)]][:, :, :], ALU.mult))

    for c in range(NCH):
        b.wait(b.ve, hw_toks[c])
        b.wait(b.pe, hw_toks[c])
        if c + 2 < NCH:
            hw_toks[c + 2] = load_hswin(c + 2, (c + 2) % 3)
        if state["ea_tok"] is not None:
            b.wait(b.pe, state["ea_tok"])
        if state["stadd_tok"] is not None:
            b.wait(b.pe, state["stadd_tok"])
        x_transposes(c, c % 3, False)
        cs = slice(c * 128, (c + 1) * 128)
        b.wait(b.pe, b.tok("ve"))
        ncb = 8 if G == 8 else 1
        for bg in range(ncb):
            pp = b.pD if bg < 4 else b.pE
            off = (bg % 4) * 128
            if L["l0"]:
                b.mm(pp[:, off:off + 128], bc[0:64, 0, cs], c_all[:, cs],
                     start=True, stop=True)
            else:
                b.mm(pp[:, off:off + 128], bc[:, bg, cs], bc[:, 8 + bg, cs],
                     start=True, stop=True)
        tcb = b.pe_tok()
        b.wait(b.ve, tcb)
        for bg in range(ncb):
            pp = b.pD if bg < 4 else b.pE
            off = (bg % 4) * 128
            b.ve.tensor_tensor(cbt[:, bg, :], pp[:, off:off + 128],
                               cons["triu_f"][:, :], ALU.mult)
        b.ve.drain()
        if c == 0:
            b.wait(b.ve, b.tok("pe"))
            b.inc("ve", b.ve.tensor_copy(stbf[:, :], St[:, :]))
            b.ve.drain()
            lam_scale_st(0)
        tbnd = b.tok("ve")
        b.wait(b.pe, tbnd)
        state["ea_tok"] = None
        state["stadd_tok"] = None
        for g in range(8):
            prep_slot(c, g)
            if g >= 1:
                ywork_slot(c, g - 1)
        ywork_slot(c, 7)
    b.barrier()

    normw = b.tile((128, 32))
    rstd = b.tile((128, T))
    sqb = b.tile((128, T), BF16)
    tok = b.dma(b.sy, "misc0", normw[:, :], cons[L["normw"]][:, :])
    pet = {}
    for f in range(32):
        b.wait(b.se, tok)
        if f >= 1:
            b.wait(b.se, pet[f - 1])
        b.se.drain()
        tsq = b.inc("se", b.se.activation(sqb[:, :], y_buf[:, f, :], AF.Square))
        b.wait(b.pe, tsq)
        b.mm(b.pA[:, 0:512], cons["onesb"][:, :], sqb[:, 0:512],
                    start=(f == 0), stop=(f == 31))
        b.mm(b.pA[:, 512:1024], cons["onesb"][:, :], sqb[:, 512:1024],
                    start=(f == 0), stop=(f == 31))
        pet[f] = b.pe_tok()
    b.wait(b.ve, pet[31])
    b.ve.tensor_scalar(rstd[:, :], b.pA[:, :], 1.0 / 4096.0, None, ALU.mult)
    b.ve.drain()
    tv = b.inc("ve", b.ve.nop(nofuse=True))
    b.wait(b.se, tv)
    b.se.activation(rstd[:, :], rstd[:, :], AF.Ln, bias=cons[L["eps_rms"]][:, 0:1])
    b.se.drain()
    ts = b.inc("se", b.se.activation(rstd[:, :], rstd[:, :], AF.Exp, scale=-0.5))
    b.wait(b.ve, ts)
    if KDBG == 2:
        dbg_dump(rstd[:, 0:512], 10)
    for f in range(32):
        b.ve.scalar_tensor_tensor(y_buf[:, f, :], y_buf[:, f, :],
                                  normw[:, f:f + 1], rstd[:, :],
                                  ALU.mult, ALU.mult)
    b.ve.drain()
    b.inc("ve", b.ve.nop(nofuse=True))
    if KDBG == 2:
        dbg_dump(y_buf[:, 0:4, 0:128], 11)
    b.barrier()
    b.release()

    b.mark()
    x_ld = [b.tile((128, T)) for _ in range(2)]
    res = [b.tile((128, T)) for _ in range(2)]
    vt = {}
    stok = {}

    def rhs_out(kc, t0, t1):
        return y_buf[:, kc, t0:t1]

    last_tv = {}

    def epi_out(m, t0, t1, ps, ps2, tok):
        if t0 == 0:
            if m >= 2:
                b.wait(b.sy, last_tv[m - 2])
            vt[m] = b.dma(b.sy, f"lnx{m % 2}", x_ld[m % 2][:, :], x_in[m])
        b.wait(b.ve, tok)
        b.wait(b.ve, vt[m])
        if m >= 2 and t0 == 0:
            b.wait(b.ve, stok[m - 2])
        tv = b.inc("ve", b.ve.tensor_tensor(res[m % 2][:, t0:t1], ps,
                                            x_ld[m % 2][:, t0:t1], ALU.add))
        if t1 == T:
            last_tv[m] = tv
            b.wait(b.po, tv)
            stok[m] = b.dma(b.po, f"st{m % 2}", x_out[m], res[m % 2][:, :])
        return tv

    emit_gemm(b, cons[L["wout"]], 16, 32, rhs_out, epi_out)
    b.release()
    b.release()


L0_ATTN = dict(l0=True, nch_xbc=33, M_in=66, N=64, G=1, pad_left=3,
               win="l0_win", wout="l0_wout", convw="l0_convw", convb="l0_convb",
               dtb="l0_dtb", alog="l0_alog", dpair="l0_dpair", normw="l0_normw",
               eps_rms="eps5")
L1_ATTN = dict(l0=False, nch_xbc=48, M_in=81, N=128, G=8, pad_left=1,
               win="l1_win", wout="l1_wout", convw="l1_convw", convb="l1_convb",
               dtb="l1_dtb", alog="l1_alog", dpair="l1_dpair", normw="l1_normw",
               eps_rms="eps6")



L0_FFN = dict(eps_ln="eps5", gwT="l0_gwT", gbrep="l0_gbrep", ngcol=8, nsig=0,
              renorm=False, E=8, EI=1024, MI=8, MSH=8, sh_split=1,
              wgt="l0_w1", wup="l0_w3", wgup=None, wdn="l0_w2",
              wsg="l0_ws1", wsu="l0_ws3", wsd="l0_ws2")
L1_FFN = dict(eps_ln="eps5", gwT="l1_gwT", gbrep=None, ngcol=9, nsig=1,
              renorm=True, E=8, EI=768, MI=12, MSH=32, sh_split=2,
              wgt=None, wup=None, wgup="l1_gup", wdn="l1_down",
              wsg="l1_sg", wsu="l1_su", wsd="l1_sd")

GLOBAL_KEYS = ["ident_f", "ident_b", "triu_f", "onesb", "i8e_b",
               "l0_win", "l0_wout", "l0_convw", "l0_convb", "l0_dtb", "l0_alog",
               "l0_dpair", "l0_normw",
               "l0_gwT", "l0_gbrep", "l0_w1", "l0_w3", "l0_w2", "l0_ws1",
               "l0_ws3", "l0_ws2",
               "l1_win", "l1_wout", "l1_convw", "l1_convb", "l1_dtb", "l1_alog",
               "l1_dpair", "l1_normw",
               "l1_gwT", "l1_gup", "l1_down", "l1_sg", "l1_su", "l1_sd"]


def _np_dtype_to_mybir(a):
    import ml_dtypes
    if a.dtype == np.float32:
        return F32
    if a.dtype == ml_dtypes.bfloat16:
        return BF16
    raise AssertionError(str(a.dtype))


def build_program(blocks, g):
    nc = bass.Bass("TRN2", target_bir_lowering=False)
    cons = {}
    for k in GLOBAL_KEYS:
        a = g[k]
        t = nc.dram_tensor(k, list(a.shape), _np_dtype_to_mybir(a),
                           kind="ExternalInput")
        cons[k] = t.ap()
    x0 = nc.dram_tensor("x0", [16, 128, T], F32, kind="ExternalInput").ap()
    y = nc.dram_tensor("y", [16, 128, T], F32, kind="ExternalOutput").ap()
    flag_hi = nc.dram_tensor("flag_hi", [128, 1], F32, kind="ExternalInput").ap()
    flag_lo = nc.dram_tensor("flag_lo", [128, 1], F32, kind="ExternalInput").ap()
    resA = nc.dram_tensor("resA", [16, 128, T], F32).ap()
    resB = nc.dram_tensor("resB", [16, 128, T], F32).ap()
    dram = dict(flag_hi=flag_hi, flag_lo=flag_lo,
                hs_spill=nc.dram_tensor("hs_spill", [32, 128, T], BF16).ap(),
                z_spill=nc.dram_tensor("z_spill", [32, 128, T], BF16).ap(),
                xbc_spill=nc.dram_tensor("xbc_spill", [48, 128, T], BF16).ap(),
                cin_tail=nc.dram_tensor("cin_tail", [128, 256], F32),
                cout_tail=nc.dram_tensor("cout_tail", [256, 256], F32),
                dbg=nc.dram_tensor("dbg", [16, 128, 512], F32),
                cin_st64=nc.dram_tensor("cin_st64", [64, 4096], F32),
                cout_st64=nc.dram_tensor("cout_st64", [128, 4096], F32),
                cin_st128=nc.dram_tensor("cin_st128", [128, 4096], F32),
                cout_st128=nc.dram_tensor("cout_st128", [256, 4096], F32))

    with ExitStack() as ctx:
        b = Builder(nc, ctx)
        blk = ctx.enter_context(nc.Block())
        e5 = b.tile((128, 1))
        e6 = b.tile((128, 1))
        b.po.memset(e5[:, :], 1e-5)
        b.po.memset(e6[:, :], 1e-6)
        b.po.drain()
        b.inc("po", b.po.nop(nofuse=True))
        b.barrier()
        cons["eps5"] = e5
        cons["eps6"] = e6

        for k, shape, dt in [("ident_f", (128, 128), F32),
                             ("ident_b", (128, 128), BF16),
                             ("triu_f", (128, 128), F32),
                             ("onesb", (128, 128), BF16)]:
            tile = b.tile(shape, dt)
            tok = b.inc("ld", b.sy.dma_start(out=tile[:, :], in_=cons[k][:, :]), 16)
            cons[k] = tile
        i8 = []
        for e in range(9):
            tile = b.tile((9, 128), BF16)
            b.inc("ld", b.sy.dma_start(out=tile[:, :], in_=cons["i8e_b"][e]), 16)
            i8.append(tile)
        cons["i8e_b"] = i8
        for nmf, src in [("flag_hi", flag_hi), ("flag_lo", flag_lo)]:
            tile = b.tile((128, 1))
            b.inc("ld", b.sy.dma_start(out=tile[:, :], in_=src[:, :]), 16)
            cons[nmf] = tile
        b.barrier()

        cur = x0
        nxt = [resA, resB]
        from_attn = globals().get("emit_attn")
        for i, name in enumerate(blocks):
            out = y if i == len(blocks) - 1 else nxt[i % 2]
            if name == "ffn0":
                emit_ffn(b, cons, L0_FFN, cur, out, dram)
            elif name == "ffn1":
                emit_ffn(b, cons, L1_FFN, cur, out, dram)
            elif name == "attn0":
                from_attn(b, cons, L0_ATTN, cur, out, dram)
            elif name == "attn1":
                from_attn(b, cons, L1_ATTN, cur, out, dram)
            cur = out
        b.wait(b.po, b.tok("gd"))
        b.wait(b.po, b.tok("ld"))
        print(f"[build] arena peak {b.peak/1024:.1f} KB/partition; "
              f"sems {b.cnt}")
    return nc



def run_blocks(inputs, blocks):
    g, percore = stage_inputs(inputs)
    nc = build_program(blocks, g)
    in_maps = []
    for c in range(NCORES):
        m = {k: g[k] for k in GLOBAL_KEYS}
        m.update(percore[c])
        in_maps.append(m)
    res = run_bass_kernel_spmd(nc, in_maps, core_ids=list(range(NCORES)))
    out = np.empty((4, 2048, 2048), np.float32)
    for c in range(NCORES):
        bb, h = c // 2, c % 2
        yc = np.asarray(res.results[c]["y"])
        out[bb, h * T:(h + 1) * T, :] = yc.reshape(2048, T).T
    return out


def kernel(**inputs):
    return run_blocks(inputs, ["attn0", "ffn0", "attn1", "ffn1"])


# version 17 (final)
# speedup vs baseline: 1.2412x; 1.0258x over previous
import numpy as np
from contextlib import ExitStack

import concourse.bass as bass
import concourse.mybir as mybir
from concourse.bass_utils import run_bass_kernel_spmd

F32 = mybir.dt.float32
BF16 = mybir.dt.bfloat16
AF = mybir.ActivationFunctionType
ALU = mybir.AluOpType
AX = mybir.AxisListType

T = 1024
NCORES = 8
DBG = {}



def _tile_w(w, transpose):
    w = np.asarray(w, np.float32)
    if transpose:
        w = np.ascontiguousarray(w.T)
    in_f, out_f = w.shape
    assert in_f % 128 == 0
    kc = in_f // 128
    m = (out_f + 127) // 128
    wp = np.zeros((in_f, m * 128), np.float32)
    wp[:, :out_f] = w
    out = wp.reshape(kc, 128, m, 128).transpose(2, 1, 0, 3)
    return np.ascontiguousarray(out)


def stage_inputs(inputs):
    import ml_dtypes
    g = {}
    g["ident_f"] = np.eye(128, dtype=np.float32)
    g["ident_b"] = np.eye(128, dtype=ml_dtypes.bfloat16)
    g["triu_f"] = np.triu(np.ones((128, 128), np.float32))
    g["onesb"] = np.ones((128, 128), dtype=ml_dtypes.bfloat16)
    i8e = np.zeros((9, 9, 128), dtype=ml_dtypes.bfloat16)
    for e in range(9):
        i8e[e, e, :] = 1.0
    g["i8e_b"] = i8e

    def dpair_of(dp):
        dp = np.asarray(dp, np.float32)
        out = np.zeros((128, 32), np.float32)
        out[0:64, :] = dp[0::2][None, :]
        out[64:128, :] = dp[1::2][None, :]
        return out

    g["l0_win"] = _tile_w(inputs["L0_attn_W_in"], True)
    g["l0_wout"] = _tile_w(inputs["L0_attn_W_out"], True)
    g["l0_convw"] = np.ascontiguousarray(
        np.asarray(inputs["L0_attn_conv_w"], np.float32).reshape(33, 128, 4)
        .transpose(1, 0, 2))
    g["l0_convb"] = np.ascontiguousarray(
        np.asarray(inputs["L0_attn_conv_b"], np.float32).reshape(33, 128)
        .transpose(1, 0))
    g["l0_dtb"] = np.asarray(inputs["L0_attn_dt_bias"], np.float32).reshape(64, 1)
    g["l0_alog"] = np.asarray(inputs["L0_attn_A_log"], np.float32).reshape(64, 1)
    g["l0_dpair"] = dpair_of(inputs["L0_attn_D_param"])
    g["l0_normw"] = np.ascontiguousarray(
        np.asarray(inputs["L0_attn_norm_w"], np.float32).reshape(32, 128)
        .transpose(1, 0))
    g["l0_gwT"] = np.ascontiguousarray(
        np.asarray(inputs["L0_ffn_gate_w"], np.float32).T.reshape(16, 128, 8)
        .transpose(1, 0, 2))
    g["l0_gbrep"] = np.ascontiguousarray(np.broadcast_to(
        np.asarray(inputs["L0_ffn_gate_b"], np.float32), (128, 8)))
    g["l0_w1"] = np.stack([_tile_w(inputs["L0_ffn_W1"][e], True) for e in range(8)])
    g["l0_w3"] = np.stack([_tile_w(inputs["L0_ffn_W3"][e], True) for e in range(8)])
    g["l0_w2"] = np.stack([_tile_w(inputs["L0_ffn_W2"][e], True) for e in range(8)])
    g["l0_ws1"] = _tile_w(inputs["L0_ffn_Ws1"], True)
    g["l0_ws3"] = _tile_w(inputs["L0_ffn_Ws3"], True)
    g["l0_ws2"] = _tile_w(inputs["L0_ffn_Ws2"], True)
    g["l1_win"] = _tile_w(inputs["L1_attn_in_proj_w"], False)
    g["l1_wout"] = _tile_w(inputs["L1_attn_out_proj_w"], False)
    cw1 = np.ascontiguousarray(np.asarray(inputs["L1_attn_conv_w"], np.float32)[:, 0, :].T)
    g["l1_convw"] = np.ascontiguousarray(cw1.reshape(48, 128, 4).transpose(1, 0, 2))
    g["l1_convb"] = np.ascontiguousarray(
        np.asarray(inputs["L1_attn_conv_b"], np.float32).reshape(48, 128)
        .transpose(1, 0))
    g["l1_dtb"] = np.asarray(inputs["L1_attn_dt_bias"], np.float32).reshape(64, 1)
    g["l1_alog"] = np.asarray(inputs["L1_attn_A_log"], np.float32).reshape(64, 1)
    g["l1_dpair"] = dpair_of(inputs["L1_attn_D"])
    g["l1_normw"] = np.ascontiguousarray(
        np.asarray(inputs["L1_attn_norm_w"], np.float32).reshape(32, 128)
        .transpose(1, 0))
    gw9 = np.concatenate([np.asarray(inputs["L1_ffn_gate_w"], np.float32),
                          np.asarray(inputs["L1_ffn_shared_expert_gate_w"], np.float32)],
                         axis=0)
    g["l1_gwT"] = np.ascontiguousarray(gw9.T.reshape(16, 128, 9).transpose(1, 0, 2))
    g["l1_gup"] = np.stack([_tile_w(inputs["L1_ffn_gate_up_proj"][e], True)
                            for e in range(8)])
    g["l1_down"] = np.stack([_tile_w(inputs["L1_ffn_down_proj"][e], True)
                             for e in range(8)])
    g["l1_sg"] = _tile_w(inputs["L1_ffn_shared_gate_w"], True)
    g["l1_su"] = _tile_w(inputs["L1_ffn_shared_up_w"], True)
    g["l1_sd"] = _tile_w(inputs["L1_ffn_shared_down_w"], True)

    u = np.asarray(inputs["L0_attn_u"], np.float32)
    percore = []
    for c in range(NCORES):
        bb, h = c // 2, c % 2
        sl = u[bb, h * T:(h + 1) * T, :]
        x0 = np.ascontiguousarray(sl.T).reshape(16, 128, T)
        percore.append({"x0": x0,
                        "flag_hi": np.full((128, 1), float(h), np.float32),
                        "flag_lo": np.full((128, 1), float(1 - h), np.float32)})
    return g, percore



class Tok:
    __slots__ = ("sem", "v")

    def __init__(self, sem, v):
        self.sem = sem
        self.v = v


class Builder:
    def __init__(self, nc, ctx):
        self.nc = nc
        self.ctx = ctx
        self.pe = nc.tensor
        self.ve = nc.vector
        self.se = nc.scalar
        self.po = nc.gpsimd
        self.sy = nc.sync
        self.sems = {}
        for name in ["ld", "gd", "pe", "ve", "se", "po", "cc", "sy"]:
            self.sems[name] = ctx.enter_context(nc.semaphore(name))
        self.cnt = {name: 0 for name in self.sems}
        self.pA = ctx.enter_context(nc.psum_tensor("pA", [128, 1024], F32))
        self.pB = ctx.enter_context(nc.psum_tensor("pB", [128, 1024], F32))
        self.pC = ctx.enter_context(nc.psum_tensor("pC", [128, 1024], F32))
        self.pD = ctx.enter_context(nc.psum_tensor("pD", [128, 512], F32))
        self.pE = ctx.enter_context(nc.psum_tensor("pE", [128, 512], F32))
        self.CAP = 206 * 1024
        self.arena = ctx.enter_context(
            nc.sbuf_tensor("arena", [128, self.CAP // 4], F32))
        self.off = 0
        self.marks = []
        self.peak = 0
        self.last_pe = None
        self.psum_tok = {"A": None, "B": None}

    def inc(self, name, inst, n=1):
        inst.then_inc(self.sems[name], n)
        self.cnt[name] += n
        return Tok(name, self.cnt[name])

    def dslot(self, name):
        if name not in self.sems:
            self.sems[name] = self.ctx.enter_context(self.nc.semaphore(name))
            self.cnt[name] = 0
        return name

    def dma(self, q, slot, out, in_):
        self.dslot(slot)
        return self.inc(slot, q.dma_start(out=out, in_=in_), 16)

    def tok(self, name):
        return Tok(name, self.cnt[name])

    def pe_tok(self, inst=None):
        if inst is None:
            inst = self.last_pe
        if inst is not None:
            return self.inc("pe", inst)
        return self.inc("pe", self.pe.drain())

    def mm(self, *a, **k):
        self.last_pe = self.pe.matmul(*a, **k)
        return self.last_pe

    def tr(self, *a, **k):
        self.last_pe = self.pe.transpose(*a, **k)
        return self.last_pe

    def pe_warm(self, cons, n, ps=None):
        if ps is None:
            ps = self.pA
        for _ in range(n):
            self.last_pe = self.pe.matmul(ps[:, 0:128], cons["onesb"][:, :],
                                          cons["ident_b"][:, :],
                                          start=True, stop=True)

    def wait(self, eng, tok):
        if tok is not None:
            eng.wait_ge(self.sems[tok.sem], tok.v)

    def barrier(self):
        b = self
        for name, c in list(b.cnt.items()):
            if name in ("pe", "ve", "se", "po", "cc", "sy", "gd") or c == 0:
                continue
            if name.startswith("st"):
                continue
            b.sy.wait_ge(b.sems[name], c)
        tsy = b.inc("sy", b.sy.nop(nofuse=True))
        for name, c in list(b.cnt.items()):
            if c and (name == "gd" or name.startswith("st")):
                b.po.wait_ge(b.sems[name], c)
        b.ve.drain()
        tv = b.inc("ve", b.ve.nop(nofuse=True))
        b.se.drain()
        ts = b.inc("se", b.se.nop(nofuse=True))
        b.po.drain()
        tp = b.inc("po", b.po.nop(nofuse=True))
        tt = b.inc("pe", b.pe.drain())
        for eng in [b.pe, b.ve, b.se, b.po, b.sy]:
            for t in [tv, ts, tp, tt, tsy]:
                b.wait(eng, t)

    def alloc(self, nbytes, align=64):
        nbytes = -(-nbytes // align) * align
        o = -(-self.off // align) * align
        self.off = o + nbytes
        self.peak = max(self.peak, self.off)
        assert self.off <= self.CAP, f"SBUF arena overflow: {self.off}"
        return o

    def mark(self):
        self.marks.append(self.off)

    def release(self):
        self.off = self.marks.pop()

    def view(self, off, shape, dtype=F32):
        p = shape[0]
        free = 1
        for s in shape[1:]:
            free *= s
        dsz = 2 if dtype == BF16 else 4
        assert off % 4 == 0
        ap = self.arena[0:p, off // 4:(off + free * dsz + 3) // 4]
        if dtype != F32:
            ap = ap.bitcast(dtype)
            ap = ap[:, 0:free]
        if len(shape) == 2:
            return ap
        if len(shape) == 3:
            return ap.rearrange("p (a c) -> p a c", a=shape[1])
        if len(shape) == 4:
            return ap.rearrange("p (a c d) -> p a c d", a=shape[1], c=shape[2])
        raise AssertionError

    def tile(self, shape, dtype=F32):
        dsz = 2 if dtype == BF16 else 4
        free = 1
        for s in shape[1:]:
            free *= s
        return self.view(self.alloc(free * dsz), shape, dtype)



def emit_ln(b, cons, x_dram, xnorm, eps_name):
    b.mark()
    x_ld = [b.tile((128, T)) for _ in range(2)]
    sq = b.tile((128, T), BF16)
    mrep = b.tile((128, T))
    rstd = b.tile((128, T))
    c0 = b.tile((128, T))
    casttok = {}
    pet = {}
    for m in range(16):
        buf = x_ld[m % 2]
        if m >= 2:
            b.wait(b.sy, casttok[m - 2])
        tok = b.dma(b.sy, f"lnx{m % 2}", buf[:, :], x_dram[m])
        b.wait(b.ve, tok)
        casttok[m] = b.inc("ve", b.ve.tensor_copy(xnorm[:, m, :], buf[:, :]))
        b.wait(b.se, casttok[m])
        if m >= 1:
            b.wait(b.se, pet[m - 1])
        tsq = b.inc("se", b.se.activation(sq[:, :], xnorm[:, m, :], AF.Square))
        b.wait(b.pe, casttok[m])
        b.mm(b.pA[:, 0:512], cons["onesb"][:, :], xnorm[:, m, 0:512],
                    start=(m == 0), stop=(m == 15))
        b.mm(b.pA[:, 512:1024], cons["onesb"][:, :], xnorm[:, m, 512:1024],
                    start=(m == 0), stop=(m == 15))
        b.wait(b.pe, tsq)
        b.mm(b.pB[:, 0:512], cons["onesb"][:, :], sq[:, 0:512],
                    start=(m == 0), stop=(m == 15))
        lastmm = b.mm(b.pB[:, 512:1024], cons["onesb"][:, :], sq[:, 512:1024],
                             start=(m == 0), stop=(m == 15))
        pet[m] = b.pe_tok(lastmm)
    tstats = pet[15]
    inv = 1.0 / 2048.0
    b.wait(b.ve, tstats)
    b.ve.tensor_scalar(mrep[:, :], b.pA[:, :], inv, None, ALU.mult)
    b.ve.drain()
    b.ve.tensor_tensor(c0[:, :], mrep[:, :], mrep[:, :], ALU.mult)
    b.ve.drain()
    tv = b.inc("ve", b.ve.scalar_tensor_tensor(
        rstd[:, :], b.pB[:, :], inv, c0[:, :], ALU.mult, ALU.subtract))
    b.wait(b.se, tv)
    b.se.activation(rstd[:, :], rstd[:, :], AF.Ln, bias=cons[eps_name][:, 0:1])
    b.se.drain()
    ts = b.inc("se", b.se.activation(rstd[:, :], rstd[:, :], AF.Exp, scale=-0.5))
    b.wait(b.ve, ts)
    b.ve.drain()
    b.inc("ve", b.ve.scalar_tensor_tensor(
        c0[:, :], mrep[:, :], -1.0, rstd[:, :], ALU.mult, ALU.mult))
    b.ve.drain()
    vt = {-2: casttok[14], -1: casttok[15]}
    for m in range(16):
        buf = x_ld[m % 2]
        b.wait(b.sy, vt[m - 2])
        tok = b.dma(b.sy, f"lnx{m % 2}", buf[:, :], x_dram[m])
        b.wait(b.ve, tok)
        b.ve.tensor_tensor(buf[:, :], buf[:, :], rstd[:, :], ALU.mult)
        vt[m] = b.inc("ve", b.ve.tensor_tensor(
            xnorm[:, m, :], buf[:, :], c0[:, :], ALU.add))
        if m >= 1:
            b.wait(b.pe, vt[m - 1])
        b.pe_warm(cons, 40, b.pC)
    b.barrier()
    b.release()



def mk_gemm_bufs(b, KC, pair, nbuf=2):
    bufs = dict(KC=KC, pair=pair, nbuf=nbuf,
                wf=[b.tile((128, KC, 128)) for _ in range(nbuf)],
                wb=[b.tile((128, KC, 128), BF16) for _ in range(nbuf)],
                tok={}, n=0)
    if pair:
        bufs["wf2"] = [b.tile((128, KC, 128)) for _ in range(nbuf)]
        bufs["wb2"] = [b.tile((128, KC, 128), BF16) for _ in range(nbuf)]
    return bufs


def emit_gemm(b, w_dram, M, KC, rhs_fn, epi_fn, w2_dram=None, n_tok=T,
              nbuf=3, chain=None, rhs_tok=None):
    own = chain is None
    if KC > 16 or w2_dram is not None:
        nbuf = 2
    if own:
        b.mark()
        chain = mk_gemm_bufs(b, KC, w2_dram is not None, nbuf)
    else:
        assert chain["KC"] == KC and chain["pair"] == (w2_dram is not None)
        nbuf = chain["nbuf"]
    wf, wb = chain["wf"], chain["wb"]
    wf2 = chain.get("wf2")
    wb2 = chain.get("wb2")
    NT = -(-n_tok // 512)
    base = chain["n"]
    ldt, pot, epit, pet = chain.setdefault("ldt", {}), chain.setdefault("pot", {}), {}, {}

    def issue_load(i):
        gi = base + i
        if gi >= nbuf:
            b.wait(b.sy, pot[gi - nbuf])
        t = b.dma(b.sy, f"gw{chain['KC']}{'p' if chain['pair'] else ''}{gi % nbuf}",
                  wf[gi % nbuf][:, :, :], w_dram[i])
        if w2_dram is not None:
            t = b.dma(b.sy, f"gw{chain['KC']}{'p' if chain['pair'] else ''}{gi % nbuf}",
                      wf2[gi % nbuf][:, :, :], w2_dram[i])
        ldt[gi] = t

    def issue_cast(i):
        gi = base + i
        b.wait(b.po, ldt[gi])
        if gi >= nbuf:
            b.wait(b.po, chain.setdefault("pet", {})[gi - nbuf])
        c = b.po.tensor_copy(wb[gi % nbuf][:, :, :], wf[gi % nbuf][:, :, :])
        if w2_dram is not None:
            c = b.po.tensor_copy(wb2[gi % nbuf][:, :, :], wf2[gi % nbuf][:, :, :])
        pot[gi] = b.inc("po", c)

    for i in range(min(nbuf, M)):
        issue_load(i)
        issue_cast(i)
    for i in range(M):
        gi = base + i
        if i + nbuf < M:
            issue_load(i + nbuf)
        b.wait(b.pe, pot[gi])
        if i == 0 and rhs_tok is not None:
            b.wait(b.pe, rhs_tok)
        if w2_dram is not None:
            ps, ps2 = b.pA, b.pB
            if i == 0:
                for tkn in (b.psum_tok["A"], b.psum_tok["B"]):
                    b.wait(b.pe, tkn)
            elif epit.get(i - 1) is not None:
                b.wait(b.pe, epit[i - 1])
        else:
            ps, ps2 = (b.pA, None) if i % 2 == 0 else (b.pB, None)
            if i == 0:
                b.wait(b.pe, b.psum_tok["A"])
            elif i == 1:
                b.wait(b.pe, b.psum_tok["B"])
            elif epit.get(i - 2) is not None:
                b.wait(b.pe, epit[i - 2])
        last = None
        for nt in range(NT):
            t0, t1 = nt * 512, min((nt + 1) * 512, n_tok)
            for kc in range(KC):
                last = b.mm(ps[:, t0:t1], wb[gi % nbuf][:, kc, :],
                            rhs_fn(kc, t0, t1),
                            start=(kc == 0), stop=(kc == KC - 1))
            if w2_dram is not None:
                for kc in range(KC):
                    last = b.mm(ps2[:, t0:t1], wb2[gi % nbuf][:, kc, :],
                                rhs_fn(kc, t0, t1),
                                start=(kc == 0), stop=(kc == KC - 1))
        pet[i] = chain.setdefault("pet", {})
        pet[i] = b.pe_tok(last)
        chain["pet"][gi] = pet[i]
        if i + nbuf < M:
            issue_cast(i + nbuf)
        tk = None
        for nt in range(NT):
            t0, t1 = nt * 512, min((nt + 1) * 512, n_tok)
            r = epi_fn(i, t0, t1, ps[:, t0:t1],
                       None if ps2 is None else ps2[:, t0:t1], pet[i])
            if r is not None:
                tk = r
        epit[i] = tk
    chain["n"] = base + M
    lastA = lastB = None
    for i in range(M):
        if epit.get(i) is None:
            continue
        if w2_dram is not None or i % 2 == 0:
            lastA = epit[i]
        if w2_dram is not None or i % 2 == 1:
            lastB = epit[i]
    b.psum_tok["A"] = lastA if lastA is not None else b.psum_tok["A"]
    b.psum_tok["B"] = lastB if lastB is not None else b.psum_tok["B"]
    if own:
        b.barrier()
        b.release()
    return epit.get(M - 1)



def emit_gating(b, cons, L, xnorm, dwT, renorm):
    ncol = L["ngcol"]
    b.mark()
    gw = b.tile((128, 16, ncol))
    gwb = b.tile((128, 16, ncol), BF16)
    logits = b.tile((128, 8, ncol))
    p = b.tile((128, 8, 8))
    msk = b.tile((128, 8, 8))
    tmp = b.tile((128, 8, 8))
    red = b.tile((128, 8))
    dwtm = b.tile((128, 8, 8), BF16)
    sgtm = b.tile((128, 8), BF16) if L.get("nsig") else None
    tok = b.dma(b.sy, "misc0", gw[:, :, :], cons[L["gwT"]])
    b.wait(b.po, tok)
    tc_ = b.inc("po", b.po.tensor_copy(gwb[:, :, :], gw[:, :, :]))
    b.wait(b.pe, tc_)
    lastmm = None
    for tc in range(8):
        for kc in range(16):
            lastmm = b.mm(b.pD[:, tc * ncol:(tc + 1) * ncol],
                                 xnorm[:, kc, tc * 128:(tc + 1) * 128],
                                 gwb[:, kc, :], start=(kc == 0), stop=(kc == 15))
    tmm = b.pe_tok()
    b.wait(b.ve, tmm)
    b.ve.tensor_copy(logits[:, :, :],
                     b.pD[:, 0:8 * ncol].rearrange("p (a c) -> p a c", a=8))
    b.ve.drain()
    lg8 = logits[:, :, 0:8]
    b.ve.tensor_reduce(red[:, :], lg8, axis=AX.X, op=ALU.max)
    b.ve.drain()
    for tc in range(8):
        b.ve.tensor_scalar(p[:, tc, :], lg8[:, tc, :], red[:, tc:tc + 1], None,
                           ALU.subtract)
    b.ve.drain()
    tv = b.inc("ve", b.ve.nop(nofuse=True))
    b.wait(b.se, tv)
    ts = b.inc("se", b.se.activation(p[:, :, :], p[:, :, :], AF.Exp))
    b.wait(b.ve, ts)
    b.ve.tensor_reduce(red[:, :], p[:, :, :], axis=AX.X, op=ALU.add)
    b.ve.drain()
    b.ve.reciprocal(red[:, :], red[:, :])
    b.ve.drain()
    for tc in range(8):
        b.ve.tensor_scalar(p[:, tc, :], p[:, tc, :], red[:, tc:tc + 1], None,
                           ALU.mult)
    b.ve.drain()
    if L.get("gbrep"):
        gb = b.tile((128, 8))
        tok = b.dma(b.sy, "misc1", gb[:, :], cons[L["gbrep"]])
        b.wait(b.ve, tok)
        bview = gb[:, :].unsqueeze(1).broadcast_to([128, 8, 8])
        b.ve.tensor_tensor(tmp[:, :, :], p[:, :, :], bview, ALU.add)
    else:
        b.ve.tensor_copy(tmp[:, :, :], p[:, :, :])
    b.ve.drain()
    b.ve.tensor_reduce(red[:, :], tmp[:, :, :], axis=AX.X, op=ALU.max)
    b.ve.drain()
    for tc in range(8):
        b.ve.tensor_scalar(msk[:, tc, :], tmp[:, tc, :], red[:, tc:tc + 1], None,
                           ALU.is_equal)
    b.ve.drain()
    b.ve.scalar_tensor_tensor(tmp[:, :, :], msk[:, :, :], -1e30, tmp[:, :, :],
                              ALU.mult, ALU.add)
    b.ve.drain()
    b.ve.tensor_reduce(red[:, :], tmp[:, :, :], axis=AX.X, op=ALU.max)
    b.ve.drain()
    for tc in range(8):
        b.ve.tensor_scalar(tmp[:, tc, :], tmp[:, tc, :], red[:, tc:tc + 1], None,
                           ALU.is_equal)
    b.ve.drain()
    b.ve.tensor_tensor(msk[:, :, :], msk[:, :, :], tmp[:, :, :], ALU.add)
    b.ve.drain()
    b.ve.tensor_tensor(p[:, :, :], p[:, :, :], msk[:, :, :], ALU.mult)
    b.ve.drain()
    if renorm:
        b.ve.tensor_reduce(red[:, :], p[:, :, :], axis=AX.X, op=ALU.add)
        b.ve.drain()
        b.ve.reciprocal(red[:, :], red[:, :])
        b.ve.drain()
        for tc in range(8):
            b.ve.tensor_scalar(p[:, tc, :], p[:, tc, :], red[:, tc:tc + 1], None,
                               ALU.mult)
        b.ve.drain()
    tv = b.inc("ve", b.ve.tensor_copy(dwtm[:, :, :], p[:, :, :]))
    tv2 = None
    if sgtm is not None:
        b.wait(b.se, tmm)
        b.se.drain()
        tv2 = b.inc("se", b.se.activation(sgtm[:, :], logits[:, :, 8], AF.Sigmoid))
    b.wait(b.pe, tv)
    for tc in range(8):
        nrow = 8 + (1 if sgtm is not None else 0)
        if sgtm is not None:
            src = b.tile((128, 9), BF16)
            b.wait(b.ve, tv2)
            b.ve.tensor_copy(src[:, 0:8], dwtm[:, tc, :])
            b.ve.drain()
            tvv = b.inc("ve", b.ve.tensor_copy(src[:, 8:9], sgtm[:, tc:tc + 1]))
            b.wait(b.pe, tvv)
            b.tr(b.pE.bitcast(BF16)[0:nrow, tc * 128:(tc + 1) * 128],
                           src[:, :], cons["ident_b"][:, :])
        else:
            b.tr(b.pE.bitcast(BF16)[0:8, tc * 128:(tc + 1) * 128],
                           dwtm[:, tc, :], cons["ident_b"][:, :])
    tpe = b.pe_tok()
    b.wait(b.ve, tpe)
    b.inc("ve", b.ve.tensor_copy(
        dwT[:, :], b.pE.bitcast(BF16)[0:dwT.shape[0], 0:1024]))
    b.barrier()
    b.release()


def emit_bcast_rows(b, cons, rowsrc, nrows, out_bf, srow=0):
    for e in range(nrows):
        ei = min(srow + e, 8)
        nr = rowsrc.shape[0]
        last = None
        for half in range(2):
            last = b.mm(b.pC[:, half * 512:(half + 1) * 512],
                               cons["i8e_b"][ei][0:nr, :],
                               rowsrc[0:nr, half * 512:(half + 1) * 512],
                               start=True, stop=True)
        t = b.pe_tok()
        b.wait(b.ve, t)
        b.inc("ve", b.ve.tensor_copy(out_bf[:, e, :], b.pC[:, 0:1024]))
        b.ve.drain()
        b.wait(b.pe, b.tok("ve"))



def emit_ffn(b, cons, L, x_in, x_out, dram):
    b.mark()
    xnorm = b.tile((128, 16, T), BF16)
    y_acc = b.tile((128, 16, T))
    emit_ln(b, cons, x_in, xnorm, L["eps_ln"])

    nsig = L.get("nsig", 0)
    sigb = b.tile((128, 1, 1024), BF16) if nsig else None

    for m in range(16):
        b.po.memset(y_acc[:, m, :], 0.0)
    b.po.drain()
    b.inc("po", b.po.nop(nofuse=True))

    def rhs(kc, t0, t1):
        return xnorm[:, kc, t0:t1]

    b.mark()
    dwT = b.tile((8 + nsig, 1024), BF16)
    emit_gating(b, cons, L, xnorm, dwT, renorm=L["renorm"])
    dwb = b.tile((128, 8, 1024), BF16)
    emit_bcast_rows(b, cons, dwT, 8, dwb)
    if nsig:
        emit_bcast_rows(b, cons, dwT, 1, sigb, srow=8)
    b.barrier()
    upset = mk_gemm_bufs(b, 16, L["wgt"] is not None)
    dnset = mk_gemm_bufs(b, L["EI"] // 128, False)
    nhc = L["EI"] // 128
    hbuf = b.tile((128, nhc, T), BF16)
    tmpb = [b.tile((128, 512), BF16) for _ in range(2)]
    gu = b.tile((128, L["MI"], T), BF16) if L["wgt"] is None else None
    cell = {"prev_ve": None, "slot": 0}

    for e in range(L["E"]):
        if L["wgt"] is not None:
            gt_d = cons[L["wgt"]][e]
            up_d = cons[L["wup"]][e]

            def epi_h(m, t0, t1, ps, ps2, tok, e=e):
                sl = cell["slot"] % 2
                cell["slot"] += 1
                b.wait(b.se, tok)
                if m == 0 and t0 == 0 and cell.get("dn_pe") is not None:
                    b.wait(b.se, cell["dn_pe"])
                    b.wait(b.ve, cell["dn_pe"])
                if cell["prev_ve"] is not None:
                    b.wait(b.se, cell["prev_ve"])
                b.se.drain()
                tsl = b.inc("se", b.se.activation(tmpb[sl][:, 0:t1 - t0], ps,
                                                  AF.Silu))
                b.wait(b.ve, tsl)
                b.ve.tensor_tensor(hbuf[:, m, t0:t1], tmpb[sl][:, 0:t1 - t0],
                                   ps2, ALU.mult)
                cell["prev_ve"] = b.inc("ve", b.ve.tensor_tensor(
                    hbuf[:, m, t0:t1], hbuf[:, m, t0:t1],
                    dwb[:, e, t0:t1], ALU.mult))
                return cell["prev_ve"]

            emit_gemm(b, gt_d, L["MI"], 16, rhs, epi_h, w2_dram=up_d,
                      chain=upset)
        else:

            def epi_gu(m, t0, t1, ps, ps2, tok):
                b.wait(b.se, tok)
                b.se.drain()
                b.inc("se", b.se.copy(gu[:, m, t0:t1], ps))
                return b.tok("se")

            emit_gemm(b, cons[L["wgup"]][e], L["MI"], 16, rhs, epi_gu,
                      chain=upset)
            half = L["MI"] // 2
            if cell.get("dn_pe") is not None:
                b.wait(b.ve, cell["dn_pe"])
                b.wait(b.se, cell["dn_pe"])
            for mi in range(half):
                for nt in range(2):
                    t0, t1 = nt * 512, (nt + 1) * 512
                    sl = cell["slot"] % 2
                    cell["slot"] += 1
                    if cell["prev_ve"] is not None:
                        b.wait(b.se, cell["prev_ve"])
                    b.se.drain()
                    tsl = b.inc("se", b.se.activation(
                        tmpb[sl][:, 0:512], gu[:, mi, t0:t1], AF.Silu))
                    b.wait(b.ve, tsl)
                    b.ve.tensor_tensor(hbuf[:, mi, t0:t1], tmpb[sl][:, 0:512],
                                       gu[:, half + mi, t0:t1], ALU.mult)
                    cell["prev_ve"] = b.inc("ve", b.ve.tensor_tensor(
                        hbuf[:, mi, t0:t1], hbuf[:, mi, t0:t1],
                        dwb[:, e, t0:t1], ALU.mult))

        def rhs_h(kc, t0, t1):
            return hbuf[:, kc, t0:t1]

        def epi_y(m, t0, t1, ps, ps2, tok):
            b.wait(b.ve, tok)
            return b.inc("ve", b.ve.tensor_tensor(
                y_acc[:, m, t0:t1], y_acc[:, m, t0:t1], ps, ALU.add))

        emit_gemm(b, cons[L["wdn"]][e], 16, nhc, rhs_h, epi_y,
                  chain=dnset, rhs_tok=cell["prev_ve"])
        cell["dn_pe"] = b.tok("pe")
    b.barrier()
    b.release()

    b.mark()
    msh = L["MSH"]
    nsplit = L["sh_split"]
    hsh = b.tile((128, msh, T // nsplit), BF16)
    tmpb2 = [b.tile((128, 512), BF16) for _ in range(2)]
    cell2 = {"prev_ve": None, "slot": 0}
    for sp in range(nsplit):
        tt0 = sp * (T // nsplit)

        def rhs_s(kc, t0, t1, tt0=tt0):
            return xnorm[:, kc, tt0 + t0:tt0 + t1]

        def epi_hs(m, t0, t1, ps, ps2, tok):
            sl = cell2["slot"] % 2
            cell2["slot"] += 1
            b.wait(b.se, tok)
            if cell2["prev_ve"] is not None:
                b.wait(b.se, cell2["prev_ve"])
            b.se.drain()
            tsl = b.inc("se", b.se.activation(tmpb2[sl][:, 0:t1 - t0], ps,
                                              AF.Silu))
            b.wait(b.ve, tsl)
            cell2["prev_ve"] = b.inc("ve", b.ve.tensor_tensor(
                hsh[:, m, t0:t1], tmpb2[sl][:, 0:t1 - t0], ps2, ALU.mult))
            return cell2["prev_ve"]

        emit_gemm(b, cons[L["wsg"]], msh, 16, rhs_s, epi_hs,
                  w2_dram=cons[L["wsu"]], n_tok=T // nsplit)

        def rhs_hs(kc, t0, t1):
            return hsh[:, kc, t0:t1]

        def epi_ys(m, t0, t1, ps, ps2, tok, tt0=tt0):
            b.wait(b.ve, tok)
            if nsig:
                b.ve.tensor_tensor(ps, ps, sigb[:, 0, tt0 + t0:tt0 + t1], ALU.mult)
                b.ve.drain()
            return b.inc("ve", b.ve.tensor_tensor(
                y_acc[:, m, tt0 + t0:tt0 + t1],
                y_acc[:, m, tt0 + t0:tt0 + t1], ps, ALU.add))

        emit_gemm(b, cons[L["wsd"]], 16, msh, rhs_hs, epi_ys, n_tok=T // nsplit)
    b.release()

    b.mark()
    x_ld = [b.tile((128, T)) for _ in range(2)]
    res = [b.tile((128, T)) for _ in range(2)]
    vt = {}
    stok = {}
    for m in range(16):
        if m >= 2:
            b.wait(b.sy, vt[m - 2])
        tok = b.dma(b.sy, f"lnx{m % 2}", x_ld[m % 2][:, :], x_in[m])
        b.wait(b.ve, tok)
        if m >= 2:
            b.wait(b.ve, stok[m - 2])
        vt[m] = b.inc("ve", b.ve.tensor_tensor(res[m % 2][:, :], y_acc[:, m, :],
                                               x_ld[m % 2][:, :], ALU.add))
        b.wait(b.po, vt[m])
        stok[m] = b.dma(b.po, f"st{m % 2}", x_out[m], res[m % 2][:, :])
    b.barrier()
    b.release()
    b.release()



def emit_attn(b, cons, L, x_in, x_out, dram):
    import os
    KDBG = int(os.environ.get("KDBG", "0"))
    nc_x = L["nch_xbc"]
    NH, P, N, G = 64, 64, L["N"], L["G"]
    NCH = 8
    b.mark()
    region = b.alloc(64 * 1024)
    DBG["region"] = region
    xnorm = b.view(region, (128, 16, T), BF16)
    y_buf = b.view(region, (128, 32, T), BF16)
    dt_raw = b.view(region + 32 * 1024, (64, T))
    dt_hm = b.view(region + 36 * 1024, (64, T))
    lndt_hm = b.view(region + 40 * 1024, (64, T))
    adt_hm = b.view(region + 44 * 1024, (64, T))
    nch_bc = nc_x - 32
    bc = b.tile((128, nch_bc, T), BF16)
    c_all = b.tile((64, T), BF16) if L["l0"] else None
    tail_st = b.tile((128, 256))
    gtail_lo = b.tile((128, 256))
    gtail_hi = b.tile((128, 256))
    diag = b.tile((128, 32, 128), BF16)
    colmisc = b.tile((64, 4))

    emit_ln(b, cons, x_in, xnorm, "eps5")

    tok = b.dma(b.sy, "misc0", colmisc[:, 0:1], cons[L["dtb"]][:, :])
    tok2 = b.dma(b.sy, "misc1", colmisc[:, 1:2], cons[L["alog"]][:, :])
    b.wait(b.se, tok2)
    b.se.drain()
    ts = b.inc("se", b.se.activation(colmisc[:, 1:2], colmisc[:, 1:2], AF.Exp))
    b.wait(b.ve, ts)
    b.ve.tensor_scalar(colmisc[:, 1:2], colmisc[:, 1:2], -1.0, None, ALU.mult)
    b.ve.memset(colmisc[:, 2:3], 1.0)
    b.ve.drain()
    b.inc("ve", b.ve.nop(nofuse=True))
    b.po.memset(tail_st[:, :], 0.0)
    b.po.drain()
    b.inc("po", b.po.nop(nofuse=True))
    dp = b.tile((128, 32))
    tok = b.dma(b.sy, "misc0", dp[:, :], cons[L["dpair"]][:, :])
    b.wait(b.ve, tok)
    for j in range(32):
        b.ve.tensor_scalar(diag[:, j, :], cons["ident_b"][:, :], dp[:, j:j + 1],
                           None, ALU.mult)
    b.ve.drain()
    b.inc("ve", b.ve.nop(nofuse=True))
    b.barrier()

    b.mark()
    z_stage = [b.tile((128, T), BF16) for _ in range(2)]
    x_stage = [b.tile((128, T), BF16) for _ in range(2)]
    st_z = {}
    st_x = {}

    def epi_in(m, t0, t1, ps, ps2, tok):
        if m < 32:
            sl = m % 2
            b.wait(b.se, tok)
            if m >= 2:
                b.wait(b.se, st_z[m - 2])
            b.se.drain()
            tsl = b.inc("se", b.se.activation(z_stage[sl][:, t0:t1], ps, AF.Silu))
            if t1 == T:
                b.wait(b.po, tsl)
                st_z[m] = b.dma(b.po, f"st{sl}", dram["z_spill"][m],
                                z_stage[sl][:, :])
            return tsl
        if m < 32 + nc_x:
            mx = m - 32
            sl = m % 2
            b.wait(b.se, tok)
            if m >= 34:
                b.wait(b.se, st_x[m - 2])
            b.se.drain()
            tsl = b.inc("se", b.se.copy(x_stage[sl][:, t0:t1], ps))
            b.wait(b.ve, tsl)
            if t0 == 0:
                b.ve.tensor_copy(tail_st[:, mx * 2:mx * 2 + 2], ps[:, 0:2])
            if t1 == T:
                b.ve.tensor_copy(tail_st[:, nc_x * 2 + mx * 3:nc_x * 2 + mx * 3 + 3],
                                 ps[:, 509:512])
            b.ve.drain()
            tv = b.inc("ve", b.ve.nop(nofuse=True))
            if t1 == T:
                b.wait(b.po, tv)
                st_x[m] = b.dma(b.po, f"st{2 + sl}", dram["xbc_spill"][mx],
                                x_stage[sl][:, :])
            return tv
        b.wait(b.se, tok)
        b.se.drain()
        return b.inc("se", b.se.copy(dt_raw[0:64, t0:t1], ps[0:64, :]))

    def rhs_in(kc, t0, t1):
        return xnorm[:, kc, t0:t1]

    emit_gemm(b, cons[L["win"]], L["M_in"], 16, rhs_in, epi_in)

    cin_t, cout_t = dram["cin_tail"], dram["cout_tail"]
    b.wait(b.po, b.tok("ve"))
    tst = b.dma(b.po, "st0", cin_t.ap(), tail_st[:, :])
    b.wait(b.po, tst)
    tcc = b.inc("cc", b.po.collective_compute(
        "AllGather", ALU.bypass, replica_groups=[[0, 1], [2, 3], [4, 5], [6, 7]],
        ins=[cin_t.ap().opt()], outs=[cout_t.ap().opt()]))
    b.wait(b.po, tcc)
    tg1 = b.dma(b.po, "st0", gtail_lo[:, :], cout_t.ap()[0:128, :])
    tg2 = b.dma(b.po, "st1", gtail_hi[:, :], cout_t.ap()[128:256, :])
    b.barrier()
    b.release()

    b.mark()
    cw = b.tile((128, nc_x, 4))
    cb = b.tile((128, nc_x))
    cbuf = [b.tile((128, 1032), BF16) for _ in range(3)]
    ctmp = [b.tile((128, T)) for _ in range(2)]
    hs_stage = [b.tile((128, T), BF16) for _ in range(2)]
    tok = b.dma(b.sy, "misc0", cw[:, :, :], cons[L["convw"]][:, :, :])
    tok2 = b.dma(b.sy, "misc1", cb[:, :], cons[L["convb"]][:, :])
    b.wait(b.ve, tok)
    b.wait(b.ve, tok2)
    pl = L["pad_left"]
    rdt = {}
    vet = {}
    set_ = {}
    stt = {}
    for m in range(nc_x):
        sl = m % 3
        if m >= 3:
            b.wait(b.sy, vet[m - 3])
        rdt[m] = b.dma(b.sy, f"cv{sl}", cbuf[sl][:, 3:1027], dram["xbc_spill"][m])
        b.wait(b.ve, rdt[m])
        if m >= 1:
            b.wait(b.ve, set_[m - 1])
        b.ve.tensor_scalar(ctmp[1][:, 0:3],
                           gtail_lo[:, nc_x * 2 + m * 3:nc_x * 2 + m * 3 + 3],
                           cons["flag_hi"][:, 0:1], None, ALU.mult)
        b.ve.drain()
        b.ve.tensor_copy(cbuf[sl][:, 0:3], ctmp[1][:, 0:3])
        if L["pad_left"] == 1:
            b.ve.tensor_scalar(ctmp[1][:, 4:6],
                               gtail_hi[:, m * 2:m * 2 + 2],
                               cons["flag_lo"][:, 0:1], None, ALU.mult)
            b.ve.drain()
            b.ve.tensor_copy(cbuf[sl][:, 1027:1029], ctmp[1][:, 4:6])
        else:
            b.ve.memset(cbuf[sl][:, 1027:1029], 0.0)
        b.ve.drain()
        off0 = 3 - pl
        b.ve.tensor_scalar(ctmp[0][:, :], cbuf[sl][:, off0:off0 + 1024],
                           cw[:, m, 0:1], None, ALU.mult)
        lastv = None
        for k in range(1, 4):
            lastv = b.ve.scalar_tensor_tensor(ctmp[0][:, :],
                                              cbuf[sl][:, off0 + k:off0 + k + 1024],
                                              cw[:, m, k:k + 1], ctmp[0][:, :],
                                              ALU.mult, ALU.add)
        vet[m] = b.inc("ve", lastv)
        if m >= 1:
            b.wait(b.pe, vet[m - 1])
        b.pe_warm(cons, 90)
        b.wait(b.se, vet[m])
        if m < 32:
            hsl = m % 2
            if m >= 2:
                b.wait(b.se, stt[m - 2])
            b.se.drain()
            tse = b.inc("se", b.se.activation(hs_stage[hsl][:, :], ctmp[0][:, :],
                                              AF.Silu, bias=cb[:, m:m + 1]))
            b.wait(b.po, tse)
            stt[m] = b.dma(b.po, f"st{hsl}", dram["hs_spill"][m],
                           hs_stage[hsl][:, :])
            set_[m] = tse
        else:
            b.se.drain()
            set_[m] = b.inc("se", b.se.activation(bc[:, m - 32, :], ctmp[0][:, :],
                                                  AF.Silu, bias=cb[:, m:m + 1]))
    b.barrier()
    if L["l0"]:
        tcp = b.dma(b.po, "st0", c_all[:, :], bc[64:128, 0, :])
        b.barrier()
    b.release()

    b.ve.tensor_scalar(dt_raw[0:64, :], dt_raw[0:64, :], colmisc[0:64, 0:1],
                       None, ALU.add)
    b.ve.drain()
    tv = b.inc("ve", b.ve.nop(nofuse=True))
    b.wait(b.se, tv)
    b.se.activation(lndt_hm[0:64, :], dt_raw[0:64, :], AF.Abs)
    b.se.drain()
    b.se.activation(lndt_hm[0:64, :], lndt_hm[0:64, :], AF.Exp, scale=-1.0)
    b.se.drain()
    ts = b.inc("se", b.se.activation(lndt_hm[0:64, :], lndt_hm[0:64, :], AF.Ln,
                                     bias=colmisc[0:64, 2:3]))
    b.wait(b.ve, ts)
    b.ve.tensor_scalar(dt_hm[0:64, :], dt_raw[0:64, :], 0.0, None, ALU.max)
    b.ve.drain()
    b.ve.tensor_tensor(dt_hm[0:64, :], dt_hm[0:64, :], lndt_hm[0:64, :], ALU.add)
    b.ve.drain()
    b.ve.tensor_scalar(dt_hm[0:64, :], dt_hm[0:64, :], 1e-30, None, ALU.max)
    b.ve.drain()
    b.ve.tensor_scalar(adt_hm[0:64, :], dt_hm[0:64, :], colmisc[0:64, 1:2],
                       None, ALU.mult)
    b.ve.drain()
    tv = b.inc("ve", b.ve.nop(nofuse=True))
    b.wait(b.se, tv)
    b.se.drain()
    b.inc("se", b.se.activation(lndt_hm[0:64, :], dt_hm[0:64, :], AF.Ln))
    b.barrier()

    b.mark()
    St = b.tile((N, 64 * P))
    stbf = b.tile((N, 64 * P), BF16)
    x_tm = b.tile((128, 32, 128), BF16)
    hswin = [b.tile((128, 32, 128), BF16) for _ in range(3)]
    b_tm = b.tile((128, G, L["N"]), BF16)
    acum = b.tile((128, 8, 64))
    vtile = b.tile((128, 8, 64))
    dtdec = b.tile((128, 8, 64))
    lamrep = b.tile((128, 8, 64))
    expacum = b.tile((128, 8, 64), BF16)
    dt_tm = b.tile((128, 64))
    scr = b.tile((128, 64))
    lam_raw = b.tile((64, 1))
    lam_rowt = b.tile((1, 64))
    xd2 = [b.tile((128, 512), BF16) for _ in range(2)]
    xd_st = xd2[0]
    cbt = b.tile((128, G, 128), BF16)
    mt2 = [b.tile((128, 1024), BF16) for _ in range(2)]
    e1 = b.tile((128, 1024))
    ea_s = b.tile((128, 512))
    t1t = b.tile((128, 512))
    t2t = t1t.rearrange("p (a c) -> p a c", a=4)
    zbuf = [b.tile((128, 4, 128), BF16) for _ in range(2)]

    dbg_scr = b.tile((128, 512)) if KDBG else None

    def dbg_dump(ap, idx, np_=128):
        nfree = 1
        for d in ap.shape[1:]:
            nfree *= d
        b.barrier()
        b.ve.tensor_copy(dbg_scr[0:np_, 0:nfree], ap)
        b.ve.drain()
        b.inc("ve", b.ve.nop(nofuse=True))
        b.barrier()
        b.dma(b.po, "st0", dram["dbg"][idx][0:np_, 0:nfree],
              dbg_scr[0:np_, 0:nfree])
        b.barrier()

    b.po.memset(St[:, :], 0.0)
    b.po.drain()
    b.inc("po", b.po.nop(nofuse=True))
    b.barrier()

    def chunk_prep(c):
        cs = slice(c * 128, (c + 1) * 128)
        pp = b.pE
        b.tr(pp[:, 0:64], adt_hm[0:64, cs], cons["ident_f"][0:64, 0:64])
        b.tr(pp[:, 64:128], dt_hm[0:64, cs], cons["ident_f"][0:64, 0:64])
        b.tr(pp[:, 128:192], lndt_hm[0:64, cs],
                       cons["ident_f"][0:64, 0:64])
        t = b.pe_tok()
        b.wait(b.se, t)
        b.se.copy(scr[:, :], pp[:, 0:64])
        b.se.copy(dt_tm[:, :], pp[:, 64:128])
        b.se.drain()
        ts = b.inc("se", b.se.copy(vtile[:, c, :], pp[:, 128:192]))
        b.wait(b.pe, ts)
        b.mm(pp[:, 192:256], cons["triu_f"][:, :], scr[:, :],
                    start=True, stop=True)
        t = b.pe_tok()
        b.wait(b.se, t)
        ts = b.inc("se", b.se.copy(acum[:, c, :], pp[:, 192:256]))
        b.wait(b.pe, ts)
        b.wait(b.sy, ts)
        trow = b.dma(b.sy, "lam0", lam_rowt[:, :], acum[127:128, c, :])
        b.wait(b.pe, trow)
        b.tr(pp[0:64, 256:257], lam_rowt[0:1, :],
                       cons["ident_f"][0:1, 0:1])
        t = b.pe_tok()
        b.wait(b.se, t)
        ts = b.inc("se", b.se.copy(lam_raw[:, :], pp[0:64, 256:257]))
        b.wait(b.pe, ts)
        b.tr(pp[:, 320:384],
                       lam_raw[:, 0:1].broadcast_to([64, 128]),
                       cons["ident_f"][0:64, 0:64])
        t = b.pe_tok()
        b.wait(b.ve, t)
        b.ve.tensor_tensor(scr[:, :], pp[:, 320:384], acum[:, c, :], ALU.subtract)
        b.ve.drain()
        tv = b.inc("ve", b.ve.tensor_tensor(vtile[:, c, :], vtile[:, c, :],
                                            acum[:, c, :], ALU.subtract))
        b.wait(b.se, tv)
        b.se.activation(scr[:, :], scr[:, :], AF.Exp)
        b.se.drain()
        b.se.activation(lamrep[:, c, :], pp[:, 320:384], AF.Exp)
        b.se.drain()
        ts = b.inc("se", b.se.activation(expacum[:, c, :], acum[:, c, :], AF.Exp))
        b.wait(b.ve, ts)
        tv = b.inc("ve", b.ve.tensor_tensor(dtdec[:, c, :], scr[:, :],
                                            dt_tm[:, :], ALU.mult))
        return tv

    def load_hswin(c, sl):
        b.wait(b.sy, b.tok("pe"))
        b.wait(b.po, b.tok("pe"))
        src = dram["hs_spill"][0:32, :, c * 128:(c + 1) * 128] \
            .rearrange("a p c -> p a c")
        b.dma(b.sy, f"hw{sl}", hswin[sl][:, 0:16, :], src[:, 0:16, :])
        return b.dma(b.po, f"hw{sl}", hswin[sl][:, 16:32, :], src[:, 16:32, :])

    def x_transposes(c, sl, want_xd):
        for batch in range(8):
            pp = b.pD if batch % 2 == 0 else b.pE
            ppb = pp.bitcast(BF16)
            for f4 in range(4):
                f = batch * 4 + f4
                b.tr(ppb[:, f4 * 128:(f4 + 1) * 128],
                               hswin[sl][:, f, :], cons["ident_b"][:, :])
            t = b.pe_tok()
            b.wait(b.ve, t)
            tv = b.inc("ve", b.ve.tensor_copy(
                x_tm[:, batch * 4:batch * 4 + 4, :]
                .rearrange("p a c -> p (a c)"), ppb[:, 0:512]))
            b.wait(b.pe, tv)
        if L["l0"]:
            b.tr(b.pD.bitcast(BF16)[:, 0:64], bc[0:64, 0, c * 128:(c + 1) * 128],
                           cons["ident_b"][0:64, 0:64])
            t = b.pe_tok()
            b.wait(b.ve, t)
            b.ve.tensor_copy(b_tm[:, 0, :], b.pD.bitcast(BF16)[:, 0:64])
        else:
            for g in range(8):
                pp = b.pD if g % 2 == 0 else b.pE
                b.tr(pp.bitcast(BF16)[:, 0:128],
                               bc[:, g, c * 128:(c + 1) * 128], cons["ident_b"][:, :])
                t = b.pe_tok()
                b.wait(b.ve, t)
                tv = b.inc("ve", b.ve.tensor_copy(b_tm[:, g, :],
                                                  pp.bitcast(BF16)[:, 0:128]))
                b.wait(b.pe, tv)
        b.ve.drain()
        return b.inc("ve", b.ve.nop(nofuse=True))

    def lam_scale_st(c):
        for h in range(64):
            b.ve.tensor_scalar(St[:, h * P:(h + 1) * P], St[:, h * P:(h + 1) * P],
                               lamrep[0:N, c, h:h + 1], None, ALU.mult)
        b.ve.drain()
        return b.inc("ve", b.ve.nop(nofuse=True))

    def states_pass(c, sl, seed_scale):
        tlam = lam_scale_st(c)
        xdt = {}
        sadd = {}

        def emit_xd(g):
            lastxd = None
            for hh in range(8):
                h = g * 8 + hh
                lastxd = b.ve.tensor_scalar(
                    xd2[g % 2][:, hh * 64:(hh + 1) * 64],
                    x_tm[:, h // 2, (h % 2) * 64:(h % 2) * 64 + 64],
                    dtdec[:, c, h:h + 1], None, ALU.mult)
            xdt[g] = b.inc("ve", lastxd)

        emit_xd(0)
        for g in range(8):
            bg = g if G == 8 else 0
            b.wait(b.pe, xdt[g])
            if g >= 2:
                b.wait(b.pe, sadd[g - 2])
            half = g % 2
            for hh in range(8):
                b.mm(b.pC[0:N, half * 512 + hh * 64:half * 512 + hh * 64 + 64],
                     b_tm[:, bg, :], xd2[g % 2][:, hh * 64:(hh + 1) * 64],
                     start=True, stop=True)
            t = b.pe_tok()
            if g + 1 < 8:
                emit_xd(g + 1)
            b.wait(b.ve, t)
            sadd[g] = b.inc("ve", b.ve.tensor_tensor(
                St[:, g * 512:(g + 1) * 512], St[:, g * 512:(g + 1) * 512],
                b.pC[0:N, half * 512:half * 512 + 512], ALU.add))

    hw_toks = {0: load_hswin(0, 0), 1: load_hswin(1, 1)}
    for c in range(NCH):
        chunk_prep(c)
        b.wait(b.ve, hw_toks[c])
        b.wait(b.pe, hw_toks[c])
        if c + 2 < NCH:
            hw_toks[c + 2] = load_hswin(c + 2, (c + 2) % 3)
        x_transposes(c, c % 3, True)
        states_pass(c, c % 3, None)
    b.barrier()

    cin_s = dram["cin_st64"] if L["l0"] else dram["cin_st128"]
    cout_s = dram["cout_st64"] if L["l0"] else dram["cout_st128"]
    tst = b.dma(b.po, "st0", cin_s.ap()[0:N, 0:64 * P], St[:, :])
    b.wait(b.po, tst)
    tcc = b.inc("cc", b.po.collective_compute(
        "AllGather", ALU.bypass, replica_groups=[[0, 1], [2, 3], [4, 5], [6, 7]],
        ins=[cin_s.ap().opt()], outs=[cout_s.ap().opt()]))
    b.wait(b.po, tcc)
    tg = b.dma(b.po, "st0", St[:, :], cout_s.ap()[0:N, 0:64 * P])
    b.wait(b.ve, tg)
    for h in range(64):
        b.ve.tensor_scalar(St[:, h * P:(h + 1) * P], St[:, h * P:(h + 1) * P],
                           cons["flag_hi"][0:N, 0:1], None, ALU.mult)
    b.ve.drain()
    b.inc("ve", b.ve.nop(nofuse=True))
    b.barrier()

    zr = {}

    def load_z(c, g, sl):
        b.wait(b.sy, b.tok("ve"))
        src = dram["z_spill"][4 * g:4 * g + 4, :, c * 128:(c + 1) * 128] \
            .rearrange("a p c -> p a c")
        return b.dma(b.sy, f"zr{sl}", zbuf[sl][:, :, :], src)

    hw_toks = {0: load_hswin(0, 0), 1: load_hswin(1, 1)}
    zr[(0, 0)] = load_z(0, 0, 0)
    zslot = {(0, 0): 0}
    state = {"mt_tok": None, "ea_tok": None, "stadd_tok": None, "xd_tok": None,
             "e1_tok": None, "ydiag_tok": {}, "states_tok": {},
             "comb_tok2": {}}

    def prep_slot(c, g):
        cs = slice(c * 128, (c + 1) * 128)
        if state["e1_tok"] is not None:
            b.wait(b.pe, state["e1_tok"])
        for j8 in range(8):
            h = g * 8 + j8
            b.tr(b.pA[:, j8 * 128:(j8 + 1) * 128],
                 acum[:, c, h:h + 1].broadcast_to([128, 128]),
                 cons["ident_f"][:, :])
        t1tok = b.pe_tok()
        b.wait(b.ve, t1tok)
        vview = vtile[:, c, g * 8:(g + 1) * 8].unsqueeze(2) \
            .broadcast_to([128, 8, 128])
        b.ve.tensor_tensor(e1.rearrange("p (a d) -> p a d", a=8),
                           b.pA.rearrange("p (a d) -> p a d", a=8),
                           vview, ALU.add)
        state["e1_tok"] = b.inc("ve", b.ve.tensor_scalar(
            e1[:, :], e1[:, :], 30.0, None, ALU.min))
        b.wait(b.se, state["e1_tok"])
        b.se.drain()
        texp = b.inc("se", b.se.activation(e1[:, :], e1[:, :], AF.Exp))
        b.wait(b.ve, texp)
        par = (c * 8 + g) % 2
        if state["ydiag_tok"].get(par) is not None:
            b.wait(b.ve, state["ydiag_tok"][par])
        bg = g if G == 8 else 0
        cview = cbt[:, bg, :].unsqueeze(1).broadcast_to([128, 8, 128])
        state["mt_tok"] = b.inc("ve", b.ve.tensor_tensor(
            mt2[par].rearrange("p (a d) -> p a d", a=8),
            e1.rearrange("p (a d) -> p a d", a=8), cview, ALU.mult))
        if state["states_tok"].get(par) is not None:
            b.wait(b.ve, state["states_tok"][par])
        lastx = None
        for j8 in range(8):
            h = g * 8 + j8
            lastx = b.ve.tensor_scalar(xd2[par][:, j8 * 64:(j8 + 1) * 64],
                                       x_tm[:, h // 2, (h % 2) * 64:(h % 2) * 64 + 64],
                                       dtdec[:, c, h:h + 1], None, ALU.mult)
        state["xd_tok"] = b.inc("ve", lastx)

    def ywork_slot(c, g):
        cs = slice(c * 128, (c + 1) * 128)
        bg = g if G == 8 else 0
        par = (c * 8 + g) % 2
        pY = b.pB if par == 0 else b.pC
        b.wait(b.pe, state["mt_tok"])
        if state["comb_tok2"].get(par) is not None:
            b.wait(b.pe, state["comb_tok2"][par])
        mt8 = mt2[par]
        first = True
        for j8 in range(8):
            h = g * 8 + j8
            jp = j8 // 2
            b.mm(pY[(h % 2) * 64:(h % 2) * 64 + 64, jp * 128:(jp + 1) * 128],
                 x_tm[:, h // 2, (h % 2) * 64:(h % 2) * 64 + 64],
                 mt8[:, j8 * 128:(j8 + 1) * 128],
                 start=(j8 <= 1), stop=False, skip_group_check=True,
                 tile_position=(0, (h % 2) * 64))
        ydiag_tok = b.pe_tok()
        state["ydiag_tok"][par] = ydiag_tok
        for j in range(4):
            jj = 4 * g + j
            b.mm(pY[:, j * 128:(j + 1) * 128], diag[:, jj, :],
                 hswin[c % 3][:, jj, :], start=False, stop=(j == 3),
                 skip_group_check=True)
        for j8 in range(8):
            h = g * 8 + j8
            jp = j8 // 2
            if L["l0"]:
                crhs = c_all[:, cs]
                lhs = stbf[0:64, h * P:(h + 1) * P]
            else:
                crhs = bc[:, 8 + bg, cs]
                lhs = stbf[:, h * P:(h + 1) * P]
            b.mm(pY[(h % 2) * 64:(h % 2) * 64 + 64,
                    512 + jp * 128:512 + (jp + 1) * 128],
                 lhs, crhs, start=(j8 <= 1), stop=(j8 == 7),
                 skip_group_check=True, tile_position=(0, (h % 2) * 64))
        if state["ea_tok"] is not None:
            b.wait(b.pe, state["ea_tok"])
        pdb = b.pD.bitcast(BF16)
        for j in range(4):
            for parh in range(2):
                h = g * 8 + 2 * j + parh
                eview = expacum[:, c, h:h + 1].broadcast_to([128, 64])
                b.tr(pdb[parh * 64:parh * 64 + 64, j * 128:(j + 1) * 128],
                     eview, cons["ident_b"][:, :])
        b.wait(b.pe, state["xd_tok"])
        if state["stadd_tok"] is not None:
            b.wait(b.pe, state["stadd_tok"])
        for j8 in range(8):
            b.mm(b.pE[0:N, j8 * 64:(j8 + 1) * 64], b_tm[:, bg, :],
                 xd2[par][:, j8 * 64:(j8 + 1) * 64], start=True, stop=True)
        tall = b.pe_tok()
        state["states_tok"][par] = tall
        b.wait(b.se, tall)
        b.se.drain()
        tea = b.inc("se", b.se.copy(ea_s[:, :], pdb[:, 0:512]))
        state["ea_tok"] = tea
        b.wait(b.ve, tall)
        lasts = b.ve.tensor_tensor(St[:, g * 512:(g + 1) * 512],
                                   St[:, g * 512:(g + 1) * 512],
                                   b.pE[0:N, 0:512], ALU.add)
        if c + 1 < NCH:
            b.ve.drain()
            b.ve.tensor_copy(stbf[:, g * 512:(g + 1) * 512],
                             St[:, g * 512:(g + 1) * 512])
            b.ve.drain()
            for j8 in range(8):
                h = g * 8 + j8
                lasts = b.ve.tensor_scalar(St[:, h * P:(h + 1) * P],
                                           St[:, h * P:(h + 1) * P],
                                           lamrep[0:N, c + 1, h:h + 1], None,
                                           ALU.mult)
        state["stadd_tok"] = b.inc("ve", lasts)
        b.wait(b.ve, tea)
        b.ve.tensor_tensor(t1t[:, :], pY[:, 512:1024], ea_s[:, :], ALU.mult)
        b.ve.scalar_tensor_tensor(t1t[:, :], t1t[:, :], 0.0, pY[:, 0:512],
                                  ALU.add, ALU.add)
        b.wait(b.ve, zr[(c, g)])
        ng = (c, g + 1) if g < 7 else (c + 1, 0)
        if ng[0] < NCH:
            zsl = (zslot[(c, g)] + 1) % 2
            zslot[ng] = zsl
            zr[ng] = load_z(ng[0], ng[1], zsl)
        state["comb_tok2"][(c * 8 + g) % 2] = b.inc(
            "ve", b.ve.tensor_tensor(y_buf[:, 4 * g:4 * g + 4, cs],
                                     t1t.rearrange("p (a d) -> p a d", a=4),
                                     zbuf[zslot[(c, g)]][:, :, :], ALU.mult))

    for c in range(NCH):
        b.wait(b.ve, hw_toks[c])
        b.wait(b.pe, hw_toks[c])
        if c + 2 < NCH:
            hw_toks[c + 2] = load_hswin(c + 2, (c + 2) % 3)
        if state["ea_tok"] is not None:
            b.wait(b.pe, state["ea_tok"])
        if state["stadd_tok"] is not None:
            b.wait(b.pe, state["stadd_tok"])
        x_transposes(c, c % 3, False)
        cs = slice(c * 128, (c + 1) * 128)
        b.wait(b.pe, b.tok("ve"))
        ncb = 8 if G == 8 else 1
        for bg in range(ncb):
            pp = b.pD if bg < 4 else b.pE
            off = (bg % 4) * 128
            if L["l0"]:
                b.mm(pp[:, off:off + 128], bc[0:64, 0, cs], c_all[:, cs],
                     start=True, stop=True)
            else:
                b.mm(pp[:, off:off + 128], bc[:, bg, cs], bc[:, 8 + bg, cs],
                     start=True, stop=True)
        tcb = b.pe_tok()
        b.wait(b.ve, tcb)
        for bg in range(ncb):
            pp = b.pD if bg < 4 else b.pE
            off = (bg % 4) * 128
            b.ve.tensor_tensor(cbt[:, bg, :], pp[:, off:off + 128],
                               cons["triu_f"][:, :], ALU.mult)
        b.ve.drain()
        if c == 0:
            b.wait(b.ve, b.tok("pe"))
            b.inc("ve", b.ve.tensor_copy(stbf[:, :], St[:, :]))
            b.ve.drain()
            lam_scale_st(0)
        tbnd = b.tok("ve")
        b.wait(b.pe, tbnd)
        state["ea_tok"] = None
        state["stadd_tok"] = None
        for g in range(8):
            prep_slot(c, g)
            if g >= 1:
                ywork_slot(c, g - 1)
        ywork_slot(c, 7)
    b.barrier()

    normw = b.tile((128, 32))
    rstd = b.tile((128, T))
    sqb = b.tile((128, T), BF16)
    tok = b.dma(b.sy, "misc0", normw[:, :], cons[L["normw"]][:, :])
    pet = {}
    for f in range(32):
        b.wait(b.se, tok)
        if f >= 1:
            b.wait(b.se, pet[f - 1])
        b.se.drain()
        tsq = b.inc("se", b.se.activation(sqb[:, :], y_buf[:, f, :], AF.Square))
        b.wait(b.pe, tsq)
        b.mm(b.pA[:, 0:512], cons["onesb"][:, :], sqb[:, 0:512],
                    start=(f == 0), stop=(f == 31))
        b.mm(b.pA[:, 512:1024], cons["onesb"][:, :], sqb[:, 512:1024],
                    start=(f == 0), stop=(f == 31))
        pet[f] = b.pe_tok()
    b.wait(b.ve, pet[31])
    b.ve.tensor_scalar(rstd[:, :], b.pA[:, :], 1.0 / 4096.0, None, ALU.mult)
    b.ve.drain()
    tv = b.inc("ve", b.ve.nop(nofuse=True))
    b.wait(b.se, tv)
    b.se.activation(rstd[:, :], rstd[:, :], AF.Ln, bias=cons[L["eps_rms"]][:, 0:1])
    b.se.drain()
    ts = b.inc("se", b.se.activation(rstd[:, :], rstd[:, :], AF.Exp, scale=-0.5))
    b.wait(b.ve, ts)
    if KDBG == 2:
        dbg_dump(rstd[:, 0:512], 10)
    for f in range(32):
        b.ve.scalar_tensor_tensor(y_buf[:, f, :], y_buf[:, f, :],
                                  normw[:, f:f + 1], rstd[:, :],
                                  ALU.mult, ALU.mult)
    b.ve.drain()
    b.inc("ve", b.ve.nop(nofuse=True))
    if KDBG == 2:
        dbg_dump(y_buf[:, 0:4, 0:128], 11)
    b.barrier()
    b.release()

    b.mark()
    x_ld = [b.tile((128, T)) for _ in range(2)]
    res = [b.tile((128, T)) for _ in range(2)]
    vt = {}
    stok = {}

    def rhs_out(kc, t0, t1):
        return y_buf[:, kc, t0:t1]

    last_tv = {}

    def epi_out(m, t0, t1, ps, ps2, tok):
        if t0 == 0:
            if m >= 2:
                b.wait(b.sy, last_tv[m - 2])
            vt[m] = b.dma(b.sy, f"lnx{m % 2}", x_ld[m % 2][:, :], x_in[m])
        b.wait(b.ve, tok)
        b.wait(b.ve, vt[m])
        if m >= 2 and t0 == 0:
            b.wait(b.ve, stok[m - 2])
        tv = b.inc("ve", b.ve.tensor_tensor(res[m % 2][:, t0:t1], ps,
                                            x_ld[m % 2][:, t0:t1], ALU.add))
        if t1 == T:
            last_tv[m] = tv
            b.wait(b.po, tv)
            stok[m] = b.dma(b.po, f"st{m % 2}", x_out[m], res[m % 2][:, :])
        return tv

    emit_gemm(b, cons[L["wout"]], 16, 32, rhs_out, epi_out)
    b.release()
    b.release()


L0_ATTN = dict(l0=True, nch_xbc=33, M_in=66, N=64, G=1, pad_left=3,
               win="l0_win", wout="l0_wout", convw="l0_convw", convb="l0_convb",
               dtb="l0_dtb", alog="l0_alog", dpair="l0_dpair", normw="l0_normw",
               eps_rms="eps5")
L1_ATTN = dict(l0=False, nch_xbc=48, M_in=81, N=128, G=8, pad_left=1,
               win="l1_win", wout="l1_wout", convw="l1_convw", convb="l1_convb",
               dtb="l1_dtb", alog="l1_alog", dpair="l1_dpair", normw="l1_normw",
               eps_rms="eps6")



L0_FFN = dict(eps_ln="eps5", gwT="l0_gwT", gbrep="l0_gbrep", ngcol=8, nsig=0,
              renorm=False, E=8, EI=1024, MI=8, MSH=8, sh_split=1,
              wgt="l0_w1", wup="l0_w3", wgup=None, wdn="l0_w2",
              wsg="l0_ws1", wsu="l0_ws3", wsd="l0_ws2")
L1_FFN = dict(eps_ln="eps5", gwT="l1_gwT", gbrep=None, ngcol=9, nsig=1,
              renorm=True, E=8, EI=768, MI=12, MSH=32, sh_split=2,
              wgt=None, wup=None, wgup="l1_gup", wdn="l1_down",
              wsg="l1_sg", wsu="l1_su", wsd="l1_sd")

GLOBAL_KEYS = ["ident_f", "ident_b", "triu_f", "onesb", "i8e_b",
               "l0_win", "l0_wout", "l0_convw", "l0_convb", "l0_dtb", "l0_alog",
               "l0_dpair", "l0_normw",
               "l0_gwT", "l0_gbrep", "l0_w1", "l0_w3", "l0_w2", "l0_ws1",
               "l0_ws3", "l0_ws2",
               "l1_win", "l1_wout", "l1_convw", "l1_convb", "l1_dtb", "l1_alog",
               "l1_dpair", "l1_normw",
               "l1_gwT", "l1_gup", "l1_down", "l1_sg", "l1_su", "l1_sd"]


def _np_dtype_to_mybir(a):
    import ml_dtypes
    if a.dtype == np.float32:
        return F32
    if a.dtype == ml_dtypes.bfloat16:
        return BF16
    raise AssertionError(str(a.dtype))


def build_program(blocks, g):
    nc = bass.Bass("TRN2", target_bir_lowering=False)
    cons = {}
    for k in GLOBAL_KEYS:
        a = g[k]
        t = nc.dram_tensor(k, list(a.shape), _np_dtype_to_mybir(a),
                           kind="ExternalInput")
        cons[k] = t.ap()
    x0 = nc.dram_tensor("x0", [16, 128, T], F32, kind="ExternalInput").ap()
    y = nc.dram_tensor("y", [16, 128, T], F32, kind="ExternalOutput").ap()
    flag_hi = nc.dram_tensor("flag_hi", [128, 1], F32, kind="ExternalInput").ap()
    flag_lo = nc.dram_tensor("flag_lo", [128, 1], F32, kind="ExternalInput").ap()
    resA = nc.dram_tensor("resA", [16, 128, T], F32).ap()
    resB = nc.dram_tensor("resB", [16, 128, T], F32).ap()
    dram = dict(flag_hi=flag_hi, flag_lo=flag_lo,
                hs_spill=nc.dram_tensor("hs_spill", [32, 128, T], BF16).ap(),
                z_spill=nc.dram_tensor("z_spill", [32, 128, T], BF16).ap(),
                xbc_spill=nc.dram_tensor("xbc_spill", [48, 128, T], BF16).ap(),
                cin_tail=nc.dram_tensor("cin_tail", [128, 256], F32),
                cout_tail=nc.dram_tensor("cout_tail", [256, 256], F32),
                dbg=nc.dram_tensor("dbg", [16, 128, 512], F32),
                cin_st64=nc.dram_tensor("cin_st64", [64, 4096], F32),
                cout_st64=nc.dram_tensor("cout_st64", [128, 4096], F32),
                cin_st128=nc.dram_tensor("cin_st128", [128, 4096], F32),
                cout_st128=nc.dram_tensor("cout_st128", [256, 4096], F32))

    with ExitStack() as ctx:
        b = Builder(nc, ctx)
        blk = ctx.enter_context(nc.Block())
        e5 = b.tile((128, 1))
        e6 = b.tile((128, 1))
        b.po.memset(e5[:, :], 1e-5)
        b.po.memset(e6[:, :], 1e-6)
        b.po.drain()
        b.inc("po", b.po.nop(nofuse=True))
        b.barrier()
        cons["eps5"] = e5
        cons["eps6"] = e6

        for k, shape, dt in [("ident_f", (128, 128), F32),
                             ("ident_b", (128, 128), BF16),
                             ("triu_f", (128, 128), F32),
                             ("onesb", (128, 128), BF16)]:
            tile = b.tile(shape, dt)
            tok = b.inc("ld", b.sy.dma_start(out=tile[:, :], in_=cons[k][:, :]), 16)
            cons[k] = tile
        i8 = []
        for e in range(9):
            tile = b.tile((9, 128), BF16)
            b.inc("ld", b.sy.dma_start(out=tile[:, :], in_=cons["i8e_b"][e]), 16)
            i8.append(tile)
        cons["i8e_b"] = i8
        for nmf, src in [("flag_hi", flag_hi), ("flag_lo", flag_lo)]:
            tile = b.tile((128, 1))
            b.inc("ld", b.sy.dma_start(out=tile[:, :], in_=src[:, :]), 16)
            cons[nmf] = tile
        b.barrier()

        cur = x0
        nxt = [resA, resB]
        from_attn = globals().get("emit_attn")
        for i, name in enumerate(blocks):
            out = y if i == len(blocks) - 1 else nxt[i % 2]
            if name == "ffn0":
                emit_ffn(b, cons, L0_FFN, cur, out, dram)
            elif name == "ffn1":
                emit_ffn(b, cons, L1_FFN, cur, out, dram)
            elif name == "attn0":
                from_attn(b, cons, L0_ATTN, cur, out, dram)
            elif name == "attn1":
                from_attn(b, cons, L1_ATTN, cur, out, dram)
            cur = out
        b.wait(b.po, b.tok("gd"))
        b.wait(b.po, b.tok("ld"))
        print(f"[build] arena peak {b.peak/1024:.1f} KB/partition; "
              f"sems {b.cnt}")
    return nc



def run_blocks(inputs, blocks):
    g, percore = stage_inputs(inputs)
    nc = build_program(blocks, g)
    in_maps = []
    for c in range(NCORES):
        m = {k: g[k] for k in GLOBAL_KEYS}
        m.update(percore[c])
        in_maps.append(m)
    res = run_bass_kernel_spmd(nc, in_maps, core_ids=list(range(NCORES)))
    out = np.empty((4, 2048, 2048), np.float32)
    for c in range(NCORES):
        bb, h = c // 2, c % 2
        yc = np.asarray(res.results[c]["y"])
        out[bb, h * T:(h + 1) * T, :] = yc.reshape(2048, T).T
    return out


def kernel(**inputs):
    return run_blocks(inputs, ["attn0", "ffn0", "attn1", "ffn1"])
